# version 17
# baseline (speedup 1.0000x reference)
import os as _os
import numpy as np
import concourse.bass as bass
import concourse.mybir as mybir
from concourse.bass_utils import run_bass_kernel_spmd
from concourse.ap import AP
from contextlib import ExitStack

F32 = mybir.dt.float32
BF16 = mybir.dt.bfloat16
I32 = mybir.dt.int32
AF = mybir.ActivationFunctionType
ALU = mybir.AluOpType
AX = mybir.AxisListType

T = 2048
TW = T + 1
NFC = 8
NEG = -1.0e30

_STAGE = None


class Sem:
    __slots__ = ("h", "n")

    def __init__(self, nc, name):
        self.h = nc.alloc_semaphore(name)
        self.n = 0


class _Call:
    __slots__ = ("name", "a", "k", "incs")

    def __init__(self, name, a, k):
        self.name, self.a, self.k = name, a, k
        self.incs = []

    def then_inc(self, sem, val=1):
        self.incs.append((sem, val))
        return self


class _Rec:

    def __init__(self):
        self.calls = []

    def __getattr__(self, name):
        def call(*a, **k):
            c = _Call(name, a, k)
            self.calls.append(c)
            return c
        return call

    def replay(self, eng):
        for c in self.calls:
            instr = getattr(eng, c.name)(*c.a, **c.k)
            for sem, val in c.incs:
                instr.then_inc(sem, val)


class Phase:
    ENG = ("tensor", "vector", "scalar", "sync", "gpsimd")

    def __init__(self, prog):
        self.prog = prog
        self.rec = {e: _Rec() for e in self.ENG}

    def t(self, f): f(self.rec["tensor"])
    def v(self, f): f(self.rec["vector"])
    def s(self, f): f(self.rec["scalar"])
    def y(self, f): f(self.rec["sync"])
    def g(self, f): f(self.rec["gpsimd"])

    def emit(self):
        with self.prog.nc.Block() as blk:
            for name in self.ENG:
                rec = self.rec[name]
                if rec.calls:
                    getattr(blk, name)(rec.replay)


class Prog:
    def hw_batch(self):
        s = self.hwpool[self._hwp % 8]
        self._hwp += 1
        return s

    def sw_batch(self):
        s = self.swpool[self._swp % 8]
        self._swp += 1
        return s

    def __init__(self, stage=None):
        self.stage = stage
        nc = bass.Bass("TRN2", target_bir_lowering=False)
        self.nc = nc
        self.st = ExitStack()
        d = {}

        def ap(n, s, dt=F32, k="ExternalInput"):
            d[n] = nc.dram_tensor(n, s, dt, kind=k).ap()
            return d[n]
        ap("x", [1024, T])
        ap("wqT", [1024, 1024]); ap("wkT", [1024, 1024])
        ap("wvT", [1024, 1024]); ap("woT", [1024, 1024])
        for l in range(2):
            ap(f"wqkv{l}", [1024, 1536]); ap(f"wo{l}", [512, 1024])
            ap(f"w1{l}", [1024, 5460]); ap(f"w2{l}", [2730, 1024])
        ap("lgcols", [128, 2 * NFC])
        ap("lbcols", [128, 2 * NFC])
        ap("b1cols", [128, 2 * 44])
        ap("b2cols", [128, 2 * NFC])
        ap("qkscale", [128, 4])
        ap("tgrid", [128, T])
        ap("icol", [128, 1]); ap("sgncol", [128, 1]); ap("gatecol", [128, 1])
        ap("mask_d0", [128, 128], BF16); ap("mask_d4", [128, 128], BF16)
        ap("mask_w0d0", [128, 128], BF16); ap("mask_w0fill", [128, 128], BF16)
        ap("out", [1024, T], F32, "ExternalOutput")
        self.d = d
        self.xo_spill = nc.dram_tensor("xo_spill", [1024, T], F32, kind="Internal")
        self.ex = {}
        for name, shape, dt in [
            ("x0col", [128, 8], F32),
            ("kv0", [512, 1032], BF16), ("nrm0", [1, 4], F32), ("xm0", [128, 8], F32),
            ("x1col", [128, 8], F32),
            ("kv1", [512, 1032], BF16), ("nrm1", [1, 4], F32), ("xm1", [128, 8], F32),
        ]:
            src = nc.dram_tensor(f"exs_{name}", shape, dt, kind="Internal")
            dst = nc.dram_tensor(f"exd_{name}", [2] + shape, dt, kind="Internal")
            self.ex[name] = (src, dst)
        self.pe = Sem(nc, "pe")
        self.act = Sem(nc, "act")
        self.dve = Sem(nc, "dve")
        self.gp = Sem(nc, "gp")
        self.dX = Sem(nc, "dX")
        self.dW = Sem(nc, "dW")
        self.cc = Sem(nc, "cc")
        self.sout = Sem(nc, "sout")
        self._ecc = {}
        self._stop = False
        self.hwpool = [Sem(nc, f"dh{i}") for i in range(8)]
        self.swpool = [Sem(nc, f"ds{i}") for i in range(8)]
        self._hwp = 0
        self._swp = 0

        sb = lambda n, s, dt: self.st.enter_context(nc.sbuf_tensor(n, s, dt))
        self.XA = [sb(f"XA{i}", [128, TW], F32) for i in range(NFC)]
        self.XS = [sb(f"XS{i}", [128, TW], BF16) for i in range(NFC)]
        self.ONESB = sb("ONESB", [128, 1], BF16)
        self.ONESH = sb("ONESH", [128, 2], BF16)
        self.ONESF = sb("ONESF", [1, 128], F32)
        self.STAT = sb("STAT", [1, 1024], F32)
        self.SROW = sb("SROW", [1, 1024], F32)
        self.BCOL = sb("BCOL", [128, 8], F32)
        self.MSC = sb("MSC", [1, 64], F32)
        self.MSC2 = sb("MSC2", [1, 40], F32)
        self.HS = sb("HS", [128, 8], F32)
        self.MD0 = sb("MD0", [128, 128], BF16)
        self.MD4 = sb("MD4", [128, 128], BF16)
        self.MW0D0 = sb("MW0D0", [128, 128], BF16)
        self.MW0F = sb("MW0F", [128, 128], BF16)
        self.ICOL = sb("ICOL", [128, 1], F32)
        self.SGN = sb("SGN", [128, 1], F32)
        self.GATE = sb("GATE", [128, 1], F32)
        self.QKS = sb("QKS", [128, 4], F32)
        self.LG = sb("LG", [128, 2 * NFC], F32)
        self.LB = sb("LB", [128, 2 * NFC], F32)
        self.B1C = sb("B1C", [128, 2 * 44], F32)
        self.B2C = sb("B2C", [128, 2 * NFC], F32)

    def build(self):
        self.ph_init()
        if self.stage == "init":
            self.ph_emit_stream(self.XA)
            return self.nc
        self.ph_outer()
        if self.stage == "x_outer":
            self.ph_emit_stream(self.XA)
            return self.nc
        self.ph_x0()
        if self.stage == "x0":
            self.ph_emit_stream(self.XA)
            return self.nc
        self.ph_tables()
        self.ph_ffn()
        if self._stop:
            return self.nc
        self.ph_final()
        return self.nc

    def ph_init(self):
        d, S = self.d, self
        p = Phase(self)

        bs = self.hw_batch()

        def y(sync):
            for i in range(NFC):
                sync.dma_start(out=self.XA[i][:, 1:TW], in_=d["x"][128 * i:128 * (i + 1), :]).then_inc(bs.h, 16)
            small = [("icol", self.ICOL), ("sgncol", self.SGN),
                     ("gatecol", self.GATE), ("qkscale", self.QKS), ("lgcols", self.LG),
                     ("lbcols", self.LB), ("b1cols", self.B1C), ("b2cols", self.B2C),
                     ("mask_d0", self.MD0), ("mask_d4", self.MD4),
                     ("mask_w0d0", self.MW0D0), ("mask_w0fill", self.MW0F)]
            for name, dst in small:
                sync.dma_start(out=dst[:], in_=d[name]).then_inc(bs.h, 16)
            bs.n += 16 * (NFC + len(small))
        p.y(y)
        nbs = bs.n

        def v(vector):
            vector.wait_ge(bs.h, nbs)
            vector.memset(self.ONESB[:], 1.0)
            vector.memset(self.ONESF[:], 1.0)
            vector.memset(self.ONESH[0:64, 0:1], 1.0)
            vector.memset(self.ONESH[64:128, 0:1], 0.0)
            vector.memset(self.ONESH[0:64, 1:2], 0.0)
            vector.memset(self.ONESH[64:128, 1:2], 1.0)
            for i in range(NFC):
                vector.memset(self.XA[i][:, 0:1], 0.0)
            vector.drain()
            for i in range(NFC):
                vector.tensor_copy(out=self.XS[i][:], in_=self.XA[i][:])
        p.v(v)
        p.emit()

    def ph_tables(self):
        nc, S = self.nc, self
        f32 = np.float32
        sp = lambda x: (float(f32(x)), float(f32(x - np.float64(f32(x)))))
        LN4_HI, LN4_LO = sp(np.log(1e4))
        PI_HI, PI_LO = sp(np.pi)
        HPI_HI, HPI_LO = sp(np.pi / 2)
        A_HI, A_LO = sp(25.6)
        B_HI, B_LO = sp(89.6)

        sb = lambda n, s, dt: self.st.enter_context(nc.sbuf_tensor(n, s, dt))
        self.TCQ = sb("TCQ", [128, T], BF16)
        self.TSQ = sb("TSQ", [128, T], BF16)
        self.TCK = sb("TCK", [128, T], BF16)
        self.TSK = sb("TSK", [128, T], BF16)
        st = ExitStack()
        sc = st.enter_context(nc.sbuf_tensor("tb_sc", [128, 8], F32))
        TGRID = st.enter_context(nc.sbuf_tensor("TGRID", [128, T], F32))
        self.TGRID = TGRID
        fr = st.enter_context(nc.sbuf_tensor("tb_fr", [128, T], F32))
        t1 = st.enter_context(nc.sbuf_tensor("tb_t1", [128, T], F32))
        t2 = st.enter_context(nc.sbuf_tensor("tb_t2", [128, T], F32))
        t3 = st.enter_context(nc.sbuf_tensor("tb_t3", [128, T], F32))
        ki = st.enter_context(nc.sbuf_tensor("tb_ki", [128, T], I32))
        sinb = st.enter_context(nc.sbuf_tensor("tb_sin", [128, T], F32))
        cosb = st.enter_context(nc.sbuf_tensor("tb_cos", [128, T], F32))
        scl = st.enter_context(nc.sbuf_tensor("tb_scl", [128, T], F32))
        scli = st.enter_context(nc.sbuf_tensor("tb_scli", [128, T], F32))
        p = Phase(self)

        bs0 = self.hw_batch()

        def y0(sync):
            sync.dma_start(out=TGRID[:], in_=self.d["tgrid"]).then_inc(bs0.h, 16)
            bs0.n += 16
        p.y(y0)
        ndx = bs0.n

        def v0(v):
            v.wait_ge(bs0.h, ndx)
        p.v(v0)

        def v1(v):
            v.tensor_scalar(out=sc[:, 0:1], in0=self.ICOL[:], scalar1=1.0 / 32.0, scalar2=-LN4_HI, op0=ALU.mult, op1=ALU.mult)
            v.tensor_scalar(out=sc[:, 1:2], in0=self.ICOL[:], scalar1=1.0 / 32.0, scalar2=-LN4_LO, op0=ALU.mult, op1=ALU.mult)
            v.drain()
            v.tensor_tensor(out=sc[:, 0:1], in0=sc[:, 0:1], in1=sc[:, 1:2], op=ALU.add)
            v.tensor_scalar(out=sc[:, 2:3], in0=self.ICOL[:], scalar1=2.0, scalar2=A_HI, op0=ALU.mult, op1=ALU.add)
            v.memset(sc[:, 3:4], B_HI)
            v.drain()
            v.tensor_scalar(out=sc[:, 2:3], in0=sc[:, 2:3], scalar1=A_LO, scalar2=None, op0=ALU.add)
            v.tensor_scalar(out=sc[:, 3:4], in0=sc[:, 3:4], scalar1=B_LO, scalar2=None, op0=ALU.add)
            v.drain()
            v.reciprocal(out=sc[:, 3:4], in_=sc[:, 3:4])
            v.drain()
            v.tensor_tensor(out=sc[:, 4:5], in0=sc[:, 2:3], in1=sc[:, 3:4], op=ALU.mult).then_inc(S.dve.h, 1)
            S.dve.n += 1
        p.v(v1)
        n1 = S.dve.n

        def s1(s):
            s.wait_ge(S.dve.h, n1)
            s.activation(out=sc[:, 0:1], in_=sc[:, 0:1], func=AF.Exp)
            s.activation(out=sc[:, 1:2], in_=sc[:, 4:5], func=AF.Ln)
            s.drain()
            s.sem_inc(S.act.h, 1)
            S.act.n += 1
        p.s(s1)
        n2 = S.act.n

        def v2(v):
            v.wait_ge(S.act.h, n2)
            v.tensor_scalar(out=fr[:], in0=self.TGRID[:], scalar1=sc[:, 0:1], scalar2=None, op0=ALU.mult)
            v.tensor_scalar(out=scl[:], in0=self.TGRID[:], scalar1=-2048.0, scalar2=1.0 / 256.0, op0=ALU.add, op1=ALU.mult)
            v.drain()
            v.tensor_scalar(out=scl[:], in0=scl[:], scalar1=sc[:, 1:2], scalar2=None, op0=ALU.mult)
            v.drain()
            v.tensor_scalar(out=scli[:], in0=scl[:], scalar1=-1.0, scalar2=None, op0=ALU.mult).then_inc(S.dve.h, 1)
            S.dve.n += 1
        p.v(v2)
        n3 = S.dve.n

        def s2(s):
            s.wait_ge(S.dve.h, n3)
            s.activation(out=scl[:], in_=scl[:], func=AF.Exp)
            s.activation(out=scli[:], in_=scli[:], func=AF.Exp)
            s.drain()
            s.sem_inc(S.act.h, 1)
            S.act.n += 1
        p.s(s2)
        n4 = S.act.n

        def emit_sin(out_buf, sh_hi, sh_lo, first):
            HPI = float(np.float32(np.pi / 2))

            def vv(v):
                if first:
                    v.wait_ge(S.act.h, n4)
                if sh_hi == 0.0:
                    v.tensor_copy(out=t1[:], in_=fr[:])
                    v.drain()
                else:
                    v.tensor_scalar(out=t1[:], in0=fr[:], scalar1=sh_hi, scalar2=None, op0=ALU.add)
                    v.drain()
                    v.tensor_scalar(out=t1[:], in0=t1[:], scalar1=sh_lo, scalar2=None, op0=ALU.add)
                    v.drain()
                v.tensor_scalar(out=t2[:], in0=t1[:], scalar1=1.0 / (2.0 * np.pi), scalar2=-0.5, op0=ALU.mult, op1=ALU.add)
                v.drain()
                v.tensor_copy(out=ki[:], in_=t2[:])
                v.drain()
                v.tensor_copy(out=t2[:], in_=ki[:])
                v.drain()
                v.tensor_scalar(out=t3[:], in0=t2[:], scalar1=2.0 * PI_HI, scalar2=None, op0=ALU.mult)
                v.drain()
                v.tensor_tensor(out=t1[:], in0=t1[:], in1=t3[:], op=ALU.subtract)
                v.drain()
                v.tensor_scalar(out=t3[:], in0=t2[:], scalar1=2.0 * PI_LO, scalar2=None, op0=ALU.mult)
                v.drain()
                v.tensor_tensor(out=t1[:], in0=t1[:], in1=t3[:], op=ALU.subtract)
                v.drain()
                v.tensor_scalar(out=t1[:], in0=t1[:], scalar1=-PI_HI, scalar2=None, op0=ALU.add)
                v.drain()
                v.tensor_scalar(out=t1[:], in0=t1[:], scalar1=-PI_LO, scalar2=None, op0=ALU.add)
                v.drain()
                v.tensor_scalar(out=t2[:], in0=t1[:], scalar1=-PI_HI, scalar2=None, op0=ALU.is_lt)
                v.drain()
                v.scalar_tensor_tensor(out=t1[:], in0=t2[:], scalar=2.0 * PI_HI, in1=t1[:], op0=ALU.mult, op1=ALU.add)
                v.drain()
                v.tensor_scalar(out=t2[:], in0=t1[:], scalar1=PI_HI, scalar2=None, op0=ALU.is_gt)
                v.drain()
                v.scalar_tensor_tensor(out=t1[:], in0=t2[:], scalar=-2.0 * PI_HI, in1=t1[:], op0=ALU.mult, op1=ALU.add)
                v.drain()
                v.tensor_scalar(out=t2[:], in0=t1[:], scalar1=HPI, scalar2=None, op0=ALU.is_gt)
                v.tensor_scalar(out=t3[:], in0=t1[:], scalar1=-2.0, scalar2=PI_HI, op0=ALU.mult, op1=ALU.add)
                v.drain()
                v.tensor_tensor(out=t2[:], in0=t2[:], in1=t3[:], op=ALU.mult)
                v.drain()
                v.tensor_tensor(out=t1[:], in0=t1[:], in1=t2[:], op=ALU.add)
                v.drain()
                v.tensor_scalar(out=t2[:], in0=t1[:], scalar1=-HPI, scalar2=None, op0=ALU.is_lt)
                v.tensor_scalar(out=t3[:], in0=t1[:], scalar1=-2.0, scalar2=-PI_HI, op0=ALU.mult, op1=ALU.add)
                v.drain()
                v.tensor_tensor(out=t2[:], in0=t2[:], in1=t3[:], op=ALU.mult)
                v.drain()
                v.tensor_tensor(out=t1[:], in0=t1[:], in1=t2[:], op=ALU.add).then_inc(S.dve.h, 1)
                S.dve.n += 1
                v.drain()
            p.v(vv)
            nred = S.dve.n

            def ss(s):
                s.wait_ge(S.dve.h, nred)
                s.activation(out=out_buf[:], in_=t1[:], func=AF.Sin)
                s.drain()
                s.sem_inc(S.act.h, 1)
                S.act.n += 1
            p.s(ss)
            nsin = S.act.n

            def v3(v):
                v.wait_ge(S.act.h, nsin)
                v.tensor_scalar(out=out_buf[:], in0=out_buf[:], scalar1=-1.0, scalar2=None, op0=ALU.mult)
                v.drain()
            p.v(v3)

        emit_sin(sinb, 0.0, 0.0, True)
        emit_sin(cosb, HPI_HI, HPI_LO, False)

        def v4(v):
            v.tensor_tensor(out=self.TCQ[:], in0=cosb[:], in1=scl[:], op=ALU.mult)
            v.tensor_tensor(out=self.TCK[:], in0=cosb[:], in1=scli[:], op=ALU.mult)
            v.tensor_scalar(out=t1[:], in0=sinb[:], scalar1=self.SGN[:, 0:1], scalar2=None, op0=ALU.mult)
            v.drain()
            v.tensor_tensor(out=self.TSQ[:], in0=t1[:], in1=scl[:], op=ALU.mult)
            v.tensor_tensor(out=self.TSK[:], in0=t1[:], in1=scli[:], op=ALU.mult)
            v.drain()
        p.v(v4)
        p.emit()
        st.close()

    def ln_block(self, p, psums, src_f32, src_bf, cols, out_tiles, out_cols=None,
                 sq_scr=None, lg_off=None, lb_off=None, res_tiles=None, res_cols=None,
                 out_bf=None, shift=False, act_gate=None, out_gate=None, pe_gate=None):
        S = self
        n = cols.stop - cols.start
        oc = out_cols if out_cols is not None else cols
        rc = res_cols if res_cols is not None else cols

        def ci(i):
            if shift and i >= 4:
                return slice(cols.start - 1, cols.stop - 1)
            return cols
        psA, psB, psR, psM, scr = psums["A"], psums["B"], psums["R"], psums["M"], psums["scr"]
        mm_done = []
        for i in range(NFC):
            def vsq(v, i=i):
                if i >= 2:
                    v.wait_ge(S.pe.h, mm_done[i - 2])
                elif pe_gate is not None:
                    v.wait_ge(S.pe.h, pe_gate)
                v.tensor_tensor(out=sq_scr[:, 512 * (i % 2):512 * (i % 2) + n],
                                in0=src_bf[i][:, ci(i)], in1=src_bf[i][:, ci(i)], op=ALU.mult).then_inc(S.dve.h, 1)
                S.dve.n += 1
            p.v(vsq)
            nv = S.dve.n

            def tmm(t, i=i, nv=nv):
                t.wait_ge(S.dve.h, nv)
                t.matmul(psA[0:1, 0:n], self.ONESB[:, 0:1], src_bf[i][:, ci(i)], start=(i == 0), stop=(i == NFC - 1))
                mm = t.matmul(psB[0:1, 0:n], self.ONESB[:, 0:1], sq_scr[:, 512 * (i % 2):512 * (i % 2) + n], start=(i == 0), stop=(i == NFC - 1))
                mm.then_inc(S.pe.h, 1)
                S.pe.n += 1
            p.t(tmm)
            mm_done.append(S.pe.n)
        n_stats = S.pe.n

        def vstat(v):
            v.wait_ge(S.pe.h, n_stats)
            if act_gate is not None:
                v.wait_ge(S.act.h, act_gate)
            v.tensor_scalar(out=self.SROW[0:1, 0:n], in0=psA[0:1, 0:n], scalar1=1.0 / 1024.0, scalar2=None, op0=ALU.mult)
            v.tensor_scalar(out=self.SROW[0:1, 512:512 + n], in0=psB[0:1, 0:n], scalar1=1.0 / 1024.0, scalar2=None, op0=ALU.mult)
            v.drain()
            v.tensor_tensor(out=self.STAT[0:1, 0:n], in0=self.SROW[0:1, 0:n], in1=self.SROW[0:1, 0:n], op=ALU.mult)
            v.drain()
            v.tensor_tensor(out=self.STAT[0:1, 0:n], in0=self.SROW[0:1, 512:512 + n], in1=self.STAT[0:1, 0:n], op=ALU.subtract)
            v.drain()
            v.tensor_scalar(out=self.STAT[0:1, 0:n], in0=self.STAT[0:1, 0:n], scalar1=1e-5, scalar2=None, op0=ALU.add).then_inc(S.dve.h, 1)
            S.dve.n += 1
        p.v(vstat)
        nv2 = S.dve.n

        def svar(s):
            s.wait_ge(S.dve.h, nv2)
            s.activation(out=self.STAT[0:1, 0:n], in_=self.STAT[0:1, 0:n], func=AF.Ln)
            s.drain()
            s.activation(out=self.STAT[0:1, 0:n], in_=self.STAT[0:1, 0:n], func=AF.Exp, scale=-0.5)
            s.drain()
            s.sem_inc(S.act.h, 1)
            S.act.n += 1
        p.s(svar)
        na = S.act.n

        def vrec(v):
            v.wait_ge(S.act.h, na)
            v.tensor_tensor(out=self.STAT[0:1, 512:512 + n], in0=self.SROW[0:1, 0:n], in1=self.STAT[0:1, 0:n], op=ALU.mult)
            v.drain()
            v.tensor_scalar(out=self.STAT[0:1, 512:512 + n], in0=self.STAT[0:1, 512:512 + n], scalar1=-1.0, scalar2=None, op0=ALU.mult).then_inc(S.dve.h, 1)
            S.dve.n += 1
        p.v(vrec)
        nv3 = S.dve.n

        def tbc(t):
            t.wait_ge(S.dve.h, nv3)
            t.matmul(psR[:, 0:n], self.ONESF[0:1, :], self.STAT[0:1, 0:n], start=True, stop=True)
            t.matmul(psM[:, 0:n], self.ONESF[0:1, :], self.STAT[0:1, 512:512 + n], start=True, stop=True).then_inc(S.pe.h, 1)
            S.pe.n += 1
        p.t(tbc)
        n_bc = S.pe.n

        for i in range(NFC):
            def vno(v, i=i):
                if i == 0:
                    v.wait_ge(S.pe.h, n_bc)
                    if out_gate is not None:
                        v.wait_ge(out_gate[0], out_gate[1])
                v.tensor_tensor(out=scr[:, 0:n], in0=src_f32[i][:, ci(i)], in1=psR[:, 0:n], op=ALU.mult)
                v.drain()
                if lg_off is not None:
                    v.tensor_tensor(out=scr[:, 0:n], in0=scr[:, 0:n], in1=psM[:, 0:n], op=ALU.add)
                    v.drain()
                    v.tensor_scalar(out=scr[:, 0:n], in0=scr[:, 0:n], scalar1=self.LG[:, lg_off + i:lg_off + i + 1], scalar2=None, op0=ALU.mult)
                    v.drain()
                    v.tensor_scalar(out=out_tiles[i][:, oc], in0=scr[:, 0:n], scalar1=self.LB[:, lb_off + i:lb_off + i + 1], scalar2=None, op0=ALU.add)
                elif res_tiles is not None:
                    v.tensor_tensor(out=scr[:, 0:n], in0=scr[:, 0:n], in1=psM[:, 0:n], op=ALU.add)
                    v.drain()
                    v.tensor_tensor(out=out_tiles[i][:, oc], in0=scr[:, 0:n], in1=res_tiles[i][:, rc], op=ALU.add)
                else:
                    v.tensor_tensor(out=out_tiles[i][:, oc], in0=scr[:, 0:n], in1=psM[:, 0:n], op=ALU.add)
                v.drain()
                if out_bf is not None:
                    v.tensor_copy(out=out_bf[i][:, oc], in_=out_tiles[i][:, oc])
                    v.drain()
                if i == NFC - 1:
                    v.sem_inc(S.dve.h, 1)
                    S.dve.n += 1
            p.v(vno)
        return S.dve.n

    def ln_ring(self, p, psums, ring_ev, src_f32, src_bf, cols, out_tiles,
                out_cols=None, sq_scr=None, res_tiles=None, res_cols=None):
        S = self
        n = cols.stop - cols.start
        oc = out_cols if out_cols is not None else cols
        rc = res_cols if res_cols is not None else cols
        psA, psB, psR, psM, scr = psums["A"], psums["B"], psums["R"], psums["M"], psums["scr"]
        claim = max(ring_ev[0], ring_ev[1])
        n0 = S.pe.n
        mm_done = []
        for i in range(NFC):
            def vsq(v, i=i):
                if i >= 2:
                    v.wait_ge(S.pe.h, mm_done[i - 2])
                else:
                    v.wait_ge(S.pe.h, n0)
                v.tensor_tensor(out=sq_scr[:, 256 * (i % 2):256 * (i % 2) + n],
                                in0=src_bf[i][:, cols], in1=src_bf[i][:, cols], op=ALU.mult).then_inc(S.dve.h, 1)
                S.dve.n += 1
            p.v(vsq)
            nv = S.dve.n

            def tmm(t, i=i, nv=nv):
                t.wait_ge(S.dve.h, max(nv, claim if i == 0 else 0))
                t.matmul(psA[0:1, 0:n], self.ONESB[:, 0:1], src_bf[i][:, cols], start=(i == 0), stop=(i == NFC - 1))
                mm = t.matmul(psB[0:1, 0:n], self.ONESB[:, 0:1], sq_scr[:, 256 * (i % 2):256 * (i % 2) + n], start=(i == 0), stop=(i == NFC - 1))
                mm.then_inc(S.pe.h, 1)
                S.pe.n += 1
            p.t(tmm)
            mm_done.append(S.pe.n)
        n_stats = S.pe.n

        def vstat(v):
            v.wait_ge(S.pe.h, n_stats)
            v.tensor_scalar(out=self.SROW[0:1, 0:n], in0=psA[0:1, 0:n], scalar1=1.0 / 1024.0, scalar2=None, op0=ALU.mult)
            v.tensor_scalar(out=self.SROW[0:1, 512:512 + n], in0=psB[0:1, 0:n], scalar1=1.0 / 1024.0, scalar2=None, op0=ALU.mult)
            v.drain()
            v.tensor_tensor(out=self.STAT[0:1, 512:512 + n], in0=self.SROW[0:1, 0:n], in1=self.SROW[0:1, 0:n], op=ALU.mult)
            v.drain()
            v.tensor_tensor(out=self.STAT[0:1, 512:512 + n], in0=self.SROW[0:1, 512:512 + n], in1=self.STAT[0:1, 512:512 + n], op=ALU.subtract)
            v.drain()
            v.tensor_scalar(out=self.STAT[0:1, 512:512 + n], in0=self.STAT[0:1, 512:512 + n], scalar1=1e-5, scalar2=None, op0=ALU.add).then_inc(S.dve.h, 1)
            S.dve.n += 1
        p.v(vstat)
        nv2 = S.dve.n

        def svar(s):
            s.wait_ge(S.dve.h, nv2)
            s.activation(out=self.STAT[0:1, 512:512 + n], in_=self.STAT[0:1, 512:512 + n], func=AF.Ln)
            s.drain()
            s.activation(out=self.STAT[0:1, 512:512 + n], in_=self.STAT[0:1, 512:512 + n], func=AF.Exp, scale=-0.5)
            s.drain()
            s.sem_inc(S.act.h, 1)
            S.act.n += 1
        p.s(svar)
        na = S.act.n

        def vrec(v):
            v.wait_ge(S.act.h, na)
            v.tensor_tensor(out=self.STAT[0:1, 768:768 + n], in0=self.SROW[0:1, 0:n], in1=self.STAT[0:1, 512:512 + n], op=ALU.mult)
            v.drain()
            v.tensor_scalar(out=self.STAT[0:1, 768:768 + n], in0=self.STAT[0:1, 768:768 + n], scalar1=-1.0, scalar2=None, op0=ALU.mult).then_inc(S.dve.h, 1)
            S.dve.n += 1
        p.v(vrec)
        nv3 = S.dve.n

        def tbc(t):
            t.wait_ge(S.dve.h, nv3)
            t.matmul(psR[:, 0:n], self.ONESF[0:1, :], self.STAT[0:1, 512:512 + n], start=True, stop=True)
            t.matmul(psM[:, 0:n], self.ONESF[0:1, :], self.STAT[0:1, 768:768 + n], start=True, stop=True).then_inc(S.pe.h, 1)
            S.pe.n += 1
        p.t(tbc)
        n_bc = S.pe.n

        for i in range(NFC):
            def vno(v, i=i):
                if i == 0:
                    v.wait_ge(S.pe.h, n_bc)
                v.tensor_tensor(out=scr[:, 0:n], in0=src_f32[i][:, cols], in1=psR[:, 0:n], op=ALU.mult)
                v.drain()
                if res_tiles is not None:
                    v.tensor_tensor(out=scr[:, 0:n], in0=scr[:, 0:n], in1=psM[:, 0:n], op=ALU.add)
                    v.drain()
                    v.tensor_tensor(out=out_tiles[i][:, oc], in0=scr[:, 0:n], in1=res_tiles[i][:, rc], op=ALU.add)
                else:
                    v.tensor_tensor(out=out_tiles[i][:, oc], in0=scr[:, 0:n], in1=psM[:, 0:n], op=ALU.add)
                v.drain()
                v.sem_inc(S.dve.h, 1)
                S.dve.n += 1
            p.v(vno)
        ring_ev[0] = ring_ev[1] = S.dve.n
        return S.dve.n

    def ph_outer(self):
        nc, S, d = self.nc, self, self.d
        st = ExitStack()
        WQ = [st.enter_context(nc.sbuf_tensor(f"WQ{i}", [128, 1024], BF16)) for i in range(NFC)]
        WK = [st.enter_context(nc.sbuf_tensor(f"WK{i}", [128, 1024], BF16)) for i in range(NFC)]
        WV = [st.enter_context(nc.sbuf_tensor(f"WV{i}", [128, 1024], BF16)) for i in range(NFC)]
        WO = [st.enter_context(nc.sbuf_tensor(f"WO{i}", [128, 1024], BF16)) for i in range(NFC)]
        LN0 = [st.enter_context(nc.sbuf_tensor(f"LN0_{i}", [128, 256], BF16)) for i in range(NFC)]
        QH = [st.enter_context(nc.sbuf_tensor(f"QH{i}", [128, 256], BF16)) for i in range(NFC)]
        KH = [st.enter_context(nc.sbuf_tensor(f"KH{i}", [128, 256], BF16)) for i in range(NFC)]
        XO = [st.enter_context(nc.sbuf_tensor(f"XOt{i}", [128, 256], BF16)) for i in range(NFC)]
        VT65 = [st.enter_context(nc.sbuf_tensor(f"VT65_{i}", [128, 1040], BF16)) for i in range(2)]
        QD2 = [st.enter_context(nc.sbuf_tensor(f"QD2_{i}", [128, 512], BF16)) for i in range(NFC)]
        PT = [st.enter_context(nc.sbuf_tensor(f"PT{i}", [128, 1024], BF16)) for i in range(2)]
        SQ = st.enter_context(nc.sbuf_tensor("SQ", [128, 512], BF16))
        BCO2 = st.enter_context(nc.sbuf_tensor("BCO2", [128, 8], F32))
        SCR = st.enter_context(nc.sbuf_tensor("SCR", [128, 512], F32))
        RB = SCR[0:64, :]
        WOUT = [QD2[hc][:, 0:256] for hc in range(NFC)]

        p0 = Phase(self)
        bw = self.sw_batch()

        def g(gp):
            for i in range(NFC):
                gp.dma_start(out=WQ[i][:], in_=d["wqT"][128 * i:128 * (i + 1), :]).then_inc(bw.h, 16)
                gp.dma_start(out=WK[i][:], in_=d["wkT"][128 * i:128 * (i + 1), :]).then_inc(bw.h, 16)
                gp.dma_start(out=WV[i][:], in_=d["wvT"][128 * i:128 * (i + 1), :]).then_inc(bw.h, 16)
                gp.dma_start(out=WO[i][:], in_=d["woT"][128 * i:128 * (i + 1), :]).then_inc(bw.h, 16)
            bw.n += 16 * 4 * NFC
        p0.g(g)
        p0.emit()
        w_ready = (bw, bw.n)

        p = Phase(self)
        pst = ExitStack()
        psS = [pst.enter_context(nc.psum_tensor(f"opsS{i}", [128, 1024], F32)) for i in range(2)]
        psAV = [pst.enter_context(nc.psum_tensor(f"opsAV{i}", [65, 512], F32)) for i in range(2)]
        psR = pst.enter_context(nc.psum_tensor("opsR", [128, 512], F32))
        psM = pst.enter_context(nc.psum_tensor("opsM", [128, 512], F32))
        psums = {"A": psR[0:1, 0:512], "B": psM[0:1, 0:512], "R": psR, "M": psM, "scr": SCR}
        ring_ev = [0, 0]
        self._rslot = 1
        self._bspo = self.hw_batch()

        def twait(t):
            t.wait_ge(w_ready[0].h, w_ready[1])
        p.t(twait)

        def ln_chunk(c):
            cols = slice(1 + 256 * c, 1 + 256 * c + 256)
            self.ln_ring(p, psums, ring_ev, self.XA, self.XS, cols, LN0,
                         out_cols=slice(0, 256), sq_scr=SQ)

        def proj_group(c, kind, j):
            W, DST = (WQ, QH) if kind == "q" else (WK, KH)
            slot = self.ring_slot()

            def tq(t, W=W, j=j, slot=slot, claim=ring_ev[slot]):
                t.wait_ge(S.dve.h, claim)
                dst = psR if slot == 0 else psM
                for i in range(NFC):
                    mm = t.matmul(dst[:, 0:256], W[i][:, 128 * j:128 * (j + 1)], LN0[i][:], start=(i == 0), stop=(i == NFC - 1))
                mm.then_inc(S.pe.h, 1)
                S.pe.n += 1
            p.t(tq)
            npe = S.pe.n

            def vq(v, DST=DST, j=j, npe=npe, slot=slot):
                v.wait_ge(S.pe.h, npe)
                src = psR if slot == 0 else psM
                v.tensor_copy(out=DST[j][:], in_=src[:, 0:256]).then_inc(S.dve.h, 1)
                S.dve.n += 1
                ring_ev[slot] = S.dve.n
            p.v(vq)

        def vproj_group(c, sh, ob):
            slot = self.ring_slot()

            def tv(t, sh=sh, ob=ob, slot=slot, claim=ring_ev[slot]):
                t.wait_ge(S.dve.h, claim)
                dst = psR if slot == 0 else psM
                for i in range(NFC):
                    mm = t.matmul(dst[:, 0:512], LN0[i][:, 128 * sh:128 * (sh + 1)],
                                  WV[i][:, 512 * ob:512 * (ob + 1)], start=(i == 0), stop=(i == NFC - 1))
                mm.then_inc(S.pe.h, 1)
                S.pe.n += 1
            p.t(tv)
            npe = S.pe.n

            def vv(v, sh=sh, ob=ob, npe=npe, slot=slot):
                v.wait_ge(S.pe.h, npe)
                src = psR if slot == 0 else psM
                for g2 in range(8):
                    gg = ob * 8 + g2
                    v.tensor_copy(out=VT65[sh][:, 65 * gg:65 * gg + 64], in_=src[:, 64 * g2:64 * (g2 + 1)])
                v.drain()
                v.sem_inc(S.dve.h, 1)
                S.dve.n += 1
                ring_ev[slot] = S.dve.n
            p.v(vv)

        def norm_pair(c, which, j0):
            buf = QH if which == 0 else KH
            claim = max(ring_ev[0], ring_ev[1])
            n0 = S.pe.n

            def vsq(v, buf=buf, j0=j0, n0=n0):
                v.wait_ge(S.pe.h, n0)
                v.tensor_tensor(out=SQ[:, 0:256], in0=buf[j0][:], in1=buf[j0][:], op=ALU.mult)
                v.tensor_tensor(out=SQ[:, 256:512], in0=buf[j0 + 1][:], in1=buf[j0 + 1][:], op=ALU.mult)
                v.drain()
                v.sem_inc(S.dve.h, 1)
                S.dve.n += 1
            p.v(vsq)
            nsq = S.dve.n

            def tnm(t, nsq=nsq, claim=claim):
                t.wait_ge(S.dve.h, max(nsq, claim))
                t.matmul(psR[0:1, 0:512], self.ONESH[:, 0:1], SQ[:, 0:512], start=True, stop=True)
                t.matmul(psM[0:1, 0:512], self.ONESH[:, 1:2], SQ[:, 0:512], start=True, stop=True).then_inc(S.pe.h, 1)
                S.pe.n += 1
            p.t(tnm)
            npe = S.pe.n

            def vnm(v, which=which, j0=j0, npe=npe):
                v.wait_ge(S.pe.h, npe)
                v.tensor_reduce(out=self.MSC[0:1, 0:1], in_=psR[0:1, 0:256], axis=AX.X, op=ALU.max)
                v.tensor_reduce(out=self.MSC[0:1, 1:2], in_=psM[0:1, 0:256], axis=AX.X, op=ALU.max)
                v.tensor_reduce(out=self.MSC[0:1, 2:3], in_=psR[0:1, 256:512], axis=AX.X, op=ALU.max)
                v.tensor_reduce(out=self.MSC[0:1, 3:4], in_=psM[0:1, 256:512], axis=AX.X, op=ALU.max)
                v.drain()
                v.tensor_tensor(out=self.MSC[0:1, 4:5], in0=self.MSC[0:1, 0:1], in1=self.MSC[0:1, 1:2], op=ALU.max)
                v.tensor_tensor(out=self.MSC[0:1, 5:6], in0=self.MSC[0:1, 2:3], in1=self.MSC[0:1, 3:4], op=ALU.max)
                v.drain()
                if which == 0:
                    v.tensor_copy(out=self.MSC[0:1, 8 + j0:10 + j0], in_=self.MSC[0:1, 4:6])
                else:
                    v.tensor_tensor(out=self.MSC[0:1, 6:7], in0=self.MSC[0:1, 4:5], in1=self.MSC[0:1, 5:6], op=ALU.max)
                    v.drain()
                    if j0 == 0:
                        v.tensor_copy(out=self.MSC[0:1, 16:17], in_=self.MSC[0:1, 6:7])
                    else:
                        v.tensor_tensor(out=self.MSC[0:1, 16:17], in0=self.MSC[0:1, 16:17], in1=self.MSC[0:1, 6:7], op=ALU.max)
                v.drain()
                v.sem_inc(S.dve.h, 1)
                S.dve.n += 1
                ring_ev[0] = ring_ev[1] = S.dve.n
            p.v(vnm)

        def bias_chunk(c):
            bcol = self.BCOL if c % 2 == 0 else BCO2
            slot = self.ring_slot()

            def vab(v):
                v.tensor_scalar(out=self.MSC[0:1, 24:32], in0=self.MSC[0:1, 8:16], scalar1=self.MSC[0:1, 16:17], scalar2=None, op0=ALU.mult).then_inc(S.dve.h, 1)
                S.dve.n += 1
            p.v(vab)
            nva = S.dve.n

            def sab(s):
                s.wait_ge(S.dve.h, nva)
                s.activation(out=self.MSC[0:1, 24:32], in_=self.MSC[0:1, 24:32], func=AF.Sqrt)
                s.drain()
                s.sem_inc(S.act.h, 1)
                S.act.n += 1
            p.s(sab)
            nsa = S.act.n

            def vab2(v):
                v.wait_ge(S.act.h, nsa)
                v.tensor_scalar(out=self.MSC[0:1, 24:32], in0=self.MSC[0:1, 24:32], scalar1=-0.125, scalar2=None, op0=ALU.mult).then_inc(S.dve.h, 1)
                S.dve.n += 1
            p.v(vab2)
            nva2 = S.dve.n

            def tab(t, slot=slot, claim=ring_ev[slot]):
                t.wait_ge(S.dve.h, max(nva2, claim))
                dst = psR if slot == 0 else psM
                t.matmul(dst[:, 0:8], self.ONESF[0:1, :], self.MSC[0:1, 24:32], start=True, stop=True).then_inc(S.pe.h, 1)
                S.pe.n += 1
            p.t(tab)
            nta = S.pe.n

            def vab3(v, bcol=bcol, slot=slot):
                v.wait_ge(S.pe.h, nta)
                src = psR if slot == 0 else psM
                v.tensor_copy(out=bcol[:, 0:8], in_=src[:, 0:8]).then_inc(S.dve.h, 1)
                S.dve.n += 1
                ring_ev[slot] = S.dve.n
            p.v(vab3)
            return S.dve.n

        def qd2_chunk(c, gate_pe):
            def vqd(v, gate_pe=gate_pe):
                if gate_pe:
                    v.wait_ge(S.pe.h, gate_pe)
                for qb in range(8):
                    dst = QD2[qb]
                    v.tensor_copy(out=dst[0:64, 0:256], in_=QH[qb][0:64, :])
                    v.tensor_copy(out=dst[0:64, 256:512], in_=QH[qb][64:128, :])
                    v.tensor_copy(out=dst[64:128, 0:256], in_=QH[qb][0:64, :])
                    v.tensor_copy(out=dst[64:128, 256:512], in_=QH[qb][64:128, :])
                v.drain()
                v.sem_inc(S.dve.h, 1)
                S.dve.n += 1
            p.v(vqd)
            return S.dve.n

        def k_slice(kc):
            g_k = kc // 2
            return KH[g_k // 2][(g_k % 2) * 64:(g_k % 2) * 64 + 64, (kc % 2) * 128:(kc % 2) * 128 + 128]

        def v_slice(kc):
            return VT65[kc % 2][:, 65 * (kc // 2):65 * (kc // 2) + 65]

        ev_list = [kc for kc in range(32) if (kc // 2) % 2 == 0]
        od_list = [kc for kc in range(32) if (kc // 2) % 2 == 1]
        pairs = list(zip(ev_list, od_list))

        state = {"vev": [0], "exp_at": {}, "av_pe": {}}
        NCH = int(_os.environ.get("K_NCH", "8"))
        SEG = int(_os.environ.get("K_SEG", "5"))
        for ci in range(NCH + 1):
            ringq = []
            if ci < NCH:
                ringq.append(lambda c=ci: ln_chunk(c))
                for j in range(NFC):
                    ringq.append(lambda c=ci, j=j: proj_group(c, "q", j))
                for j0 in range(0, NFC, 2):
                    ringq.append(lambda c=ci, j0=j0: norm_pair(c, 0, j0))
            if ci >= 1 and SEG >= 4:
                bcol = self.BCOL if (ci - 1) % 2 == 0 else BCO2
                self.outer_attn_merged(p, ci - 1, psS, psAV, PT, QD2, XO,
                                       k_slice, v_slice, pairs, state, ringq, RB, bcol)
            for f in ringq:
                f()
            if ci < NCH and SEG >= 2:
                for j in range(NFC):
                    proj_group(ci, "k", j)
                for sh in range(2):
                    for ob in range(2):
                        vproj_group(ci, sh, ob)
            if ci < NCH and SEG >= 3:
                SUB = _os.environ.get("K_SUB", "full")
                for j0 in range(0, NFC, 2):
                    norm_pair(ci, 1, j0)
            if ci >= 1 and SEG >= 5:
                self.outer_wo_merged(p, ci - 1, WO, XO, WOUT, SQ, psums, ring_ev, state)
            if ci < NCH and SEG >= 3:
                state["bias", ci] = bias_chunk(ci) if SUB in ("nb", "nbq", "full") else 0
                state["qd2", ci] = qd2_chunk(ci, max(state.get(("qklast", ci - 1), 0),
                                                      state.get(("wo_pe", ci - 1), 0))) if SUB in ("nbq", "full") else 0
                if SUB in ("n", "nb", "nbq"):
                    state["vt", ci] = 0
                    continue

                def vones(v, ci=ci):
                    for sh in range(2):
                        for g2 in range(16):
                            v.memset(VT65[sh][:, 65 * g2 + 64:65 * g2 + 65], 1.0)
                    v.drain()
                    v.sem_inc(S.dve.h, 1)
                    S.dve.n += 1
                p.v(vones)
                state["vt", ci] = S.dve.n
            elif ci < NCH:
                state["bias", ci] = 0
                state["qd2", ci] = 0
                state["vt", ci] = 0
        p.emit()
        pst.close()
        st.close()

    def ring_slot(self):
        self._rslot ^= 1
        return self._rslot

    def outer_attn_merged(self, p, c, psS, psAV, PT, QD2, XO, k_slice,
                          v_slice, pairs, state, ringq, RB, bcol):
        S = self
        base = c * 128
        qk_pe = {}
        exp_at = state["exp_at"]
        av_pe = state["av_pe"]
        vev_done = state["vev"]
        vev_par = state.setdefault("vevp", [0, 0])
        srec_at = state.setdefault("srec", {})
        n_ready = max(state[("bias", c)], state[("qd2", c)], state[("vt", c)])

        def emit_av(qb, kg):
            kce, kco = pairs[kg]
            gk = base + qb * 16 + kg

            def tav(t, qb=qb, kg=kg, kce=kce, kco=kco, gk=gk, vp=vev_par[qb % 2]):
                t.wait_ge(S.act.h, exp_at[gk])
                if kg == 0 and vp:
                    t.wait_ge(S.dve.h, vp)
                ps = psAV[qb % 2]
                t.matmul(ps[:], v_slice(kce), PT[kg % 2][:, 0:512], start=(kg == 0), stop=False)
                t.matmul(ps[:], v_slice(kco), PT[kg % 2][:, 512:1024], start=False, stop=(kg == 15)).then_inc(S.pe.h, 1)
                S.pe.n += 1
            p.t(tav)
            av_pe[gk] = S.pe.n

        def emit_fin(qb):
            nrec = srec_at[qb]

            def trb(t, qb=qb, nrec=nrec):
                t.wait_ge(S.act.h, nrec)
                t.matmul(psS[1][0:64, 0:512], self.ONESF[0:1, 0:64], self.STAT[0:1, 0:512], start=True, stop=True).then_inc(S.pe.h, 1)
                S.pe.n += 1
            p.t(trb)
            nrb = S.pe.n

            def vev(v, qb=qb, nrb=nrb):
                v.wait_ge(S.pe.h, nrb)
                v.tensor_copy(out=RB, in_=psS[1][0:64, 0:512])
                v.drain()
                ps = psAV[qb % 2]
                v.tensor_tensor(out=XO[qb][0:64, :], in0=ps[0:64, 0:256], in1=RB[:, 0:256], op=ALU.mult)
                v.tensor_tensor(out=XO[qb][64:128, :], in0=ps[0:64, 256:512], in1=RB[:, 256:512], op=ALU.mult)
                v.drain()
                v.sem_inc(S.dve.h, 1)
                S.dve.n += 1
            p.v(vev)
            vev_done[0] = S.dve.n
            vev_par[qb % 2] = S.dve.n

        for qb in range(8):
            for kg, (kce, kco) in enumerate(pairs):
                gk = base + qb * 16 + kg

                def tqk(t, kg=kg, kce=kce, kco=kco, qb=qb, gk=gk, vev_prev=vev_done[0]):
                    if qb == 0 and kg == 0:
                        t.wait_ge(S.dve.h, n_ready)
                    if kg == 1 and vev_prev:
                        t.wait_ge(S.dve.h, vev_prev)
                    if gk >= 2 and (gk - 2) in exp_at:
                        t.wait_ge(S.act.h, exp_at[gk - 2])
                    sl = psS[kg % 2]
                    t.matmul(sl[:, 0:512], k_slice(kce), QD2[qb][0:64, :], start=True, stop=True)
                    t.matmul(sl[:, 512:1024], k_slice(kco), QD2[qb][64:128, :], start=True, stop=True).then_inc(S.pe.h, 1)
                    S.pe.n += 1
                p.t(tqk)
                qk_pe[gk] = S.pe.n
                if kg == 0 and qb >= 1:
                    emit_fin(qb - 1)

                def sexp(s, kg=kg, qb=qb, gk=gk):
                    s.wait_ge(S.pe.h, qk_pe[gk])
                    if gk >= 2 and (gk - 2) in av_pe:
                        s.wait_ge(S.pe.h, av_pe[gk - 2])
                    s.activation(out=PT[kg % 2][:], in_=psS[kg % 2][:], func=AF.Exp,
                                 bias=bcol[:, qb:qb + 1], scale=0.125).then_inc(S.act.h, 1)
                    S.act.n += 1
                p.s(sexp)
                exp_at[gk] = S.act.n
                if kg >= 1:
                    emit_av(qb, kg - 1)
                if kg == 15:
                    emit_av(qb, 15)
            n_av = av_pe[base + qb * 16 + 15]

            def srec(s, qb=qb, n_av=n_av):
                s.wait_ge(S.pe.h, n_av)
                s.activation(out=self.STAT[0:1, 0:512], in_=psAV[qb % 2][64:65, :], func=AF.Ln)
                s.drain()
                s.activation(out=self.STAT[0:1, 0:512], in_=self.STAT[0:1, 0:512], func=AF.Exp, scale=-1.0)
                s.drain()
                s.sem_inc(S.act.h, 1)
                S.act.n += 1
            p.s(srec)
            srec_at[qb] = S.act.n
            if not _os.environ.get("K_NOWEAVE"):
                npop = 1 if qb < 4 else 2
                for _ in range(npop):
                    if ringq:
                        ringq.pop(0)()
        emit_fin(7)
        state[("qklast", c)] = qk_pe[base + 7 * 16 + 15]
        state[("xo", c)] = vev_done[0]

    def outer_wo_merged(self, p, c, WO, XO, WOUT, SQ, psums, ring_ev, state):
        nc, S = self.nc, self
        cols = slice(1 + 256 * c, 1 + 256 * c + 256)
        psR, psM = psums["R"], psums["M"]
        n_xo = state[("xo", c)]
        for hc in range(NFC):
            slot = self.ring_slot()

            def tmm(t, hc=hc, slot=slot, claim=ring_ev[slot]):
                t.wait_ge(S.dve.h, max(claim, n_xo))
                dst = psR if slot == 0 else psM
                for ot in range(NFC):
                    mm = t.matmul(dst[:, 0:256], WO[ot][:, 128 * hc:128 * (hc + 1)], XO[ot][:], start=(ot == 0), stop=(ot == NFC - 1))
                mm.then_inc(S.pe.h, 1)
                S.pe.n += 1
            p.t(tmm)
            npe = S.pe.n

            def vev(v, hc=hc, npe=npe, slot=slot):
                v.wait_ge(S.pe.h, npe)
                src = psR if slot == 0 else psM
                v.tensor_copy(out=WOUT[hc][:], in_=src[:, 0:256]).then_inc(S.dve.h, 1)
                S.dve.n += 1
                ring_ev[slot] = S.dve.n
            p.v(vev)
        self.ln_ring(p, psums, ring_ev, WOUT, WOUT, slice(0, 256), self.XA,
                     out_cols=cols, sq_scr=SQ, res_tiles=self.XA, res_cols=cols)
        state[("wo_pe", c)] = S.pe.n
        nd = S.dve.n
        bspo = self._bspo

        def ysp(sync, c=c, nd=nd):
            sync.wait_ge(S.dve.h, nd)
            for i in range(NFC):
                sync.dma_start(out=self.xo_spill.ap()[128 * i:128 * (i + 1), 256 * c:256 * c + 256],
                               in_=self.XA[i][:, cols]).then_inc(bspo.h, 16)
            bspo.n += 16 * NFC
        p.y(ysp)

    def ex_send(self, p, name, sends, extra=None):
        S = self
        bs = self.hw_batch()
        rounds = [(name, sends)] + ([extra] if extra else [])

        def y(sync):
            for nm, snds in rounds:
                for dram_ap, sb_ap in snds:
                    sync.dma_start(out=dram_ap, in_=sb_ap).then_inc(bs.h, 16)
                    bs.n += 16
        p.y(y)
        ndx = bs.n

        def g(gp):
            gp.wait_ge(bs.h, ndx)
            for nm, snds in rounds:
                gp.collective_compute(
                    "AllGather", ALU.bypass,
                    replica_groups=[[0, 1], [2, 3], [4, 5], [6, 7]],
                    ins=[self.ex[nm][0].ap()], outs=[self.ex[nm][1].ap()],
                ).then_inc(S.cc.h, 1)
                S.cc.n += 1
        p.g(g)
        return S.cc.n

    def ex_send_cols(self, p, name, col_ap_list):
        S = self

        def v(vv):
            for i, ap in enumerate(col_ap_list):
                vv.tensor_copy(out=self.HS[:, i:i + 1], in_=ap)
            vv.drain()
            vv.sem_inc(S.dve.h, 1)
            S.dve.n += 1
        p.v(v)
        nv = S.dve.n

        def y(sync):
            sync.wait_ge(S.dve.h, nv)
        p.y(y)
        return self.ex_send(p, name, [(self.ex[name][0].ap()[:, :], self.HS[:, :])])

    def ex_recv(self, p, name, cc_at, recvs):
        S = self
        bs = self.hw_batch()

        def y(sync):
            sync.wait_ge(S.cc.h, cc_at)
            for sb_ap, dram_ap in recvs:
                sync.dma_start(out=sb_ap, in_=dram_ap).then_inc(bs.h, 16)
                bs.n += 16
        p.y(y)
        return (bs, bs.n)

    def ph_x0(self):
        nc, S = self.nc, self
        st = ExitStack()
        psA = st.enter_context(nc.psum_tensor("x0psA", [1, 512], F32))
        psB = st.enter_context(nc.psum_tensor("x0psB", [1, 512], F32))
        psR = st.enter_context(nc.psum_tensor("x0psR", [128, 512], F32))
        psM = st.enter_context(nc.psum_tensor("x0psM", [128, 512], F32))
        scr = st.enter_context(nc.sbuf_tensor("x0scr", [128, 512], F32))
        sq = st.enter_context(nc.sbuf_tensor("x0sq", [128, 1024], BF16))
        psums = {"A": psA, "B": psB, "R": psR, "M": psM, "scr": scr}
        p = Phase(self)

        bsp = self._bspo
        nspill = bsp.n

        def tws(t):
            t.wait_ge(bsp.h, nspill)
        p.t(tws)

        def vws(v):
            v.wait_ge(bsp.h, nspill)
            for i in range(NFC):
                v.tensor_copy(out=self.XS[i][:, 1:TW], in_=self.XA[i][:, 1:TW])
            v.drain()
            v.sem_inc(S.dve.h, 1)
            S.dve.n += 1
        p.v(vws)

        for tb in [3, 0, 1, 2]:
            cols = slice(1 + 512 * tb, 1 + 512 * tb + 512)
            self.ln_block(p, psums, self.XA, self.XS, cols, self.XA,
                          sq_scr=sq, out_bf=self.XS)
            if tb == 3:
                self._ecc["x0col"] = self.ex_send_cols(p, "x0col", [self.XA[i][:, TW - 1:TW] for i in range(NFC)])
        p.emit()
        st.close()

    def recv_halo_cols(self, p, name, cc_at, scr_tile):
        S = self
        rb, ndx = self.ex_recv(p, name, cc_at, [(scr_tile[:, 0:8], self.ex[name][1].ap()[0])])

        def v(vv):
            vv.wait_ge(rb.h, ndx)
            for i in range(NFC):
                vv.tensor_scalar(out=self.XA[i][:, 0:1], in0=scr_tile[:, i:i + 1], scalar1=self.GATE[:, 0:1], scalar2=None, op0=ALU.mult)
            vv.drain()
            for i in range(NFC):
                vv.tensor_copy(out=self.XS[i][:, 0:1], in_=self.XA[i][:, 0:1])
            vv.drain()
            vv.sem_inc(S.dve.h, 1)
            S.dve.n += 1
        p.v(v)
        return S.dve.n

    def ph_ffn(self):
        for l in range(2):
            self.inner_layer(l)
            if self._stop:
                return

    def proj_norm_rope(self, p, tb, W, wcol0, scale_col, tc, ts, ps,
                       dst_tile, dst_col0, max_cell):
        S = self
        PQ, psA, psB, psRN = ps["PQ"], ps["A"], ps["B"], ps["RN"]
        KNS, TMPA, TMPB, SQ2, RNB = ps["KNS"], ps["TMPA"], ps["TMPB"], ps["SQ2"], ps["RNB"]
        STT = ps.get("STAT", self.STAT)
        MC = ps.get("MC", 0)
        c_hi = slice(1 + 512 * tb, 1 + 512 * tb + 512)
        c_lo = slice(512 * tb, 512 * tb + 512)

        def tmm(t):
            if ps["free"][0]:
                t.wait_ge(S.dve.h, ps["free"][0])
            for i in range(NFC):
                mm = t.matmul(PQ[:], W[i][:, wcol0:wcol0 + 128],
                              self.XS[i][:, c_hi if i < 4 else c_lo],
                              start=(i == 0), stop=(i == NFC - 1))
            mm.then_inc(S.pe.h, 1)
            S.pe.n += 1
        p.t(tmm)
        n_pq = S.pe.n

        def vsq(v):
            v.wait_ge(S.pe.h, n_pq)
            v.tensor_copy(out=ps["QRW"], in_=PQ[:])
            v.drain()
            v.tensor_tensor(out=ps["SQB"], in0=ps["QRW"], in1=ps["QRW"], op=ALU.mult).then_inc(S.dve.h, 1)
            S.dve.n += 1
        p.v(vsq)
        n_sq = S.dve.n

        def tsum(t):
            t.wait_ge(S.dve.h, n_sq)
            t.matmul(psA[0:1, :], self.ONESB[0:64, 0:1], ps["SQB"][0:64, :], start=True, stop=True)
            t.matmul(psB[0:1, :], self.ONESB[64:128, 0:1], ps["SQB"][64:128, :], start=True, stop=True).then_inc(S.pe.h, 1)
            S.pe.n += 1
        p.t(tsum)
        n_sums = S.pe.n

        def vrn(v):
            v.wait_ge(S.pe.h, n_sums)
            v.tensor_scalar(out=STT[0:1, 0:512], in0=psA[0:1, :], scalar1=1e-24, scalar2=None, op0=ALU.max)
            v.tensor_scalar(out=STT[0:1, 512:1024], in0=psB[0:1, :], scalar1=1e-24, scalar2=None, op0=ALU.max).then_inc(S.dve.h, 1)
            S.dve.n += 1
        p.v(vrn)
        n_st = S.dve.n

        def ssqrt(s):
            s.wait_ge(S.dve.h, n_st)
            s.activation(out=STT[0:1, :], in_=STT[0:1, :], func=AF.Ln)
            s.drain()
            s.activation(out=STT[0:1, :], in_=STT[0:1, :], func=AF.Exp, scale=-0.5)
            s.drain()
            s.sem_inc(S.act.h, 1)
            S.act.n += 1
        p.s(ssqrt)
        n_rec = S.act.n

        def tbc(t):
            t.wait_ge(S.act.h, n_rec)
            t.matmul(psRN[0:64, :], self.ONESF[0:1, 0:64], STT[0:1, 0:512], start=True, stop=True)
            t.matmul(psRN[64:128, :], self.ONESF[0:1, 0:64], STT[0:1, 512:1024], start=True, stop=True).then_inc(S.pe.h, 1)
            S.pe.n += 1
        p.t(tbc)
        n_bc = S.pe.n

        def vkn(v):
            v.wait_ge(S.pe.h, n_bc)
            v.tensor_copy(out=RNB[:], in_=psRN[:])
            v.drain()
            v.scalar_tensor_tensor(out=KNS[:], in0=ps["QRW"], scalar=self.QKS[:, scale_col:scale_col + 1],
                                   in1=RNB[:], op0=ALU.mult, op1=ALU.mult)
            v.drain()
            v.tensor_tensor(out=SQ2[:], in0=KNS[:], in1=KNS[:], op=ALU.mult).then_inc(S.dve.h, 1)
            S.dve.n += 1
        p.v(vkn)
        n_kn = S.dve.n

        def tnm(t):
            t.wait_ge(S.dve.h, n_kn)
            t.matmul(psA[0:1, :], self.ONESB[0:64, 0:1], SQ2[0:64, :], start=True, stop=True)
            t.matmul(psB[0:1, :], self.ONESB[64:128, 0:1], SQ2[64:128, :], start=True, stop=True).then_inc(S.pe.h, 1)
            S.pe.n += 1
        p.t(tnm)
        n_nm = S.pe.n

        def vmax(v):
            v.wait_ge(S.pe.h, n_nm)
            v.tensor_reduce(out=self.MSC[0:1, MC:MC + 1], in_=psA[0:1, :], axis=AX.X, op=ALU.max)
            v.tensor_reduce(out=self.MSC[0:1, MC + 1:MC + 2], in_=psB[0:1, :], axis=AX.X, op=ALU.max)
            v.drain()
            v.tensor_tensor(out=self.MSC2[0:1, max_cell:max_cell + 1], in0=self.MSC[0:1, MC:MC + 1], in1=self.MSC[0:1, MC + 1:MC + 2], op=ALU.max)
            v.drain()
            tcs = tc[:, 512 * tb:512 * tb + 512]
            tss = ts[:, 512 * tb:512 * tb + 512]
            for b0 in range(0, 128, 32):
                r0 = b0 ^ 32
                v.tensor_copy(out=TMPA[b0:b0 + 32, :], in_=KNS[r0:r0 + 32, :])
            v.tensor_tensor(out=TMPB[:], in0=KNS[:], in1=tcs, op=ALU.mult)
            v.drain()
            v.tensor_tensor(out=TMPA[:], in0=TMPA[:], in1=tss, op=ALU.mult)
            v.drain()
            v.tensor_tensor(out=dst_tile[:, dst_col0:dst_col0 + 512], in0=TMPB[:], in1=TMPA[:], op=ALU.add)
            v.drain()
            v.sem_inc(S.dve.h, 1)
            S.dve.n += 1
            ps["free"][0] = S.dve.n
        p.v(vmax)
        return S.dve.n

    def inner_layer(self, l):
        nc, S, d = self.nc, self, self.d
        stKV = ExitStack()
        KH = [stKV.enter_context(nc.sbuf_tensor(f"iKH{l}_{j}", [128, 512 + T], BF16)) for j in range(4)]
        VI = [stKV.enter_context(nc.sbuf_tensor(f"iVI{l}_{n}", [128, 520], BF16)) for n in range(20)]
        HCOL = stKV.enter_context(nc.sbuf_tensor(f"iHCOL{l}", [128, 8], F32))
        p = Phase(self)

        def vones(v):
            for n in range(20):
                for g in range(8):
                    v.memset(VI[n][:, 65 * g + 64:65 * g + 65], 1.0)
            v.drain()
            v.sem_inc(S.dve.h, 1)
            S.dve.n += 1
        p.v(vones)
        p.emit()
        kv_name = f"kv{l}"
        nrm_name = f"nrm{l}"
        xm_name = f"xm{l}"
        stA = ExitStack()
        KNS = [stA.enter_context(nc.sbuf_tensor(f"KNS{l}_{i}", [128, 512], BF16)) for i in range(2)]
        TMPA = [stA.enter_context(nc.sbuf_tensor(f"TMPA{l}_{i}", [128, 512], BF16)) for i in range(2)]
        TMPB = [stA.enter_context(nc.sbuf_tensor(f"TMPB{l}_{i}", [128, 512], BF16)) for i in range(2)]
        SQ2 = [stA.enter_context(nc.sbuf_tensor(f"SQ2{l}_{i}", [128, 512], BF16)) for i in range(2)]
        SQB = [stA.enter_context(nc.sbuf_tensor(f"SQB{l}_{i}", [128, 512], BF16)) for i in range(2)]
        RNB2 = [stA.enter_context(nc.sbuf_tensor(f"RNB2{l}_{i}", [128, 512], F32)) for i in range(2)]
        STATB = self.SROW
        WOI = [stA.enter_context(nc.sbuf_tensor(f"WOI{l}_{i}", [128, 1024], BF16, side="right")) for i in range(4)]
        stW = ExitStack()
        WKV = [stW.enter_context(nc.sbuf_tensor(f"WKV{l}_{i}", [128, 1024], BF16)) for i in range(NFC)]

        pst = ExitStack()
        PQ = [pst.enter_context(nc.psum_tensor(f"iPQ{l}_{i}", [128, 512], F32)) for i in range(2)]
        psA = [pst.enter_context(nc.psum_tensor(f"ipsA{l}_{i}", [1, 512], F32)) for i in range(2)]
        psB = [pst.enter_context(nc.psum_tensor(f"ipsB{l}_{i}", [1, 512], F32)) for i in range(2)]
        psRN = [pst.enter_context(nc.psum_tensor(f"ipsRN{l}_{i}", [128, 512], F32)) for i in range(2)]
        p = Phase(self)

        bw = self.sw_batch()

        def gw(gp):
            w = d[f"wqkv{l}"]
            for i in range(NFC):
                gp.dma_start(out=WKV[i][:], in_=w[128 * i:128 * (i + 1), 512:1536]).then_inc(bw.h, 16)
            wo = d[f"wo{l}"]
            for i in range(4):
                gp.dma_start(out=WOI[i][:], in_=wo[128 * i:128 * (i + 1), :]).then_inc(bw.h, 16)
            bw.n += 16 * (NFC + 4)
        p.g(gw)
        n_w = bw.n

        colname = "x0col" if l == 0 else "x1col"
        n_xcol = self.recv_halo_cols(p, colname, self._ecc[colname], HCOL)

        def tww(t):
            t.wait_ge(bw.h, n_w)
        p.t(tww)

        pss = [{"PQ": PQ[i], "A": psA[i], "B": psB[i], "RN": psRN[i], "KNS": KNS[i],
                "TMPA": TMPA[i], "TMPB": TMPB[i], "SQ2": SQ2[i], "RNB": RNB2[i],
                "SQB": SQB[i][:], "QRW": TMPA[i][:], "free": [0],
                "STAT": (self.STAT if i == 0 else STATB),
                "MC": (0 if i == 0 else 2)} for i in range(2)]
        OT = [KNS[1], TMPA[1], TMPB[1], SQB[1]]
        pqi = [0]
        for tb in [3, 0, 1, 2]:
            if tb == 0:
                def twx(t):
                    t.wait_ge(S.dve.h, n_xcol)
                p.t(twx)

                def vwx(v):
                    v.wait_ge(S.dve.h, n_xcol)
                p.v(vwx)
            for j in range(4):
                self.proj_norm_rope(p, tb, WKV, 128 * j, 2 * l + 1, self.TCK, self.TSK,
                                    pss[pqi[0] % 2], KH[j], 512 + 512 * tb, 16 + 5 * j + 1 + tb)
                pqi[0] += 1
            for n4 in range(4):
                tt = 4 * tb + n4
                slot = pqi[0] % 2
                pqi[0] += 1
                ps_v = pss[slot]

                def tvm(t, tb=tb, n4=n4, slot=slot, ps_v=ps_v):
                    t.wait_ge(S.dve.h, ps_v["free"][0])
                    dst = PQ[slot]
                    for i in range(NFC):
                        c0 = (1 if i < 4 else 0) + 512 * tb + 128 * n4
                        mm = t.matmul(dst[:], self.XS[i][:, c0:c0 + 128], WKV[i][:, 512:1024], start=(i == 0), stop=(i == NFC - 1))
                    mm.then_inc(S.pe.h, 1)
                    S.pe.n += 1
                p.t(tvm)
                npe = S.pe.n

                def vvev(v, tt=tt, npe=npe, slot=slot, ps_v=ps_v):
                    v.wait_ge(S.pe.h, npe)
                    src = PQ[slot]
                    for g in range(8):
                        v.tensor_copy(out=VI[4 + tt][:, 65 * g:65 * g + 64], in_=src[:, 64 * g:64 * (g + 1)])
                    v.drain()
                    v.sem_inc(S.dve.h, 1)
                    S.dve.n += 1
                    ps_v["free"][0] = S.dve.n
                p.v(vvev)
            if tb == 3:
                n_tb3 = S.dve.n

                def ysw(sync):
                    sync.wait_ge(S.dve.h, n_tb3)
                p.y(ysw)
                kvsrc = self.ex[kv_name][0]
                sends = []
                for j in range(4):
                    sends.append((kvsrc.ap()[128 * j:128 * (j + 1), 0:512], KH[j][:, 2048:2560]))
                for n in range(4):
                    sends.append((kvsrc.ap()[128 * n:128 * (n + 1), 512:1032], VI[16 + n][:]))
                nsend = []
                for j in range(4):
                    nsend.append((self.ex[nrm_name][0].ap()[0:1, j:j + 1], self.MSC2[0:1, 16 + 5 * j + 4:16 + 5 * j + 5]))
                cc_kv = self.ex_send(p, kv_name, sends, extra=(nrm_name, nsend))
                recvs = []
                for j in range(4):
                    recvs.append((KH[j][:, 0:512], self.ex[kv_name][1].ap()[0][128 * j:128 * (j + 1), 0:512]))
                for n in range(4):
                    recvs.append((VI[n][:], self.ex[kv_name][1].ap()[0][128 * n:128 * (n + 1), 512:1032]))
                recvs.append((self.MSC[0:1, 4:8], self.ex[nrm_name][1].ap()[0][0:1, 0:4]))
                rkb, ndx_kv = self.ex_recv(p, kv_name, cc_kv, recvs)

                def vnrm(v):
                    v.wait_ge(rkb.h, ndx_kv)
                    for j in range(4):
                        v.tensor_copy(out=self.MSC2[0:1, 16 + 5 * j:16 + 5 * j + 1], in_=self.MSC[0:1, 4 + j:5 + j])
                    v.drain()
                    v.sem_inc(S.dve.h, 1)
                    S.dve.n += 1
                p.v(vnrm)
                n_kvhalo = S.dve.n
        p.emit()
        pst.close()
        stW.close()
        self._n_kvhalo = n_kvhalo

        stB = ExitStack()
        PT = [stB.enter_context(nc.sbuf_tensor(f"PT{l}_{i}", [128, 1024], BF16)) for i in range(5)]
        QW = [stB.enter_context(nc.sbuf_tensor(f"iQW{l}_{j}", [128, 512], BF16)) for j in range(4)]
        WQ1 = [stB.enter_context(nc.sbuf_tensor(f"WQ1{l}_{i}", [128, 512], BF16)) for i in range(NFC)]
        pq1 = Phase(self)
        bq = self.sw_batch()

        def gq(gp):
            wq = self.d[f"wqkv{l}"]
            for i in range(NFC):
                gp.dma_start(out=WQ1[i][:], in_=wq[128 * i:128 * (i + 1), 0:512]).then_inc(bq.h, 16)
            bq.n += 16 * NFC
        pq1.g(gq)
        nbq = bq.n

        def tq1(t):
            t.wait_ge(bq.h, nbq)
        pq1.t(tq1)
        pq1.emit()
        for w in [1, 2, 3, 0]:
            self.inner_window(l, w, KH, VI, QW, WQ1, OT, WOI, PT, RNB2[0],
                              KNS, TMPA, TMPB, SQ2, SQB, RNB2, STATB, HCOL)
        stB.close()
        p = Phase(self)

        def vsh(v):
            for i in range(NFC):
                v.tensor_copy(out=self.XS[i][:, 1:TW], in_=self.XA[i][:, 1:TW])
            v.drain()
            v.sem_inc(S.dve.h, 1)
            S.dve.n += 1
        p.v(vsh)
        p.emit()
        stA.close()
        stKV.close()
        if self.stage == f"L{l}attn":
            self.ph_emit_stream(self.XA)
            self._stop = True
            return

        self.inner_ffn(l)
        if self.stage == f"L{l}":
            self.ph_emit_stream(self.XA)
            self._stop = True

    def inner_window(self, l, w, KH, VI, QW, WQ1, OT, WOI, PT, RNB,
                     KNS, TMPA, TMPB, SQ2, SQB, RNB2, STATB, HCOL):
        nc, S = self.nc, self
        st = ExitStack()
        p = Phase(self)
        pq_st = ExitStack()
        PQ = [pq_st.enter_context(nc.psum_tensor(f"qPQ{l}_{w}_{i}", [128, 512], F32)) for i in range(2)]
        psA = [pq_st.enter_context(nc.psum_tensor(f"qpsA{l}_{w}_{i}", [1, 512], F32)) for i in range(2)]
        psB = [pq_st.enter_context(nc.psum_tensor(f"qpsB{l}_{w}_{i}", [1, 512], F32)) for i in range(2)]
        psRN = [pq_st.enter_context(nc.psum_tensor(f"qpsRN{l}_{w}_{i}", [128, 512], F32)) for i in range(2)]
        pssq = [{"PQ": PQ[i], "A": psA[i], "B": psB[i], "RN": psRN[i], "KNS": KNS[i],
                 "TMPA": TMPA[i], "TMPB": TMPB[i], "SQ2": SQ2[i], "RNB": RNB2[i],
                 "SQB": SQB[i][:], "QRW": TMPA[i][:], "free": [0],
                 "STAT": (self.STAT if i == 0 else STATB),
                 "MC": (0 if i == 0 else 2)} for i in range(2)]
        for j in range(4):
            self.proj_norm_rope(p, w, WQ1, 128 * j, 2 * l, self.TCQ, self.TSQ,
                                pssq[j % 2], QW[j], 0, 4 * j + w)
        n_qdone = S.dve.n
        pq_st.close()

        at_st = ExitStack()
        psS = [at_st.enter_context(nc.psum_tensor(f"apsS{l}_{w}_{i}", [128, 1024], F32)) for i in range(2)]
        psAV = [at_st.enter_context(nc.psum_tensor(f"apsAV{l}_{w}_{i}", [65, 512], F32)) for i in range(2)]
        psBC = at_st.enter_context(nc.psum_tensor(f"apsBC{l}_{w}", [64, 512], F32))

        def vb1(v):
            for hp in range(4):
                v.tensor_tensor(out=self.MSC[0:1, hp:hp + 1],
                                in0=self.MSC2[0:1, 16 + 5 * hp + w:17 + 5 * hp + w],
                                in1=self.MSC2[0:1, 17 + 5 * hp + w:18 + 5 * hp + w], op=ALU.max)
            v.drain()
            for hp in range(4):
                v.tensor_tensor(out=self.MSC[0:1, hp:hp + 1], in0=self.MSC[0:1, hp:hp + 1],
                                in1=self.MSC2[0:1, 4 * hp + w:4 * hp + w + 1], op=ALU.mult)
            v.drain()
            v.sem_inc(S.dve.h, 1)
            S.dve.n += 1
        p.v(vb1)
        nb1 = S.dve.n

        def sb1(s):
            s.wait_ge(S.dve.h, nb1)
            s.activation(out=self.MSC[0:1, 0:4], in_=self.MSC[0:1, 0:4], func=AF.Sqrt)
            s.drain()
            s.sem_inc(S.act.h, 1)
            S.act.n += 1
        p.s(sb1)
        nb2 = S.act.n

        def vb2(v):
            v.wait_ge(S.act.h, nb2)
            v.tensor_scalar(out=self.MSC[0:1, 0:4], in0=self.MSC[0:1, 0:4], scalar1=-8.0, scalar2=None, op0=ALU.mult).then_inc(S.dve.h, 1)
            S.dve.n += 1
        p.v(vb2)
        nb3 = S.dve.n

        def tb1(t):
            t.wait_ge(S.dve.h, nb3)
            t.matmul(psS[0][:, 0:4], self.ONESF[0:1, :], self.MSC[0:1, 0:4], start=True, stop=True).then_inc(S.pe.h, 1)
            S.pe.n += 1
        p.t(tb1)
        nb4 = S.pe.n

        def vb3(v):
            v.wait_ge(S.pe.h, nb4)
            v.tensor_copy(out=self.BCOL[:, 0:4], in_=psS[0][:, 0:4]).then_inc(S.dve.h, 1)
            S.dve.n += 1
        p.v(vb3)
        n_bias = S.dve.n

        def mask_sites():
            sites = []
            for kc in range(8):
                if w == 0 and kc < 4:
                    for qb in range(0, min(kc, 3) + 1):
                        sites.append((kc, qb, self.MW0D0 if qb == kc else self.MW0F))
                elif kc <= 3:
                    sites.append((kc, kc, self.MD0))
                if kc >= 4:
                    sites.append((kc, kc - 4, self.MD4))
            return sites
        SITES = mask_sites()

        qk_pe = {}
        exp_at = {}
        av_pe = {}
        ev_dve = {}
        for hp in range(4):
            for kc in range(9):
                kci = hp * 8 + kc
                if kc < 8:
                    kcc = 512 * w + 128 * kc
                    c0 = max(0, 128 * (kc - 4))
                    c1 = min(512, 128 * (kc + 1))

                    def tqk(t, hp=hp, kc=kc, kci=kci, kcc=kcc, c0=c0, c1=c1):
                        if kci == 0:
                            t.wait_ge(S.dve.h, max(n_qdone, n_bias, self._n_kvhalo if w == 0 else 0))
                        if kci >= 2:
                            t.wait_ge(S.act.h, exp_at[kci - 2])
                        sl = psS[kci % 2]
                        t.matmul(sl[:, c0:c1], KH[hp][0:64, kcc:kcc + 128], QW[hp][0:64, c0:c1], start=True, stop=True)
                        t.matmul(sl[:, 512 + c0:512 + c1], KH[hp][64:128, kcc:kcc + 128], QW[hp][64:128, c0:c1], start=True, stop=True).then_inc(S.pe.h, 1)
                        S.pe.n += 1
                    p.t(tqk)
                    qk_pe[kci] = S.pe.n

                    msites = [(qb, m) for (kc2, qb, m) in SITES if kc2 == kc]
                    if msites:
                        def vmask(v, hp=hp, kc=kc, kci=kci, msites=msites):
                            v.wait_ge(S.pe.h, qk_pe[kci])
                            for qb, m in msites:
                                v.tensor_tensor(out=psS[kci % 2][:, 128 * qb:128 * qb + 128], in0=psS[kci % 2][:, 128 * qb:128 * qb + 128], in1=m[:], op=ALU.add)
                                v.tensor_tensor(out=psS[kci % 2][:, 512 + 128 * qb:512 + 128 * qb + 128], in0=psS[kci % 2][:, 512 + 128 * qb:512 + 128 * qb + 128], in1=m[:], op=ALU.add)
                            v.drain()
                            v.sem_inc(S.dve.h, 1)
                            S.dve.n += 1
                        p.v(vmask)
                        n_mask = S.dve.n
                    else:
                        n_mask = None

                    def sexp(s, hp=hp, kc=kc, kci=kci, n_mask=n_mask, c0=c0, c1=c1):
                        if kci == 0:
                            s.wait_ge(S.dve.h, n_bias)
                        s.wait_ge(S.pe.h, qk_pe[kci])
                        if n_mask is not None:
                            s.wait_ge(S.dve.h, n_mask)
                        if kci >= 5:
                            kprev = kci - 5
                            s.wait_ge(S.pe.h, av_pe.get((kprev // 8, min(3, kprev % 8)), 0))
                        s.activation(out=PT[kci % 5][:, c0:c1], in_=psS[kci % 2][:, c0:c1], func=AF.Exp,
                                     bias=self.BCOL[:, hp:hp + 1], scale=8.0)
                        s.activation(out=PT[kci % 5][:, 512 + c0:512 + c1], in_=psS[kci % 2][:, 512 + c0:512 + c1], func=AF.Exp,
                                     bias=self.BCOL[:, hp:hp + 1], scale=8.0).then_inc(S.act.h, 1)
                        S.act.n += 1
                    p.s(sexp)
                    exp_at[kci] = S.act.n

                if kc >= 5:
                    qb = kc - 5

                    def tav(t, hp=hp, kc=kc, qb=qb, kci=kci):
                        t.wait_ge(S.act.h, exp_at[hp * 8 + qb + 4])
                        if qb == 0 and hp >= 1:
                            t.wait_ge(S.dve.h, ev_dve.get(hp - 1, 0))
                        for head in range(2):
                            for k5 in range(qb, qb + 5):
                                mm = t.matmul(psAV[head][:, 128 * qb:128 * qb + 128],
                                              VI[4 * w + k5][:, 65 * (2 * hp + head):65 * (2 * hp + head) + 65],
                                              PT[(hp * 8 + k5) % 5][:, 512 * head + 128 * qb:512 * head + 128 * qb + 128],
                                              start=(k5 == qb), stop=(k5 == qb + 4))
                        mm.then_inc(S.pe.h, 1)
                        S.pe.n += 1
                    p.t(tav)
                    av_pe[(hp, qb)] = S.pe.n
            for qb in [3]:
                def tav2(t, hp=hp, qb=qb):
                    t.wait_ge(S.act.h, exp_at[hp * 8 + 7])
                    for head in range(2):
                        for k5 in range(qb, qb + 5):
                            mm = t.matmul(psAV[head][:, 128 * qb:128 * qb + 128],
                                          VI[4 * w + k5][:, 65 * (2 * hp + head):65 * (2 * hp + head) + 65],
                                          PT[(hp * 8 + k5) % 5][:, 512 * head + 128 * qb:512 * head + 128 * qb + 128],
                                          start=(k5 == qb), stop=(k5 == qb + 4))
                    mm.then_inc(S.pe.h, 1)
                    S.pe.n += 1
                p.t(tav2)
                av_pe[(hp, 3)] = S.pe.n
            n_av = S.pe.n

            def srec2(s, hp=hp, n_av=n_av):
                s.wait_ge(S.pe.h, n_av)
                s.activation(out=self.STAT[0:1, 0:512], in_=psAV[0][64:65, :], func=AF.Ln)
                s.activation(out=self.STAT[0:1, 512:1024], in_=psAV[1][64:65, :], func=AF.Ln)
                s.drain()
                s.activation(out=self.STAT[0:1, :], in_=self.STAT[0:1, :], func=AF.Exp, scale=-1.0)
                s.drain()
                s.sem_inc(S.act.h, 1)
                S.act.n += 1
            p.s(srec2)
            n_rc = S.act.n

            def trb(t, hp=hp):
                t.wait_ge(S.act.h, n_rc)
                t.matmul(psBC[:, :], self.ONESF[0:1, 0:64], self.STAT[0:1, 0:512], start=True, stop=True).then_inc(S.pe.h, 1)
                S.pe.n += 1
            p.t(trb)
            n_rb = S.pe.n

            def vev(v, hp=hp, n_rb=n_rb):
                v.wait_ge(S.pe.h, n_rb)
                v.tensor_copy(out=RNB[0:64, :], in_=psBC[:, :])
                v.drain()
                v.tensor_tensor(out=OT[hp][0:64, :], in0=psAV[0][0:64, :], in1=RNB[0:64, :], op=ALU.mult).then_inc(S.dve.h, 1)
                S.dve.n += 1
            p.v(vev)
            n_ev1 = S.dve.n

            def trb2(t, hp=hp):
                t.wait_ge(S.dve.h, n_ev1)
                t.matmul(psBC[:, :], self.ONESF[0:1, 0:64], self.STAT[0:1, 512:1024], start=True, stop=True).then_inc(S.pe.h, 1)
                S.pe.n += 1
            p.t(trb2)
            n_rb2 = S.pe.n

            def vev2(v, hp=hp, n_rb2=n_rb2):
                v.wait_ge(S.pe.h, n_rb2)
                v.tensor_copy(out=RNB[0:64, :], in_=psBC[:, :])
                v.drain()
                v.tensor_tensor(out=OT[hp][64:128, :], in0=psAV[1][0:64, :], in1=RNB[0:64, :], op=ALU.mult)
                v.drain()
                v.sem_inc(S.dve.h, 1)
                S.dve.n += 1
            p.v(vev2)
            ev_dve[hp] = S.dve.n
        n_ot = S.dve.n

        at_st.close()
        psW = [st.enter_context(nc.psum_tensor(f"wpsW{l}_{w}_{i}", [128, 512], F32)) for i in range(2)]
        cols = slice(1 + 512 * w, 1 + 512 * w + 512)
        ev_w = []
        for hc in range(NFC):
            def two(t, hc=hc):
                if hc == 0:
                    t.wait_ge(S.dve.h, n_ot)
                if hc >= 2:
                    t.wait_ge(S.dve.h, ev_w[hc - 2])
                dst = psW[hc % 2]
                for kt in range(4):
                    mm = t.matmul(dst[:], WOI[kt][:, 128 * hc:128 * (hc + 1)], OT[kt][:], start=(kt == 0), stop=(kt == 3))
                mm.then_inc(S.pe.h, 1)
                S.pe.n += 1
            p.t(two)
            npe = S.pe.n

            def vwo(v, hc=hc, npe=npe):
                v.wait_ge(S.pe.h, npe)
                v.tensor_tensor(out=self.XA[hc][:, cols], in0=psW[hc % 2][:], in1=self.XA[hc][:, cols], op=ALU.add)
                v.drain()
                v.sem_inc(S.dve.h, 1)
                S.dve.n += 1
            p.v(vwo)
            ev_w.append(S.dve.n)
        if w == 3:
            xm_name = f"xm{l}"
            self._ecc[xm_name] = self.ex_send_cols(p, xm_name, [self.XA[i][:, TW - 1:TW] for i in range(NFC)])
        p.emit()
        st.close()

    def inner_ffn(self, l):
        nc, S, d = self.nc, self, self.d
        st = ExitStack()
        T2 = [st.enter_context(nc.sbuf_tensor(f"T2_{l}_{i}", [128, 1024], BF16)) for i in range(NFC)]
        ACTB = [st.enter_context(nc.sbuf_tensor(f"ACTB{l}_{i}", [128, 1024], BF16)) for i in range(22)]
        GLB = [st.enter_context(nc.sbuf_tensor(f"GLB{l}_0", [128, 1024], BF16))] * 2
        W1S = [st.enter_context(nc.sbuf_tensor(f"W1S{l}_{i}", [128, 2048], BF16)) for i in range(2)]
        W2S = [st.enter_context(nc.sbuf_tensor(f"W2S{l}_{i}", [128, 2816], BF16)) for i in range(2)]
        HC2 = st.enter_context(nc.sbuf_tensor(f"HC2{l}", [128, 8], F32))
        jw = [128] * 21 + [42]

        for tt in [1, 0]:
            pst = ExitStack()
            psA = pst.enter_context(nc.psum_tensor(f"fpsA{l}_{tt}", [1, 512], F32))
            psB = pst.enter_context(nc.psum_tensor(f"fpsB{l}_{tt}", [1, 512], F32))
            psR = pst.enter_context(nc.psum_tensor(f"fpsR{l}_{tt}", [128, 512], F32))
            psM = pst.enter_context(nc.psum_tensor(f"fpsM{l}_{tt}", [128, 512], F32))
            scr = pst.enter_context(nc.sbuf_tensor(f"fscr{l}_{tt}", [128, 512], F32))
            sq = GLB[0]
            psums = {"A": psA, "B": psB, "R": psR, "M": psM, "scr": scr}
            p = Phase(self)
            if tt == 0:
                xm_name = f"xm{l}"
                self.recv_halo_cols(p, xm_name, self._ecc[xm_name], HC2)
            for half in range(2):
                cols = slice(1 + 1024 * tt + 512 * half, 1 + 1024 * tt + 512 * half + 512)
                self.ln_block(p, psums, self.XA, self.XS, cols, T2,
                              out_cols=slice(512 * half, 512 * half + 512),
                              sq_scr=sq, lg_off=l * NFC, lb_off=l * NFC, shift=True)
            p.emit()
            pst.close()

            pst = ExitStack()
            psa2 = [pst.enter_context(nc.psum_tensor(f"gpsa{l}_{tt}_{i}", [128, 1024], F32)) for i in range(2)]
            psg2 = [pst.enter_context(nc.psum_tensor(f"gpsg{l}_{tt}_{i}", [128, 1024], F32)) for i in range(2)]
            p = Phase(self)
            w1 = d[f"w1{l}"]
            w1s_done = [0, 0]
            act_done = []
            for jc in range(22):
                wj = jw[jc]
                slot = jc % 2

                bj = self.sw_batch()

                def gld(gp, jc=jc, wj=wj, slot=slot):
                    if jc >= 2:
                        gp.wait_ge(S.pe.h, w1s_done[slot])
                    a0 = 128 * jc
                    g0 = 2730 + 128 * jc
                    ina = AP(tensor=w1.tensor, offset=a0, ap=[[5460, 128], [128 * 5460, 8], [1, wj]])
                    ing = AP(tensor=w1.tensor, offset=g0, ap=[[5460, 128], [128 * 5460, 8], [1, wj]])
                    gp.dma_start(out=W1S[slot][:, 0:8 * wj].rearrange("p (k w) -> p k w", k=8), in_=ina).then_inc(bj.h, 16)
                    gp.dma_start(out=W1S[slot][:, 1024:1024 + 8 * wj].rearrange("p (k w) -> p k w", k=8), in_=ing).then_inc(bj.h, 16)
                    bj.n += 32
                p.g(gld)
                n_ld = bj.n

                def tg(t, jc=jc, wj=wj, slot=slot):
                    t.wait_ge(bj.h, n_ld)
                    if jc >= 2:
                        t.wait_ge(S.dve.h, act_done[jc - 2])
                    psa, psg = psa2[slot], psg2[slot]
                    for h2 in range(2):
                        for i in range(NFC):
                            t.matmul(psa[0:wj, 512 * h2:512 * h2 + 512], W1S[slot][:, wj * i:wj * (i + 1)],
                                     T2[i][:, 512 * h2:512 * h2 + 512], start=(i == 0), stop=(i == NFC - 1))
                        for i in range(NFC):
                            mm = t.matmul(psg[0:wj, 512 * h2:512 * h2 + 512], W1S[slot][:, 1024 + wj * i:1024 + wj * (i + 1)],
                                          T2[i][:, 512 * h2:512 * h2 + 512], start=(i == 0), stop=(i == NFC - 1))
                    mm.then_inc(S.pe.h, 1)
                    S.pe.n += 1
                p.t(tg)
                w1s_done[slot] = S.pe.n
                n_mm = S.pe.n

                def sgl(s, jc=jc, wj=wj, slot=slot):
                    s.wait_ge(S.pe.h, n_mm)
                    if jc >= 1:
                        s.wait_ge(S.dve.h, act_done[jc - 1])
                    s.activation(out=GLB[slot][0:wj, :], in_=psg2[slot][0:wj, :], func=AF.Gelu,
                                 bias=self.B1C[0:wj, l * 44 + 22 + jc:l * 44 + 23 + jc], scale=1.0).then_inc(S.act.h, 1)
                    S.act.n += 1
                p.s(sgl)
                n_gl = S.act.n

                def vac(v, jc=jc, wj=wj, slot=slot):
                    v.wait_ge(S.act.h, n_gl)
                    v.scalar_tensor_tensor(out=ACTB[jc][0:wj, :], in0=psa2[slot][0:wj, :],
                                           scalar=self.B1C[0:wj, l * 44 + jc:l * 44 + jc + 1],
                                           in1=GLB[slot][0:wj, :], op0=ALU.add, op1=ALU.mult)
                    v.drain()
                    v.sem_inc(S.dve.h, 1)
                    S.dve.n += 1
                p.v(vac)
                act_done.append(S.dve.n)
            n_act_all = S.dve.n
            p.emit()
            pst.close()

            pst = ExitStack()
            psw = [pst.enter_context(nc.psum_tensor(f"hpsw{l}_{tt}_{i}", [128, 1024], F32)) for i in range(2)]
            p = Phase(self)
            w2 = d[f"w2{l}"]
            w2_done = [0, 0]
            ev_done = []
            for hc in range(NFC):
                slot = hc % 2

                bh = self.sw_batch()

                def gld2(gp, hc=hc, slot=slot):
                    if hc >= 2:
                        gp.wait_ge(S.pe.h, w2_done[slot])
                    c0 = 128 * hc
                    in1 = AP(tensor=w2.tensor, offset=c0, ap=[[1024, 128], [128 * 1024, 21], [1, 128]])
                    in2 = AP(tensor=w2.tensor, offset=21 * 128 * 1024 + c0, ap=[[1024, 42], [1, 128]])
                    gp.dma_start(out=W2S[slot][:, 0:21 * 128].rearrange("p (k w) -> p k w", k=21), in_=in1).then_inc(bh.h, 16)
                    gp.dma_start(out=W2S[slot][0:42, 21 * 128:22 * 128], in_=in2).then_inc(bh.h, 16)
                    bh.n += 32
                p.g(gld2)
                n_ld2 = bh.n

                def tw2(t, hc=hc, slot=slot):
                    t.wait_ge(bh.h, n_ld2)
                    if hc == 0:
                        t.wait_ge(S.dve.h, n_act_all)
                    if hc >= 2:
                        t.wait_ge(S.dve.h, ev_done[hc - 2])
                    for h2 in range(2):
                        for jc in range(22):
                            mm = t.matmul(psw[slot][:, 512 * h2:512 * h2 + 512],
                                          W2S[slot][0:jw[jc], 128 * jc:128 * jc + 128],
                                          ACTB[jc][0:jw[jc], 512 * h2:512 * h2 + 512],
                                          start=(jc == 0), stop=(jc == 21))
                    mm.then_inc(S.pe.h, 1)
                    S.pe.n += 1
                p.t(tw2)
                w2_done[slot] = S.pe.n
                n_mm2 = S.pe.n

                def vev3(v, hc=hc, slot=slot, tt=tt):
                    v.wait_ge(S.pe.h, n_mm2)
                    cols = slice(1 + 1024 * tt, 1 + 1024 * tt + 1024)
                    v.scalar_tensor_tensor(out=self.XA[hc][:, cols], in0=psw[slot][:],
                                           scalar=self.B2C[:, l * NFC + hc:l * NFC + hc + 1],
                                           in1=self.XA[hc][:, cols], op0=ALU.add, op1=ALU.add)
                    v.drain()
                    v.tensor_copy(out=self.XS[hc][:, cols], in_=self.XA[hc][:, cols])
                    v.drain()
                    v.sem_inc(S.dve.h, 1)
                    S.dve.n += 1
                p.v(vev3)
                ev_done.append(S.dve.n)
            if tt == 1 and l == 0:
                self._ecc["x1col"] = self.ex_send_cols(p, "x1col", [self.XA[i][:, TW - 1:TW] for i in range(NFC)])
            p.emit()
            pst.close()
        st.close()

    def ph_final(self):
        nc, S = self.nc, self
        st = ExitStack()
        XR = [[st.enter_context(nc.sbuf_tensor(f"fXR{b}_{i}", [128, 512], F32)) for i in range(NFC)] for b in range(2)]
        OUTT = [st.enter_context(nc.sbuf_tensor(f"fOT{i}", [128, 512], F32)) for i in range(NFC)]
        psA = st.enter_context(nc.psum_tensor("fpsA", [1, 512], F32))
        psB = st.enter_context(nc.psum_tensor("fpsB", [1, 512], F32))
        psR = st.enter_context(nc.psum_tensor("fpsR", [128, 512], F32))
        psM = st.enter_context(nc.psum_tensor("fpsM", [128, 512], F32))
        scr = st.enter_context(nc.sbuf_tensor("fscr", [128, 512], F32))
        sq = st.enter_context(nc.sbuf_tensor("fsq", [128, 1024], BF16))
        psums = {"A": psA, "B": psB, "R": psR, "M": psM, "scr": scr}
        p = Phase(self)
        bl = self.hw_batch()
        vno_at = {}

        def yld(tb):
            gate = vno_at.get(tb - 2)

            def y(sync, tb=tb, gate=gate):
                if gate:
                    sync.wait_ge(S.dve.h, gate)
                for i in range(NFC):
                    sync.dma_start(out=XR[tb % 2][i][:], in_=self.xo_spill.ap()[128 * i:128 * (i + 1), 512 * tb:512 * tb + 512]).then_inc(bl.h, 16)
                bl.n += 16 * NFC
            p.y(y)
            return bl.n

        ld_at = [yld(0), yld(1)]
        for tb in range(4):
            cols = slice(1 + 512 * tb, 1 + 512 * tb + 512)
            nld = ld_at[tb]

            def vw(v, nld=nld):
                v.wait_ge(bl.h, nld)
            p.v(vw)
            og = (S.sout.h, S.sout.n) if tb >= 1 else None
            self.ln_block(p, psums, self.XA, self.XS, cols, OUTT,
                          out_cols=slice(0, 512), sq_scr=sq,
                          res_tiles=XR[tb % 2], res_cols=slice(0, 512),
                          act_gate=S.act.n, out_gate=og, pe_gate=S.pe.n)
            vno_at[tb] = S.dve.n

            def yout(sync, tb=tb, nd=S.dve.n):
                sync.wait_ge(S.dve.h, nd)
                for i in range(NFC):
                    sync.dma_start(out=self.d["out"][128 * i:128 * (i + 1), 512 * tb:512 * tb + 512], in_=OUTT[i][:]).then_inc(S.sout.h, 16)
                S.sout.n += 16 * NFC
                if tb == 3:
                    sync.wait_ge(S.sout.h, S.sout.n)
            p.y(yout)
            if tb + 2 < 4:
                ld_at.append(yld(tb + 2))
        p.emit()
        st.close()

    def ph_emit_stream(self, tiles):
        S = self
        p = Phase(self)

        def y(sync):
            for i in range(NFC):
                sync.dma_start(out=self.d["out"][128 * i:128 * (i + 1), :], in_=tiles[i][:, 1:TW]).then_inc(S.sout.h, 16)
            S.sout.n += 16 * NFC
            sync.wait_ge(S.sout.h, S.sout.n)
        p.y(y)
        p.emit()


def _host_inputs(inputs, core):
    b, h = core // 2, core % 2
    sl = slice(2048 * h, 2048 * h + 2048)
    m = {}
    x = np.asarray(inputs["L0_attn_x"][b, sl], dtype=np.float32)
    m["x"] = np.ascontiguousarray(x.T)
    m["wqT"] = np.ascontiguousarray(np.asarray(inputs["L0_attn_wq"], np.float32).T)
    m["wkT"] = np.ascontiguousarray(np.asarray(inputs["L0_attn_wk"], np.float32).T)
    m["wvT"] = np.ascontiguousarray(np.asarray(inputs["L0_attn_wv"], np.float32).T)
    m["woT"] = np.ascontiguousarray(np.asarray(inputs["L0_attn_wo"], np.float32).T)
    for l in range(2):
        m[f"wqkv{l}"] = np.ascontiguousarray(np.asarray(inputs["L0_ffn_wqkv"][l], np.float32))
        m[f"wo{l}"] = np.ascontiguousarray(np.asarray(inputs["L0_ffn_wo"][l], np.float32))
        m[f"w1{l}"] = np.ascontiguousarray(np.asarray(inputs["L0_ffn_w1"][l], np.float32))
        m[f"w2{l}"] = np.ascontiguousarray(np.asarray(inputs["L0_ffn_w2"][l], np.float32))
    lg = np.zeros((128, 2 * NFC), np.float32)
    lb = np.zeros((128, 2 * NFC), np.float32)
    for l in range(2):
        lg[:, l * NFC:(l + 1) * NFC] = np.asarray(inputs["L0_ffn_ln_g"][l], np.float32).reshape(NFC, 128).T
        lb[:, l * NFC:(l + 1) * NFC] = np.asarray(inputs["L0_ffn_ln_b"][l], np.float32).reshape(NFC, 128).T
    m["lgcols"] = lg
    m["lbcols"] = lb
    b1 = np.zeros((128, 2 * 44), np.float32)
    for l in range(2):
        v = np.asarray(inputs["L0_ffn_b1"][l], np.float32)
        a = np.zeros(22 * 128, np.float32); a[:2730] = v[:2730]
        g = np.zeros(22 * 128, np.float32); g[:2730] = v[2730:]
        b1[:, l * 44:l * 44 + 22] = a.reshape(22, 128).T
        b1[:, l * 44 + 22:l * 44 + 44] = g.reshape(22, 128).T
    m["b1cols"] = b1
    b2 = np.zeros((128, 2 * NFC), np.float32)
    for l in range(2):
        b2[:, l * NFC:(l + 1) * NFC] = np.asarray(inputs["L0_ffn_b2"][l], np.float32).reshape(NFC, 128).T
    m["b2cols"] = b2
    qk = np.zeros((128, 4), np.float32)
    qk[:, 0] = np.tile(np.asarray(inputs["L0_ffn_q_scale"][0], np.float32), 2)
    qk[:, 1] = np.tile(np.asarray(inputs["L0_ffn_k_scale"][0], np.float32), 2)
    qk[:, 2] = np.tile(np.asarray(inputs["L0_ffn_q_scale"][1], np.float32), 2)
    qk[:, 3] = np.tile(np.asarray(inputs["L0_ffn_k_scale"][1], np.float32), 2)
    m["qkscale"] = qk
    tg = np.broadcast_to(np.arange(2048 * h, 2048 * h + T, dtype=np.float32), (128, T))
    m["tgrid"] = np.ascontiguousarray(tg)
    ic = ((np.arange(128) % 64) % 32).astype(np.float32)[:, None]
    m["icol"] = np.ascontiguousarray(ic)
    sg = np.where((np.arange(128) % 64) < 32, -1.0, 1.0).astype(np.float32)[:, None]
    m["sgncol"] = np.ascontiguousarray(sg)
    m["gatecol"] = np.full((128, 1), float(h), np.float32)
    jj = np.arange(128)[:, None]
    tt = np.arange(128)[None, :]
    import ml_dtypes
    bfd = ml_dtypes.bfloat16
    d0 = np.where(jj - tt >= 0, 0.0, NEG).astype(bfd)
    d4 = np.where(jj - tt <= 0, 0.0, NEG).astype(bfd)
    m["mask_d0"] = d0
    m["mask_d4"] = d4
    m["mask_w0d0"] = d0 if h == 1 else np.full((128, 128), NEG, bfd)
    m["mask_w0fill"] = (np.zeros((128, 128)) if h == 1 else np.full((128, 128), NEG)).astype(bfd)
    return m


def kernel(**inputs):
    prog = Prog(stage=_STAGE)
    nc = prog.build()
    in_maps = [_host_inputs(inputs, c) for c in range(8)]
    res = run_bass_kernel_spmd(nc, in_maps, core_ids=list(range(8)))
    out = np.zeros((4, 4096, 1024), dtype=np.float32)
    for c in range(8):
        b, h = c // 2, c % 2
        out[b, 2048 * h:2048 * h + 2048] = np.asarray(res.results[c]["out"]).T
    return out


# version 18 (final)
# speedup vs baseline: 1.0124x; 1.0124x over previous
import os as _os
import numpy as np
import concourse.bass as bass
import concourse.mybir as mybir
from concourse.bass_utils import run_bass_kernel_spmd
from concourse.ap import AP
from contextlib import ExitStack

F32 = mybir.dt.float32
BF16 = mybir.dt.bfloat16
I32 = mybir.dt.int32
AF = mybir.ActivationFunctionType
ALU = mybir.AluOpType
AX = mybir.AxisListType

T = 2048
TW = T + 1
NFC = 8
NEG = -1.0e30

_STAGE = None


class Sem:
    __slots__ = ("h", "n")

    def __init__(self, nc, name):
        self.h = nc.alloc_semaphore(name)
        self.n = 0


class _Call:
    __slots__ = ("name", "a", "k", "incs")

    def __init__(self, name, a, k):
        self.name, self.a, self.k = name, a, k
        self.incs = []

    def then_inc(self, sem, val=1):
        self.incs.append((sem, val))
        return self


class _Rec:

    def __init__(self):
        self.calls = []

    def __getattr__(self, name):
        def call(*a, **k):
            c = _Call(name, a, k)
            self.calls.append(c)
            return c
        return call

    def replay(self, eng):
        for c in self.calls:
            instr = getattr(eng, c.name)(*c.a, **c.k)
            for sem, val in c.incs:
                instr.then_inc(sem, val)


class Phase:
    ENG = ("tensor", "vector", "scalar", "sync", "gpsimd")

    def __init__(self, prog):
        self.prog = prog
        self.rec = {e: _Rec() for e in self.ENG}

    def t(self, f): f(self.rec["tensor"])
    def v(self, f): f(self.rec["vector"])
    def s(self, f): f(self.rec["scalar"])
    def y(self, f): f(self.rec["sync"])
    def g(self, f): f(self.rec["gpsimd"])

    def emit(self):
        with self.prog.nc.Block() as blk:
            for name in self.ENG:
                rec = self.rec[name]
                if rec.calls:
                    getattr(blk, name)(rec.replay)


class Prog:
    def hw_batch(self):
        s = self.hwpool[self._hwp % 8]
        self._hwp += 1
        return s

    def sw_batch(self):
        s = self.swpool[self._swp % 8]
        self._swp += 1
        return s

    def __init__(self, stage=None):
        self.stage = stage
        nc = bass.Bass("TRN2", target_bir_lowering=False)
        self.nc = nc
        self.st = ExitStack()
        d = {}

        def ap(n, s, dt=F32, k="ExternalInput"):
            d[n] = nc.dram_tensor(n, s, dt, kind=k).ap()
            return d[n]
        ap("x", [1024, T])
        ap("wqT", [1024, 1024]); ap("wkT", [1024, 1024])
        ap("wvT", [1024, 1024]); ap("woT", [1024, 1024])
        for l in range(2):
            ap(f"wqkv{l}", [1024, 1536]); ap(f"wo{l}", [512, 1024])
            ap(f"w1{l}", [1024, 5460]); ap(f"w2{l}", [2730, 1024])
        ap("lgcols", [128, 2 * NFC])
        ap("lbcols", [128, 2 * NFC])
        ap("b1cols", [128, 2 * 44])
        ap("b2cols", [128, 2 * NFC])
        ap("qkscale", [128, 4])
        ap("tgrid", [128, T])
        ap("icol", [128, 1]); ap("sgncol", [128, 1]); ap("gatecol", [128, 1])
        ap("mask_d0", [128, 128], BF16); ap("mask_d4", [128, 128], BF16)
        ap("mask_w0d0", [128, 128], BF16); ap("mask_w0fill", [128, 128], BF16)
        ap("out", [1024, T], F32, "ExternalOutput")
        self.d = d
        self.xo_spill = nc.dram_tensor("xo_spill", [1024, T], F32, kind="Internal")
        self.ex = {}
        for name, shape, dt in [
            ("x0col", [128, 8], F32),
            ("kv0", [512, 1032], BF16), ("nrm0", [1, 4], F32), ("xm0", [128, 8], F32),
            ("x1col", [128, 8], F32),
            ("kv1", [512, 1032], BF16), ("nrm1", [1, 4], F32), ("xm1", [128, 8], F32),
        ]:
            src = nc.dram_tensor(f"exs_{name}", shape, dt, kind="Internal")
            dst = nc.dram_tensor(f"exd_{name}", [2] + shape, dt, kind="Internal")
            self.ex[name] = (src, dst)
        self.pe = Sem(nc, "pe")
        self.act = Sem(nc, "act")
        self.dve = Sem(nc, "dve")
        self.gp = Sem(nc, "gp")
        self.dX = Sem(nc, "dX")
        self.dW = Sem(nc, "dW")
        self.cc = Sem(nc, "cc")
        self.sout = Sem(nc, "sout")
        self._ecc = {}
        self._stop = False
        self.hwpool = [Sem(nc, f"dh{i}") for i in range(8)]
        self.swpool = [Sem(nc, f"ds{i}") for i in range(8)]
        self._hwp = 0
        self._swp = 0

        sb = lambda n, s, dt: self.st.enter_context(nc.sbuf_tensor(n, s, dt))
        self.XA = [sb(f"XA{i}", [128, TW], F32) for i in range(NFC)]
        self.XS = [sb(f"XS{i}", [128, TW], BF16) for i in range(NFC)]
        self.ONESB = sb("ONESB", [128, 1], BF16)
        self.ONESH = sb("ONESH", [128, 2], BF16)
        self.ONESF = sb("ONESF", [1, 128], F32)
        self.STAT = sb("STAT", [1, 1024], F32)
        self.SROW = sb("SROW", [1, 1024], F32)
        self.BCOL = sb("BCOL", [128, 8], F32)
        self.MSC = sb("MSC", [1, 64], F32)
        self.MSC2 = sb("MSC2", [1, 40], F32)
        self.HS = sb("HS", [128, 8], F32)
        self.MD0 = sb("MD0", [128, 128], BF16)
        self.MD4 = sb("MD4", [128, 128], BF16)
        self.MW0D0 = sb("MW0D0", [128, 128], BF16)
        self.MW0F = sb("MW0F", [128, 128], BF16)
        self.ICOL = sb("ICOL", [128, 1], F32)
        self.SGN = sb("SGN", [128, 1], F32)
        self.GATE = sb("GATE", [128, 1], F32)
        self.QKS = sb("QKS", [128, 4], F32)
        self.LG = sb("LG", [128, 2 * NFC], F32)
        self.LB = sb("LB", [128, 2 * NFC], F32)
        self.B1C = sb("B1C", [128, 2 * 44], F32)
        self.B2C = sb("B2C", [128, 2 * NFC], F32)

    def build(self):
        self.ph_init()
        if self.stage == "init":
            self.ph_emit_stream(self.XA)
            return self.nc
        self.ph_outer()
        if self.stage == "x_outer":
            self.ph_emit_stream(self.XA)
            return self.nc
        self.ph_x0()
        if self.stage == "x0":
            self.ph_emit_stream(self.XA)
            return self.nc
        self.ph_tables()
        self.ph_ffn()
        if self._stop:
            return self.nc
        self.ph_final()
        return self.nc

    def ph_init(self):
        d, S = self.d, self
        p = Phase(self)

        bs = self.hw_batch()

        def y(sync):
            for i in range(NFC):
                sync.dma_start(out=self.XA[i][:, 1:TW], in_=d["x"][128 * i:128 * (i + 1), :]).then_inc(bs.h, 16)
            small = [("icol", self.ICOL), ("sgncol", self.SGN),
                     ("gatecol", self.GATE), ("qkscale", self.QKS), ("lgcols", self.LG),
                     ("lbcols", self.LB), ("b1cols", self.B1C), ("b2cols", self.B2C),
                     ("mask_d0", self.MD0), ("mask_d4", self.MD4),
                     ("mask_w0d0", self.MW0D0), ("mask_w0fill", self.MW0F)]
            for name, dst in small:
                sync.dma_start(out=dst[:], in_=d[name]).then_inc(bs.h, 16)
            bs.n += 16 * (NFC + len(small))
        p.y(y)
        nbs = bs.n

        def v(vector):
            vector.wait_ge(bs.h, nbs)
            vector.memset(self.ONESB[:], 1.0)
            vector.memset(self.ONESF[:], 1.0)
            vector.memset(self.ONESH[0:64, 0:1], 1.0)
            vector.memset(self.ONESH[64:128, 0:1], 0.0)
            vector.memset(self.ONESH[0:64, 1:2], 0.0)
            vector.memset(self.ONESH[64:128, 1:2], 1.0)
            for i in range(NFC):
                vector.memset(self.XA[i][:, 0:1], 0.0)
            vector.drain()
            for i in range(NFC):
                vector.tensor_copy(out=self.XS[i][:], in_=self.XA[i][:])
        p.v(v)
        p.emit()

    def ph_tables(self):
        nc, S = self.nc, self
        f32 = np.float32
        sp = lambda x: (float(f32(x)), float(f32(x - np.float64(f32(x)))))
        LN4_HI, LN4_LO = sp(np.log(1e4))
        PI_HI, PI_LO = sp(np.pi)
        HPI_HI, HPI_LO = sp(np.pi / 2)
        A_HI, A_LO = sp(25.6)
        B_HI, B_LO = sp(89.6)

        sb = lambda n, s, dt: self.st.enter_context(nc.sbuf_tensor(n, s, dt))
        self.TCQ = sb("TCQ", [128, T], BF16)
        self.TSQ = sb("TSQ", [128, T], BF16)
        self.TCK = sb("TCK", [128, T], BF16)
        self.TSK = sb("TSK", [128, T], BF16)
        st = ExitStack()
        sc = st.enter_context(nc.sbuf_tensor("tb_sc", [128, 8], F32))
        TGRID = st.enter_context(nc.sbuf_tensor("TGRID", [128, T], F32))
        self.TGRID = TGRID
        fr = st.enter_context(nc.sbuf_tensor("tb_fr", [128, T], F32))
        t1 = st.enter_context(nc.sbuf_tensor("tb_t1", [128, T], F32))
        t2 = st.enter_context(nc.sbuf_tensor("tb_t2", [128, T], F32))
        t3 = st.enter_context(nc.sbuf_tensor("tb_t3", [128, T], F32))
        ki = st.enter_context(nc.sbuf_tensor("tb_ki", [128, T], I32))
        sinb = st.enter_context(nc.sbuf_tensor("tb_sin", [128, T], F32))
        cosb = st.enter_context(nc.sbuf_tensor("tb_cos", [128, T], F32))
        scl = st.enter_context(nc.sbuf_tensor("tb_scl", [128, T], F32))
        scli = st.enter_context(nc.sbuf_tensor("tb_scli", [128, T], F32))
        p = Phase(self)

        bs0 = self.hw_batch()

        def y0(sync):
            sync.dma_start(out=TGRID[:], in_=self.d["tgrid"]).then_inc(bs0.h, 16)
            bs0.n += 16
        p.y(y0)
        ndx = bs0.n

        def v0(v):
            v.wait_ge(bs0.h, ndx)
        p.v(v0)

        def v1(v):
            v.tensor_scalar(out=sc[:, 0:1], in0=self.ICOL[:], scalar1=1.0 / 32.0, scalar2=-LN4_HI, op0=ALU.mult, op1=ALU.mult)
            v.tensor_scalar(out=sc[:, 1:2], in0=self.ICOL[:], scalar1=1.0 / 32.0, scalar2=-LN4_LO, op0=ALU.mult, op1=ALU.mult)
            v.drain()
            v.tensor_tensor(out=sc[:, 0:1], in0=sc[:, 0:1], in1=sc[:, 1:2], op=ALU.add)
            v.tensor_scalar(out=sc[:, 2:3], in0=self.ICOL[:], scalar1=2.0, scalar2=A_HI, op0=ALU.mult, op1=ALU.add)
            v.memset(sc[:, 3:4], B_HI)
            v.drain()
            v.tensor_scalar(out=sc[:, 2:3], in0=sc[:, 2:3], scalar1=A_LO, scalar2=None, op0=ALU.add)
            v.tensor_scalar(out=sc[:, 3:4], in0=sc[:, 3:4], scalar1=B_LO, scalar2=None, op0=ALU.add)
            v.drain()
            v.reciprocal(out=sc[:, 3:4], in_=sc[:, 3:4])
            v.drain()
            v.tensor_tensor(out=sc[:, 4:5], in0=sc[:, 2:3], in1=sc[:, 3:4], op=ALU.mult).then_inc(S.dve.h, 1)
            S.dve.n += 1
        p.v(v1)
        n1 = S.dve.n

        def s1(s):
            s.wait_ge(S.dve.h, n1)
            s.activation(out=sc[:, 0:1], in_=sc[:, 0:1], func=AF.Exp)
            s.activation(out=sc[:, 1:2], in_=sc[:, 4:5], func=AF.Ln)
            s.drain()
            s.sem_inc(S.act.h, 1)
            S.act.n += 1
        p.s(s1)
        n2 = S.act.n

        def v2(v):
            v.wait_ge(S.act.h, n2)
            v.tensor_scalar(out=fr[:], in0=self.TGRID[:], scalar1=sc[:, 0:1], scalar2=None, op0=ALU.mult)
            v.tensor_scalar(out=scl[:], in0=self.TGRID[:], scalar1=-2048.0, scalar2=1.0 / 256.0, op0=ALU.add, op1=ALU.mult)
            v.drain()
            v.tensor_scalar(out=scl[:], in0=scl[:], scalar1=sc[:, 1:2], scalar2=None, op0=ALU.mult)
            v.drain()
            v.tensor_scalar(out=scli[:], in0=scl[:], scalar1=-1.0, scalar2=None, op0=ALU.mult).then_inc(S.dve.h, 1)
            S.dve.n += 1
        p.v(v2)
        n3 = S.dve.n

        def s2(s):
            s.wait_ge(S.dve.h, n3)
            s.activation(out=scl[:], in_=scl[:], func=AF.Exp)
            s.activation(out=scli[:], in_=scli[:], func=AF.Exp)
            s.drain()
            s.sem_inc(S.act.h, 1)
            S.act.n += 1
        p.s(s2)
        n4 = S.act.n

        def emit_sin(out_buf, sh_hi, sh_lo, first):
            HPI = float(np.float32(np.pi / 2))

            def vv(v):
                if first:
                    v.wait_ge(S.act.h, n4)
                if sh_hi == 0.0:
                    v.tensor_copy(out=t1[:], in_=fr[:])
                    v.drain()
                else:
                    v.tensor_scalar(out=t1[:], in0=fr[:], scalar1=sh_hi, scalar2=None, op0=ALU.add)
                    v.drain()
                    v.tensor_scalar(out=t1[:], in0=t1[:], scalar1=sh_lo, scalar2=None, op0=ALU.add)
                    v.drain()
                v.tensor_scalar(out=t2[:], in0=t1[:], scalar1=1.0 / (2.0 * np.pi), scalar2=-0.5, op0=ALU.mult, op1=ALU.add)
                v.drain()
                v.tensor_copy(out=ki[:], in_=t2[:])
                v.drain()
                v.tensor_copy(out=t2[:], in_=ki[:])
                v.drain()
                v.tensor_scalar(out=t3[:], in0=t2[:], scalar1=2.0 * PI_HI, scalar2=None, op0=ALU.mult)
                v.drain()
                v.tensor_tensor(out=t1[:], in0=t1[:], in1=t3[:], op=ALU.subtract)
                v.drain()
                v.tensor_scalar(out=t3[:], in0=t2[:], scalar1=2.0 * PI_LO, scalar2=None, op0=ALU.mult)
                v.drain()
                v.tensor_tensor(out=t1[:], in0=t1[:], in1=t3[:], op=ALU.subtract)
                v.drain()
                v.tensor_scalar(out=t1[:], in0=t1[:], scalar1=-PI_HI, scalar2=None, op0=ALU.add)
                v.drain()
                v.tensor_scalar(out=t1[:], in0=t1[:], scalar1=-PI_LO, scalar2=None, op0=ALU.add)
                v.drain()
                v.tensor_scalar(out=t2[:], in0=t1[:], scalar1=-PI_HI, scalar2=None, op0=ALU.is_lt)
                v.drain()
                v.scalar_tensor_tensor(out=t1[:], in0=t2[:], scalar=2.0 * PI_HI, in1=t1[:], op0=ALU.mult, op1=ALU.add)
                v.drain()
                v.tensor_scalar(out=t2[:], in0=t1[:], scalar1=PI_HI, scalar2=None, op0=ALU.is_gt)
                v.drain()
                v.scalar_tensor_tensor(out=t1[:], in0=t2[:], scalar=-2.0 * PI_HI, in1=t1[:], op0=ALU.mult, op1=ALU.add)
                v.drain()
                v.tensor_scalar(out=t2[:], in0=t1[:], scalar1=HPI, scalar2=None, op0=ALU.is_gt)
                v.tensor_scalar(out=t3[:], in0=t1[:], scalar1=-2.0, scalar2=PI_HI, op0=ALU.mult, op1=ALU.add)
                v.drain()
                v.tensor_tensor(out=t2[:], in0=t2[:], in1=t3[:], op=ALU.mult)
                v.drain()
                v.tensor_tensor(out=t1[:], in0=t1[:], in1=t2[:], op=ALU.add)
                v.drain()
                v.tensor_scalar(out=t2[:], in0=t1[:], scalar1=-HPI, scalar2=None, op0=ALU.is_lt)
                v.tensor_scalar(out=t3[:], in0=t1[:], scalar1=-2.0, scalar2=-PI_HI, op0=ALU.mult, op1=ALU.add)
                v.drain()
                v.tensor_tensor(out=t2[:], in0=t2[:], in1=t3[:], op=ALU.mult)
                v.drain()
                v.tensor_tensor(out=t1[:], in0=t1[:], in1=t2[:], op=ALU.add).then_inc(S.dve.h, 1)
                S.dve.n += 1
                v.drain()
            p.v(vv)
            nred = S.dve.n

            def ss(s):
                s.wait_ge(S.dve.h, nred)
                s.activation(out=out_buf[:], in_=t1[:], func=AF.Sin)
                s.drain()
                s.sem_inc(S.act.h, 1)
                S.act.n += 1
            p.s(ss)
            nsin = S.act.n

            def v3(v):
                v.wait_ge(S.act.h, nsin)
                v.tensor_scalar(out=out_buf[:], in0=out_buf[:], scalar1=-1.0, scalar2=None, op0=ALU.mult)
                v.drain()
            p.v(v3)

        emit_sin(sinb, 0.0, 0.0, True)
        emit_sin(cosb, HPI_HI, HPI_LO, False)

        def v4(v):
            v.tensor_tensor(out=self.TCQ[:], in0=cosb[:], in1=scl[:], op=ALU.mult)
            v.tensor_tensor(out=self.TCK[:], in0=cosb[:], in1=scli[:], op=ALU.mult)
            v.tensor_scalar(out=t1[:], in0=sinb[:], scalar1=self.SGN[:, 0:1], scalar2=None, op0=ALU.mult)
            v.drain()
            v.tensor_tensor(out=self.TSQ[:], in0=t1[:], in1=scl[:], op=ALU.mult)
            v.tensor_tensor(out=self.TSK[:], in0=t1[:], in1=scli[:], op=ALU.mult)
            v.drain()
        p.v(v4)
        p.emit()
        st.close()

    def ln_block(self, p, psums, src_f32, src_bf, cols, out_tiles, out_cols=None,
                 sq_scr=None, lg_off=None, lb_off=None, res_tiles=None, res_cols=None,
                 out_bf=None, shift=False, act_gate=None, out_gate=None, pe_gate=None):
        S = self
        n = cols.stop - cols.start
        oc = out_cols if out_cols is not None else cols
        rc = res_cols if res_cols is not None else cols

        def ci(i):
            if shift and i >= 4:
                return slice(cols.start - 1, cols.stop - 1)
            return cols
        psA, psB, psR, psM, scr = psums["A"], psums["B"], psums["R"], psums["M"], psums["scr"]
        mm_done = []
        for i in range(NFC):
            def vsq(v, i=i):
                if i >= 2:
                    v.wait_ge(S.pe.h, mm_done[i - 2])
                elif pe_gate is not None:
                    v.wait_ge(S.pe.h, pe_gate)
                v.tensor_tensor(out=sq_scr[:, 512 * (i % 2):512 * (i % 2) + n],
                                in0=src_bf[i][:, ci(i)], in1=src_bf[i][:, ci(i)], op=ALU.mult).then_inc(S.dve.h, 1)
                S.dve.n += 1
            p.v(vsq)
            nv = S.dve.n

            def tmm(t, i=i, nv=nv):
                t.wait_ge(S.dve.h, nv)
                t.matmul(psA[0:1, 0:n], self.ONESB[:, 0:1], src_bf[i][:, ci(i)], start=(i == 0), stop=(i == NFC - 1))
                mm = t.matmul(psB[0:1, 0:n], self.ONESB[:, 0:1], sq_scr[:, 512 * (i % 2):512 * (i % 2) + n], start=(i == 0), stop=(i == NFC - 1))
                mm.then_inc(S.pe.h, 1)
                S.pe.n += 1
            p.t(tmm)
            mm_done.append(S.pe.n)
        n_stats = S.pe.n

        def vstat(v):
            v.wait_ge(S.pe.h, n_stats)
            if act_gate is not None:
                v.wait_ge(S.act.h, act_gate)
            v.tensor_scalar(out=self.SROW[0:1, 0:n], in0=psA[0:1, 0:n], scalar1=1.0 / 1024.0, scalar2=None, op0=ALU.mult)
            v.tensor_scalar(out=self.SROW[0:1, 512:512 + n], in0=psB[0:1, 0:n], scalar1=1.0 / 1024.0, scalar2=None, op0=ALU.mult)
            v.drain()
            v.tensor_tensor(out=self.STAT[0:1, 0:n], in0=self.SROW[0:1, 0:n], in1=self.SROW[0:1, 0:n], op=ALU.mult)
            v.drain()
            v.tensor_tensor(out=self.STAT[0:1, 0:n], in0=self.SROW[0:1, 512:512 + n], in1=self.STAT[0:1, 0:n], op=ALU.subtract)
            v.drain()
            v.tensor_scalar(out=self.STAT[0:1, 0:n], in0=self.STAT[0:1, 0:n], scalar1=1e-5, scalar2=None, op0=ALU.add).then_inc(S.dve.h, 1)
            S.dve.n += 1
        p.v(vstat)
        nv2 = S.dve.n

        def svar(s):
            s.wait_ge(S.dve.h, nv2)
            s.activation(out=self.STAT[0:1, 0:n], in_=self.STAT[0:1, 0:n], func=AF.Ln)
            s.drain()
            s.activation(out=self.STAT[0:1, 0:n], in_=self.STAT[0:1, 0:n], func=AF.Exp, scale=-0.5)
            s.drain()
            s.sem_inc(S.act.h, 1)
            S.act.n += 1
        p.s(svar)
        na = S.act.n

        def vrec(v):
            v.wait_ge(S.act.h, na)
            v.tensor_tensor(out=self.STAT[0:1, 512:512 + n], in0=self.SROW[0:1, 0:n], in1=self.STAT[0:1, 0:n], op=ALU.mult)
            v.drain()
            v.tensor_scalar(out=self.STAT[0:1, 512:512 + n], in0=self.STAT[0:1, 512:512 + n], scalar1=-1.0, scalar2=None, op0=ALU.mult).then_inc(S.dve.h, 1)
            S.dve.n += 1
        p.v(vrec)
        nv3 = S.dve.n

        def tbc(t):
            t.wait_ge(S.dve.h, nv3)
            t.matmul(psR[:, 0:n], self.ONESF[0:1, :], self.STAT[0:1, 0:n], start=True, stop=True)
            t.matmul(psM[:, 0:n], self.ONESF[0:1, :], self.STAT[0:1, 512:512 + n], start=True, stop=True).then_inc(S.pe.h, 1)
            S.pe.n += 1
        p.t(tbc)
        n_bc = S.pe.n

        for i in range(NFC):
            def vno(v, i=i):
                if i == 0:
                    v.wait_ge(S.pe.h, n_bc)
                    if out_gate is not None:
                        v.wait_ge(out_gate[0], out_gate[1])
                v.tensor_tensor(out=scr[:, 0:n], in0=src_f32[i][:, ci(i)], in1=psR[:, 0:n], op=ALU.mult)
                v.drain()
                if lg_off is not None:
                    v.tensor_tensor(out=scr[:, 0:n], in0=scr[:, 0:n], in1=psM[:, 0:n], op=ALU.add)
                    v.drain()
                    v.tensor_scalar(out=scr[:, 0:n], in0=scr[:, 0:n], scalar1=self.LG[:, lg_off + i:lg_off + i + 1], scalar2=None, op0=ALU.mult)
                    v.drain()
                    v.tensor_scalar(out=out_tiles[i][:, oc], in0=scr[:, 0:n], scalar1=self.LB[:, lb_off + i:lb_off + i + 1], scalar2=None, op0=ALU.add)
                elif res_tiles is not None:
                    v.tensor_tensor(out=scr[:, 0:n], in0=scr[:, 0:n], in1=psM[:, 0:n], op=ALU.add)
                    v.drain()
                    v.tensor_tensor(out=out_tiles[i][:, oc], in0=scr[:, 0:n], in1=res_tiles[i][:, rc], op=ALU.add)
                else:
                    v.tensor_tensor(out=out_tiles[i][:, oc], in0=scr[:, 0:n], in1=psM[:, 0:n], op=ALU.add)
                v.drain()
                if out_bf is not None:
                    v.tensor_copy(out=out_bf[i][:, oc], in_=out_tiles[i][:, oc])
                    v.drain()
                if i == NFC - 1:
                    v.sem_inc(S.dve.h, 1)
                    S.dve.n += 1
            p.v(vno)
        return S.dve.n

    def ln_ring(self, p, psums, ring_ev, src_f32, src_bf, cols, out_tiles,
                out_cols=None, sq_scr=None, res_tiles=None, res_cols=None):
        S = self
        n = cols.stop - cols.start
        oc = out_cols if out_cols is not None else cols
        rc = res_cols if res_cols is not None else cols
        psA, psB, psR, psM, scr = psums["A"], psums["B"], psums["R"], psums["M"], psums["scr"]
        claim = max(ring_ev[0], ring_ev[1])
        n0 = S.pe.n
        mm_done = []
        for i in range(NFC):
            def vsq(v, i=i):
                if i >= 2:
                    v.wait_ge(S.pe.h, mm_done[i - 2])
                else:
                    v.wait_ge(S.pe.h, n0)
                v.tensor_tensor(out=sq_scr[:, 256 * (i % 2):256 * (i % 2) + n],
                                in0=src_bf[i][:, cols], in1=src_bf[i][:, cols], op=ALU.mult).then_inc(S.dve.h, 1)
                S.dve.n += 1
            p.v(vsq)
            nv = S.dve.n

            def tmm(t, i=i, nv=nv):
                t.wait_ge(S.dve.h, max(nv, claim if i == 0 else 0))
                t.matmul(psA[0:1, 0:n], self.ONESB[:, 0:1], src_bf[i][:, cols], start=(i == 0), stop=(i == NFC - 1))
                mm = t.matmul(psB[0:1, 0:n], self.ONESB[:, 0:1], sq_scr[:, 256 * (i % 2):256 * (i % 2) + n], start=(i == 0), stop=(i == NFC - 1))
                mm.then_inc(S.pe.h, 1)
                S.pe.n += 1
            p.t(tmm)
            mm_done.append(S.pe.n)
        n_stats = S.pe.n

        def vstat(v):
            v.wait_ge(S.pe.h, n_stats)
            v.tensor_scalar(out=self.SROW[0:1, 0:n], in0=psA[0:1, 0:n], scalar1=1.0 / 1024.0, scalar2=None, op0=ALU.mult)
            v.tensor_scalar(out=self.SROW[0:1, 512:512 + n], in0=psB[0:1, 0:n], scalar1=1.0 / 1024.0, scalar2=None, op0=ALU.mult)
            v.drain()
            v.tensor_tensor(out=self.STAT[0:1, 512:512 + n], in0=self.SROW[0:1, 0:n], in1=self.SROW[0:1, 0:n], op=ALU.mult)
            v.drain()
            v.tensor_tensor(out=self.STAT[0:1, 512:512 + n], in0=self.SROW[0:1, 512:512 + n], in1=self.STAT[0:1, 512:512 + n], op=ALU.subtract)
            v.drain()
            v.tensor_scalar(out=self.STAT[0:1, 512:512 + n], in0=self.STAT[0:1, 512:512 + n], scalar1=1e-5, scalar2=None, op0=ALU.add).then_inc(S.dve.h, 1)
            S.dve.n += 1
        p.v(vstat)
        nv2 = S.dve.n

        def svar(s):
            s.wait_ge(S.dve.h, nv2)
            s.activation(out=self.STAT[0:1, 512:512 + n], in_=self.STAT[0:1, 512:512 + n], func=AF.Ln)
            s.drain()
            s.activation(out=self.STAT[0:1, 512:512 + n], in_=self.STAT[0:1, 512:512 + n], func=AF.Exp, scale=-0.5)
            s.drain()
            s.sem_inc(S.act.h, 1)
            S.act.n += 1
        p.s(svar)
        na = S.act.n

        def vrec(v):
            v.wait_ge(S.act.h, na)
            v.tensor_tensor(out=self.STAT[0:1, 768:768 + n], in0=self.SROW[0:1, 0:n], in1=self.STAT[0:1, 512:512 + n], op=ALU.mult)
            v.drain()
            v.tensor_scalar(out=self.STAT[0:1, 768:768 + n], in0=self.STAT[0:1, 768:768 + n], scalar1=-1.0, scalar2=None, op0=ALU.mult).then_inc(S.dve.h, 1)
            S.dve.n += 1
        p.v(vrec)
        nv3 = S.dve.n

        def tbc(t):
            t.wait_ge(S.dve.h, nv3)
            t.matmul(psR[:, 0:n], self.ONESF[0:1, :], self.STAT[0:1, 512:512 + n], start=True, stop=True)
            t.matmul(psM[:, 0:n], self.ONESF[0:1, :], self.STAT[0:1, 768:768 + n], start=True, stop=True).then_inc(S.pe.h, 1)
            S.pe.n += 1
        p.t(tbc)
        n_bc = S.pe.n

        for i in range(NFC):
            def vno(v, i=i):
                if i == 0:
                    v.wait_ge(S.pe.h, n_bc)
                v.tensor_tensor(out=scr[:, 0:n], in0=src_f32[i][:, cols], in1=psR[:, 0:n], op=ALU.mult)
                v.drain()
                if res_tiles is not None:
                    v.tensor_tensor(out=scr[:, 0:n], in0=scr[:, 0:n], in1=psM[:, 0:n], op=ALU.add)
                    v.drain()
                    v.tensor_tensor(out=out_tiles[i][:, oc], in0=scr[:, 0:n], in1=res_tiles[i][:, rc], op=ALU.add)
                else:
                    v.tensor_tensor(out=out_tiles[i][:, oc], in0=scr[:, 0:n], in1=psM[:, 0:n], op=ALU.add)
                v.drain()
                v.sem_inc(S.dve.h, 1)
                S.dve.n += 1
            p.v(vno)
        ring_ev[0] = ring_ev[1] = S.dve.n
        return S.dve.n

    def ph_outer(self):
        nc, S, d = self.nc, self, self.d
        st = ExitStack()
        WQ = [st.enter_context(nc.sbuf_tensor(f"WQ{i}", [128, 1024], BF16)) for i in range(NFC)]
        WK = [st.enter_context(nc.sbuf_tensor(f"WK{i}", [128, 1024], BF16)) for i in range(NFC)]
        WV = [st.enter_context(nc.sbuf_tensor(f"WV{i}", [128, 1024], BF16)) for i in range(NFC)]
        WO = [st.enter_context(nc.sbuf_tensor(f"WO{i}", [128, 1024], BF16)) for i in range(NFC)]
        LN0 = [st.enter_context(nc.sbuf_tensor(f"LN0_{i}", [128, 256], BF16)) for i in range(NFC)]
        QH = [st.enter_context(nc.sbuf_tensor(f"QH{i}", [128, 256], BF16)) for i in range(NFC)]
        KH = [st.enter_context(nc.sbuf_tensor(f"KH{i}", [128, 256], BF16)) for i in range(NFC)]
        XO = [st.enter_context(nc.sbuf_tensor(f"XOt{i}", [128, 256], BF16)) for i in range(NFC)]
        VT65 = [st.enter_context(nc.sbuf_tensor(f"VT65_{i}", [128, 1040], BF16)) for i in range(2)]
        QD2 = [st.enter_context(nc.sbuf_tensor(f"QD2_{i}", [128, 512], BF16)) for i in range(NFC)]
        PT = [st.enter_context(nc.sbuf_tensor(f"PT{i}", [128, 1024], BF16)) for i in range(2)]
        SQ = st.enter_context(nc.sbuf_tensor("SQ", [128, 512], BF16))
        BCO2 = st.enter_context(nc.sbuf_tensor("BCO2", [128, 8], F32))
        SCR = st.enter_context(nc.sbuf_tensor("SCR", [128, 512], F32))
        RB = SCR[0:64, :]
        WOUT = [QD2[hc][:, 0:256] for hc in range(NFC)]

        p0 = Phase(self)
        bw = self.sw_batch()

        def g(gp):
            for i in range(NFC):
                gp.dma_start(out=WQ[i][:], in_=d["wqT"][128 * i:128 * (i + 1), :]).then_inc(bw.h, 16)
                gp.dma_start(out=WK[i][:], in_=d["wkT"][128 * i:128 * (i + 1), :]).then_inc(bw.h, 16)
                gp.dma_start(out=WV[i][:], in_=d["wvT"][128 * i:128 * (i + 1), :]).then_inc(bw.h, 16)
                gp.dma_start(out=WO[i][:], in_=d["woT"][128 * i:128 * (i + 1), :]).then_inc(bw.h, 16)
            bw.n += 16 * 4 * NFC
        p0.g(g)
        p0.emit()
        w_ready = (bw, bw.n)

        p = Phase(self)
        pst = ExitStack()
        psS = [pst.enter_context(nc.psum_tensor(f"opsS{i}", [128, 1024], F32)) for i in range(2)]
        psAV = [pst.enter_context(nc.psum_tensor(f"opsAV{i}", [65, 512], F32)) for i in range(2)]
        psR = pst.enter_context(nc.psum_tensor("opsR", [128, 512], F32))
        psM = pst.enter_context(nc.psum_tensor("opsM", [128, 512], F32))
        psums = {"A": psR[0:1, 0:512], "B": psM[0:1, 0:512], "R": psR, "M": psM, "scr": SCR}
        ring_ev = [0, 0]
        self._rslot = 1
        self._bspo = self.hw_batch()

        def twait(t):
            t.wait_ge(w_ready[0].h, w_ready[1])
        p.t(twait)

        def ln_chunk(c):
            cols = slice(1 + 256 * c, 1 + 256 * c + 256)
            self.ln_ring(p, psums, ring_ev, self.XA, self.XS, cols, LN0,
                         out_cols=slice(0, 256), sq_scr=SQ)

        def proj_group(c, kind, j):
            W, DST = (WQ, QH) if kind == "q" else (WK, KH)
            slot = self.ring_slot()

            def tq(t, W=W, j=j, slot=slot, claim=ring_ev[slot]):
                t.wait_ge(S.dve.h, claim)
                dst = psR if slot == 0 else psM
                for i in range(NFC):
                    mm = t.matmul(dst[:, 0:256], W[i][:, 128 * j:128 * (j + 1)], LN0[i][:], start=(i == 0), stop=(i == NFC - 1))
                mm.then_inc(S.pe.h, 1)
                S.pe.n += 1
            p.t(tq)
            npe = S.pe.n

            def vq(v, DST=DST, j=j, npe=npe, slot=slot):
                v.wait_ge(S.pe.h, npe)
                src = psR if slot == 0 else psM
                v.tensor_copy(out=DST[j][:], in_=src[:, 0:256]).then_inc(S.dve.h, 1)
                S.dve.n += 1
                ring_ev[slot] = S.dve.n
            p.v(vq)

        def vproj_group(c, sh, ob):
            slot = self.ring_slot()

            def tv(t, sh=sh, ob=ob, slot=slot, claim=ring_ev[slot]):
                t.wait_ge(S.dve.h, claim)
                dst = psR if slot == 0 else psM
                for i in range(NFC):
                    mm = t.matmul(dst[:, 0:512], LN0[i][:, 128 * sh:128 * (sh + 1)],
                                  WV[i][:, 512 * ob:512 * (ob + 1)], start=(i == 0), stop=(i == NFC - 1))
                mm.then_inc(S.pe.h, 1)
                S.pe.n += 1
            p.t(tv)
            npe = S.pe.n

            def vv(v, sh=sh, ob=ob, npe=npe, slot=slot):
                v.wait_ge(S.pe.h, npe)
                src = psR if slot == 0 else psM
                for g2 in range(8):
                    gg = ob * 8 + g2
                    v.tensor_copy(out=VT65[sh][:, 65 * gg:65 * gg + 64], in_=src[:, 64 * g2:64 * (g2 + 1)])
                v.drain()
                v.sem_inc(S.dve.h, 1)
                S.dve.n += 1
                ring_ev[slot] = S.dve.n
            p.v(vv)

        def norm_pair(c, which, j0):
            buf = QH if which == 0 else KH
            claim = max(ring_ev[0], ring_ev[1])
            n0 = S.pe.n

            def vsq(v, buf=buf, j0=j0, n0=n0):
                v.wait_ge(S.pe.h, n0)
                v.tensor_tensor(out=SQ[:, 0:256], in0=buf[j0][:], in1=buf[j0][:], op=ALU.mult)
                v.tensor_tensor(out=SQ[:, 256:512], in0=buf[j0 + 1][:], in1=buf[j0 + 1][:], op=ALU.mult)
                v.drain()
                v.sem_inc(S.dve.h, 1)
                S.dve.n += 1
            p.v(vsq)
            nsq = S.dve.n

            def tnm(t, nsq=nsq, claim=claim):
                t.wait_ge(S.dve.h, max(nsq, claim))
                t.matmul(psR[0:1, 0:512], self.ONESH[:, 0:1], SQ[:, 0:512], start=True, stop=True)
                t.matmul(psM[0:1, 0:512], self.ONESH[:, 1:2], SQ[:, 0:512], start=True, stop=True).then_inc(S.pe.h, 1)
                S.pe.n += 1
            p.t(tnm)
            npe = S.pe.n

            def vnm(v, which=which, j0=j0, npe=npe):
                v.wait_ge(S.pe.h, npe)
                v.tensor_reduce(out=self.MSC[0:1, 0:1], in_=psR[0:1, 0:256], axis=AX.X, op=ALU.max)
                v.tensor_reduce(out=self.MSC[0:1, 1:2], in_=psM[0:1, 0:256], axis=AX.X, op=ALU.max)
                v.tensor_reduce(out=self.MSC[0:1, 2:3], in_=psR[0:1, 256:512], axis=AX.X, op=ALU.max)
                v.tensor_reduce(out=self.MSC[0:1, 3:4], in_=psM[0:1, 256:512], axis=AX.X, op=ALU.max)
                v.drain()
                v.tensor_tensor(out=self.MSC[0:1, 4:5], in0=self.MSC[0:1, 0:1], in1=self.MSC[0:1, 1:2], op=ALU.max)
                v.tensor_tensor(out=self.MSC[0:1, 5:6], in0=self.MSC[0:1, 2:3], in1=self.MSC[0:1, 3:4], op=ALU.max)
                v.drain()
                if which == 0:
                    v.tensor_copy(out=self.MSC[0:1, 8 + j0:10 + j0], in_=self.MSC[0:1, 4:6])
                else:
                    v.tensor_tensor(out=self.MSC[0:1, 6:7], in0=self.MSC[0:1, 4:5], in1=self.MSC[0:1, 5:6], op=ALU.max)
                    v.drain()
                    if j0 == 0:
                        v.tensor_copy(out=self.MSC[0:1, 16:17], in_=self.MSC[0:1, 6:7])
                    else:
                        v.tensor_tensor(out=self.MSC[0:1, 16:17], in0=self.MSC[0:1, 16:17], in1=self.MSC[0:1, 6:7], op=ALU.max)
                v.drain()
                v.sem_inc(S.dve.h, 1)
                S.dve.n += 1
                ring_ev[0] = ring_ev[1] = S.dve.n
            p.v(vnm)

        def bias_chunk(c):
            bcol = self.BCOL if c % 2 == 0 else BCO2
            slot = self.ring_slot()

            def vab(v):
                v.tensor_scalar(out=self.MSC[0:1, 24:32], in0=self.MSC[0:1, 8:16], scalar1=self.MSC[0:1, 16:17], scalar2=None, op0=ALU.mult).then_inc(S.dve.h, 1)
                S.dve.n += 1
            p.v(vab)
            nva = S.dve.n

            def sab(s):
                s.wait_ge(S.dve.h, nva)
                s.activation(out=self.MSC[0:1, 24:32], in_=self.MSC[0:1, 24:32], func=AF.Ln)
                s.drain()
                s.activation(out=self.MSC[0:1, 24:32], in_=self.MSC[0:1, 24:32], func=AF.Exp, scale=0.5)
                s.drain()
                s.sem_inc(S.act.h, 1)
                S.act.n += 1
            p.s(sab)
            nsa = S.act.n

            def vab2(v):
                v.wait_ge(S.act.h, nsa)
                v.tensor_scalar(out=self.MSC[0:1, 24:32], in0=self.MSC[0:1, 24:32], scalar1=-0.125, scalar2=None, op0=ALU.mult).then_inc(S.dve.h, 1)
                S.dve.n += 1
            p.v(vab2)
            nva2 = S.dve.n

            def tab(t, slot=slot, claim=ring_ev[slot]):
                t.wait_ge(S.dve.h, max(nva2, claim))
                dst = psR if slot == 0 else psM
                t.matmul(dst[:, 0:8], self.ONESF[0:1, :], self.MSC[0:1, 24:32], start=True, stop=True).then_inc(S.pe.h, 1)
                S.pe.n += 1
            p.t(tab)
            nta = S.pe.n

            def vab3(v, bcol=bcol, slot=slot):
                v.wait_ge(S.pe.h, nta)
                src = psR if slot == 0 else psM
                v.tensor_copy(out=bcol[:, 0:8], in_=src[:, 0:8]).then_inc(S.dve.h, 1)
                S.dve.n += 1
                ring_ev[slot] = S.dve.n
            p.v(vab3)
            return S.dve.n

        def qd2_chunk(c, gate_pe):
            def vqd(v, gate_pe=gate_pe):
                if gate_pe:
                    v.wait_ge(S.pe.h, gate_pe)
                for qb in range(8):
                    dst = QD2[qb]
                    v.tensor_copy(out=dst[0:64, 0:256], in_=QH[qb][0:64, :])
                    v.tensor_copy(out=dst[0:64, 256:512], in_=QH[qb][64:128, :])
                    v.tensor_copy(out=dst[64:128, 0:256], in_=QH[qb][0:64, :])
                    v.tensor_copy(out=dst[64:128, 256:512], in_=QH[qb][64:128, :])
                v.drain()
                v.sem_inc(S.dve.h, 1)
                S.dve.n += 1
            p.v(vqd)
            return S.dve.n

        def k_slice(kc):
            g_k = kc // 2
            return KH[g_k // 2][(g_k % 2) * 64:(g_k % 2) * 64 + 64, (kc % 2) * 128:(kc % 2) * 128 + 128]

        def v_slice(kc):
            return VT65[kc % 2][:, 65 * (kc // 2):65 * (kc // 2) + 65]

        ev_list = [kc for kc in range(32) if (kc // 2) % 2 == 0]
        od_list = [kc for kc in range(32) if (kc // 2) % 2 == 1]
        pairs = list(zip(ev_list, od_list))

        state = {"vev": [0], "exp_at": {}, "av_pe": {}}
        NCH = int(_os.environ.get("K_NCH", "8"))
        SEG = int(_os.environ.get("K_SEG", "5"))
        for ci in range(NCH + 1):
            ringq = []
            if ci < NCH:
                ringq.append(lambda c=ci: ln_chunk(c))
                for j in range(NFC):
                    ringq.append(lambda c=ci, j=j: proj_group(c, "q", j))
                for j0 in range(0, NFC, 2):
                    ringq.append(lambda c=ci, j0=j0: norm_pair(c, 0, j0))
            if ci >= 1 and SEG >= 4:
                bcol = self.BCOL if (ci - 1) % 2 == 0 else BCO2
                self.outer_attn_merged(p, ci - 1, psS, psAV, PT, QD2, XO,
                                       k_slice, v_slice, pairs, state, ringq, RB, bcol)
            for f in ringq:
                f()
            if ci < NCH and SEG >= 2:
                for j in range(NFC):
                    proj_group(ci, "k", j)
                for sh in range(2):
                    for ob in range(2):
                        vproj_group(ci, sh, ob)
            if ci < NCH and SEG >= 3:
                SUB = _os.environ.get("K_SUB", "full")
                for j0 in range(0, NFC, 2):
                    norm_pair(ci, 1, j0)
            if ci >= 1 and SEG >= 5:
                self.outer_wo_merged(p, ci - 1, WO, XO, WOUT, SQ, psums, ring_ev, state)
            if ci < NCH and SEG >= 3:
                state["bias", ci] = bias_chunk(ci) if SUB in ("nb", "nbq", "full") else 0
                state["qd2", ci] = qd2_chunk(ci, max(state.get(("qklast", ci - 1), 0),
                                                      state.get(("wo_pe", ci - 1), 0))) if SUB in ("nbq", "full") else 0
                if SUB in ("n", "nb", "nbq"):
                    state["vt", ci] = 0
                    continue

                def vones(v, ci=ci):
                    for sh in range(2):
                        for g2 in range(16):
                            v.memset(VT65[sh][:, 65 * g2 + 64:65 * g2 + 65], 1.0)
                    v.drain()
                    v.sem_inc(S.dve.h, 1)
                    S.dve.n += 1
                p.v(vones)
                state["vt", ci] = S.dve.n
            elif ci < NCH:
                state["bias", ci] = 0
                state["qd2", ci] = 0
                state["vt", ci] = 0
        p.emit()
        pst.close()
        st.close()

    def ring_slot(self):
        self._rslot ^= 1
        return self._rslot

    def outer_attn_merged(self, p, c, psS, psAV, PT, QD2, XO, k_slice,
                          v_slice, pairs, state, ringq, RB, bcol):
        S = self
        base = c * 128
        qk_pe = {}
        exp_at = state["exp_at"]
        av_pe = state["av_pe"]
        vev_done = state["vev"]
        vev_par = state.setdefault("vevp", [0, 0])
        psrel = state.setdefault("psrel", [0])
        srec_at = state.setdefault("srec", {})
        n_ready = max(state[("bias", c)], state[("qd2", c)], state[("vt", c)])

        def emit_av(qb, kg):
            kce, kco = pairs[kg]
            gk = base + qb * 16 + kg

            def tav(t, qb=qb, kg=kg, kce=kce, kco=kco, gk=gk, vp=vev_par[qb % 2]):
                t.wait_ge(S.act.h, exp_at[gk])
                if kg == 0 and vp:
                    t.wait_ge(S.dve.h, vp)
                ps = psAV[qb % 2]
                t.matmul(ps[:], v_slice(kce), PT[kg % 2][:, 0:512], start=(kg == 0), stop=False)
                t.matmul(ps[:], v_slice(kco), PT[kg % 2][:, 512:1024], start=False, stop=(kg == 15)).then_inc(S.pe.h, 1)
                S.pe.n += 1
            p.t(tav)
            av_pe[gk] = S.pe.n

        def emit_fin(qb):
            nrec = srec_at[qb]

            def trb(t, qb=qb, nrec=nrec):
                t.wait_ge(S.act.h, nrec)
                t.matmul(psS[1][0:64, 0:512], self.ONESF[0:1, 0:64], self.STAT[0:1, 0:512], start=True, stop=True).then_inc(S.pe.h, 1)
                S.pe.n += 1
            p.t(trb)
            nrb = S.pe.n

            def vev(v, qb=qb, nrb=nrb):
                v.wait_ge(S.pe.h, nrb)
                v.tensor_copy(out=RB, in_=psS[1][0:64, 0:512]).then_inc(S.dve.h, 1)
                S.dve.n += 1
                psrel[0] = S.dve.n
                v.drain()
                ps = psAV[qb % 2]
                v.tensor_tensor(out=XO[qb][0:64, :], in0=ps[0:64, 0:256], in1=RB[:, 0:256], op=ALU.mult)
                v.tensor_tensor(out=XO[qb][64:128, :], in0=ps[0:64, 256:512], in1=RB[:, 256:512], op=ALU.mult)
                v.drain()
                v.sem_inc(S.dve.h, 1)
                S.dve.n += 1
            p.v(vev)
            vev_done[0] = S.dve.n
            vev_par[qb % 2] = S.dve.n

        for qb in range(8):
            for kg, (kce, kco) in enumerate(pairs):
                gk = base + qb * 16 + kg

                def tqk(t, kg=kg, kce=kce, kco=kco, qb=qb, gk=gk, vev_prev=state["psrel"][0]):
                    if qb == 0 and kg == 0:
                        t.wait_ge(S.dve.h, n_ready)
                    if kg == 1 and vev_prev:
                        t.wait_ge(S.dve.h, vev_prev)
                    if gk >= 2 and (gk - 2) in exp_at:
                        t.wait_ge(S.act.h, exp_at[gk - 2])
                    sl = psS[kg % 2]
                    t.matmul(sl[:, 0:512], k_slice(kce), QD2[qb][0:64, :], start=True, stop=True)
                    t.matmul(sl[:, 512:1024], k_slice(kco), QD2[qb][64:128, :], start=True, stop=True).then_inc(S.pe.h, 1)
                    S.pe.n += 1
                p.t(tqk)
                qk_pe[gk] = S.pe.n
                if kg == 0 and qb >= 1:
                    emit_fin(qb - 1)

                def sexp(s, kg=kg, qb=qb, gk=gk):
                    s.wait_ge(S.pe.h, qk_pe[gk])
                    if gk >= 2 and (gk - 2) in av_pe:
                        s.wait_ge(S.pe.h, av_pe[gk - 2])
                    s.activation(out=PT[kg % 2][:], in_=psS[kg % 2][:], func=AF.Exp,
                                 bias=bcol[:, qb:qb + 1], scale=0.125).then_inc(S.act.h, 1)
                    S.act.n += 1
                p.s(sexp)
                exp_at[gk] = S.act.n
                if kg >= 1:
                    emit_av(qb, kg - 1)
                if kg == 15:
                    emit_av(qb, 15)
            n_av = av_pe[base + qb * 16 + 15]

            def srec(s, qb=qb, n_av=n_av):
                s.wait_ge(S.pe.h, n_av)
                s.activation(out=self.STAT[0:1, 0:512], in_=psAV[qb % 2][64:65, :], func=AF.Ln)
                s.drain()
                s.activation(out=self.STAT[0:1, 0:512], in_=self.STAT[0:1, 0:512], func=AF.Exp, scale=-1.0)
                s.drain()
                s.sem_inc(S.act.h, 1)
                S.act.n += 1
            p.s(srec)
            srec_at[qb] = S.act.n
            if not _os.environ.get("K_NOWEAVE"):
                npop = 1 if qb < 4 else 2
                for _ in range(npop):
                    if ringq:
                        ringq.pop(0)()
        emit_fin(7)
        state[("qklast", c)] = qk_pe[base + 7 * 16 + 15]
        state[("xo", c)] = vev_done[0]

    def outer_wo_merged(self, p, c, WO, XO, WOUT, SQ, psums, ring_ev, state):
        nc, S = self.nc, self
        cols = slice(1 + 256 * c, 1 + 256 * c + 256)
        psR, psM = psums["R"], psums["M"]
        n_xo = state[("xo", c)]
        for hc in range(NFC):
            slot = self.ring_slot()

            def tmm(t, hc=hc, slot=slot, claim=ring_ev[slot]):
                t.wait_ge(S.dve.h, max(claim, n_xo))
                dst = psR if slot == 0 else psM
                for ot in range(NFC):
                    mm = t.matmul(dst[:, 0:256], WO[ot][:, 128 * hc:128 * (hc + 1)], XO[ot][:], start=(ot == 0), stop=(ot == NFC - 1))
                mm.then_inc(S.pe.h, 1)
                S.pe.n += 1
            p.t(tmm)
            npe = S.pe.n

            def vev(v, hc=hc, npe=npe, slot=slot):
                v.wait_ge(S.pe.h, npe)
                src = psR if slot == 0 else psM
                v.tensor_copy(out=WOUT[hc][:], in_=src[:, 0:256]).then_inc(S.dve.h, 1)
                S.dve.n += 1
                ring_ev[slot] = S.dve.n
            p.v(vev)
        self.ln_ring(p, psums, ring_ev, WOUT, WOUT, slice(0, 256), self.XA,
                     out_cols=cols, sq_scr=SQ, res_tiles=self.XA, res_cols=cols)
        state[("wo_pe", c)] = S.pe.n
        nd = S.dve.n
        bspo = self._bspo

        def ysp(sync, c=c, nd=nd):
            sync.wait_ge(S.dve.h, nd)
            for i in range(NFC):
                sync.dma_start(out=self.xo_spill.ap()[128 * i:128 * (i + 1), 256 * c:256 * c + 256],
                               in_=self.XA[i][:, cols]).then_inc(bspo.h, 16)
            bspo.n += 16 * NFC
        p.y(ysp)

    def ex_send(self, p, name, sends, extra=None):
        S = self
        bs = self.hw_batch()
        rounds = [(name, sends)] + ([extra] if extra else [])

        def y(sync):
            for nm, snds in rounds:
                for dram_ap, sb_ap in snds:
                    sync.dma_start(out=dram_ap, in_=sb_ap).then_inc(bs.h, 16)
                    bs.n += 16
        p.y(y)
        ndx = bs.n

        def g(gp):
            gp.wait_ge(bs.h, ndx)
            for nm, snds in rounds:
                gp.collective_compute(
                    "AllGather", ALU.bypass,
                    replica_groups=[[0, 1], [2, 3], [4, 5], [6, 7]],
                    ins=[self.ex[nm][0].ap()], outs=[self.ex[nm][1].ap()],
                ).then_inc(S.cc.h, 1)
                S.cc.n += 1
        p.g(g)
        return S.cc.n

    def ex_send_cols(self, p, name, col_ap_list):
        S = self

        def v(vv):
            for i, ap in enumerate(col_ap_list):
                vv.tensor_copy(out=self.HS[:, i:i + 1], in_=ap)
            vv.drain()
            vv.sem_inc(S.dve.h, 1)
            S.dve.n += 1
        p.v(v)
        nv = S.dve.n

        def y(sync):
            sync.wait_ge(S.dve.h, nv)
        p.y(y)
        return self.ex_send(p, name, [(self.ex[name][0].ap()[:, :], self.HS[:, :])])

    def ex_recv(self, p, name, cc_at, recvs):
        S = self
        bs = self.hw_batch()

        def y(sync):
            sync.wait_ge(S.cc.h, cc_at)
            for sb_ap, dram_ap in recvs:
                sync.dma_start(out=sb_ap, in_=dram_ap).then_inc(bs.h, 16)
                bs.n += 16
        p.y(y)
        return (bs, bs.n)

    def ph_x0(self):
        nc, S = self.nc, self
        st = ExitStack()
        psA = st.enter_context(nc.psum_tensor("x0psA", [1, 512], F32))
        psB = st.enter_context(nc.psum_tensor("x0psB", [1, 512], F32))
        psR = st.enter_context(nc.psum_tensor("x0psR", [128, 512], F32))
        psM = st.enter_context(nc.psum_tensor("x0psM", [128, 512], F32))
        scr = st.enter_context(nc.sbuf_tensor("x0scr", [128, 512], F32))
        sq = st.enter_context(nc.sbuf_tensor("x0sq", [128, 1024], BF16))
        psums = {"A": psA, "B": psB, "R": psR, "M": psM, "scr": scr}
        p = Phase(self)

        bsp = self._bspo
        nspill = bsp.n

        def tws(t):
            t.wait_ge(bsp.h, nspill)
        p.t(tws)

        def vws(v):
            v.wait_ge(bsp.h, nspill)
            for i in range(NFC):
                v.tensor_copy(out=self.XS[i][:, 1:TW], in_=self.XA[i][:, 1:TW])
            v.drain()
            v.sem_inc(S.dve.h, 1)
            S.dve.n += 1
        p.v(vws)

        for tb in [3, 0, 1, 2]:
            cols = slice(1 + 512 * tb, 1 + 512 * tb + 512)
            self.ln_block(p, psums, self.XA, self.XS, cols, self.XA,
                          sq_scr=sq, out_bf=self.XS)
            if tb == 3:
                self._ecc["x0col"] = self.ex_send_cols(p, "x0col", [self.XA[i][:, TW - 1:TW] for i in range(NFC)])
        p.emit()
        st.close()

    def recv_halo_cols(self, p, name, cc_at, scr_tile):
        S = self
        rb, ndx = self.ex_recv(p, name, cc_at, [(scr_tile[:, 0:8], self.ex[name][1].ap()[0])])

        def v(vv):
            vv.wait_ge(rb.h, ndx)
            for i in range(NFC):
                vv.tensor_scalar(out=self.XA[i][:, 0:1], in0=scr_tile[:, i:i + 1], scalar1=self.GATE[:, 0:1], scalar2=None, op0=ALU.mult)
            vv.drain()
            for i in range(NFC):
                vv.tensor_copy(out=self.XS[i][:, 0:1], in_=self.XA[i][:, 0:1])
            vv.drain()
            vv.sem_inc(S.dve.h, 1)
            S.dve.n += 1
        p.v(v)
        return S.dve.n

    def ph_ffn(self):
        for l in range(2):
            self.inner_layer(l)
            if self._stop:
                return

    def proj_norm_rope(self, p, tb, W, wcol0, scale_col, tc, ts, ps,
                       dst_tile, dst_col0, max_cell):
        S = self
        PQ, psA, psB, psRN = ps["PQ"], ps["A"], ps["B"], ps["RN"]
        KNS, TMPA, TMPB, SQ2, RNB = ps["KNS"], ps["TMPA"], ps["TMPB"], ps["SQ2"], ps["RNB"]
        STT = ps.get("STAT", self.STAT)
        MC = ps.get("MC", 0)
        c_hi = slice(1 + 512 * tb, 1 + 512 * tb + 512)
        c_lo = slice(512 * tb, 512 * tb + 512)

        def tmm(t):
            if ps["free"][0]:
                t.wait_ge(S.dve.h, ps["free"][0])
            for i in range(NFC):
                mm = t.matmul(PQ[:], W[i][:, wcol0:wcol0 + 128],
                              self.XS[i][:, c_hi if i < 4 else c_lo],
                              start=(i == 0), stop=(i == NFC - 1))
            mm.then_inc(S.pe.h, 1)
            S.pe.n += 1
        p.t(tmm)
        n_pq = S.pe.n

        def vsq(v):
            v.wait_ge(S.pe.h, n_pq)
            v.tensor_copy(out=ps["QRW"], in_=PQ[:])
            v.drain()
            v.tensor_tensor(out=ps["SQB"], in0=ps["QRW"], in1=ps["QRW"], op=ALU.mult).then_inc(S.dve.h, 1)
            S.dve.n += 1
        p.v(vsq)
        n_sq = S.dve.n

        def tsum(t):
            t.wait_ge(S.dve.h, n_sq)
            t.matmul(psA[0:1, :], self.ONESB[0:64, 0:1], ps["SQB"][0:64, :], start=True, stop=True)
            t.matmul(psB[0:1, :], self.ONESB[64:128, 0:1], ps["SQB"][64:128, :], start=True, stop=True).then_inc(S.pe.h, 1)
            S.pe.n += 1
        p.t(tsum)
        n_sums = S.pe.n

        def vrn(v):
            v.wait_ge(S.pe.h, n_sums)
            v.tensor_scalar(out=STT[0:1, 0:512], in0=psA[0:1, :], scalar1=1e-24, scalar2=None, op0=ALU.max)
            v.tensor_scalar(out=STT[0:1, 512:1024], in0=psB[0:1, :], scalar1=1e-24, scalar2=None, op0=ALU.max).then_inc(S.dve.h, 1)
            S.dve.n += 1
        p.v(vrn)
        n_st = S.dve.n

        def ssqrt(s):
            s.wait_ge(S.dve.h, n_st)
            s.activation(out=STT[0:1, :], in_=STT[0:1, :], func=AF.Ln)
            s.drain()
            s.activation(out=STT[0:1, :], in_=STT[0:1, :], func=AF.Exp, scale=-0.5)
            s.drain()
            s.sem_inc(S.act.h, 1)
            S.act.n += 1
        p.s(ssqrt)
        n_rec = S.act.n

        def tbc(t):
            t.wait_ge(S.act.h, n_rec)
            t.matmul(psRN[0:64, :], self.ONESF[0:1, 0:64], STT[0:1, 0:512], start=True, stop=True)
            t.matmul(psRN[64:128, :], self.ONESF[0:1, 0:64], STT[0:1, 512:1024], start=True, stop=True).then_inc(S.pe.h, 1)
            S.pe.n += 1
        p.t(tbc)
        n_bc = S.pe.n

        def vkn(v):
            v.wait_ge(S.pe.h, n_bc)
            v.tensor_copy(out=RNB[:], in_=psRN[:])
            v.drain()
            v.scalar_tensor_tensor(out=KNS[:], in0=ps["QRW"], scalar=self.QKS[:, scale_col:scale_col + 1],
                                   in1=RNB[:], op0=ALU.mult, op1=ALU.mult)
            v.drain()
            v.tensor_tensor(out=SQ2[:], in0=KNS[:], in1=KNS[:], op=ALU.mult).then_inc(S.dve.h, 1)
            S.dve.n += 1
        p.v(vkn)
        n_kn = S.dve.n

        def tnm(t):
            t.wait_ge(S.dve.h, n_kn)
            t.matmul(psA[0:1, :], self.ONESB[0:64, 0:1], SQ2[0:64, :], start=True, stop=True)
            t.matmul(psB[0:1, :], self.ONESB[64:128, 0:1], SQ2[64:128, :], start=True, stop=True).then_inc(S.pe.h, 1)
            S.pe.n += 1
        p.t(tnm)
        n_nm = S.pe.n

        def vmax(v):
            v.wait_ge(S.pe.h, n_nm)
            v.tensor_reduce(out=self.MSC[0:1, MC:MC + 1], in_=psA[0:1, :], axis=AX.X, op=ALU.max)
            v.tensor_reduce(out=self.MSC[0:1, MC + 1:MC + 2], in_=psB[0:1, :], axis=AX.X, op=ALU.max)
            v.drain()
            v.tensor_tensor(out=self.MSC2[0:1, max_cell:max_cell + 1], in0=self.MSC[0:1, MC:MC + 1], in1=self.MSC[0:1, MC + 1:MC + 2], op=ALU.max)
            v.drain()
            tcs = tc[:, 512 * tb:512 * tb + 512]
            tss = ts[:, 512 * tb:512 * tb + 512]
            for b0 in range(0, 128, 32):
                r0 = b0 ^ 32
                v.tensor_copy(out=TMPA[b0:b0 + 32, :], in_=KNS[r0:r0 + 32, :])
            v.tensor_tensor(out=TMPB[:], in0=KNS[:], in1=tcs, op=ALU.mult)
            v.drain()
            v.tensor_tensor(out=TMPA[:], in0=TMPA[:], in1=tss, op=ALU.mult)
            v.drain()
            v.tensor_tensor(out=dst_tile[:, dst_col0:dst_col0 + 512], in0=TMPB[:], in1=TMPA[:], op=ALU.add)
            v.drain()
            v.sem_inc(S.dve.h, 1)
            S.dve.n += 1
            ps["free"][0] = S.dve.n
        p.v(vmax)
        return S.dve.n

    def inner_layer(self, l):
        nc, S, d = self.nc, self, self.d
        stKV = ExitStack()
        KH = [stKV.enter_context(nc.sbuf_tensor(f"iKH{l}_{j}", [128, 512 + T], BF16)) for j in range(4)]
        VI = [stKV.enter_context(nc.sbuf_tensor(f"iVI{l}_{n}", [128, 520], BF16)) for n in range(20)]
        HCOL = stKV.enter_context(nc.sbuf_tensor(f"iHCOL{l}", [128, 8], F32))
        p = Phase(self)

        def vones(v):
            for n in range(20):
                for g in range(8):
                    v.memset(VI[n][:, 65 * g + 64:65 * g + 65], 1.0)
            v.drain()
            v.sem_inc(S.dve.h, 1)
            S.dve.n += 1
        p.v(vones)
        p.emit()
        kv_name = f"kv{l}"
        nrm_name = f"nrm{l}"
        xm_name = f"xm{l}"
        stA = ExitStack()
        KNS = [stA.enter_context(nc.sbuf_tensor(f"KNS{l}_{i}", [128, 512], BF16)) for i in range(2)]
        TMPA = [stA.enter_context(nc.sbuf_tensor(f"TMPA{l}_{i}", [128, 512], BF16)) for i in range(2)]
        TMPB = [stA.enter_context(nc.sbuf_tensor(f"TMPB{l}_{i}", [128, 512], BF16)) for i in range(2)]
        SQ2 = [stA.enter_context(nc.sbuf_tensor(f"SQ2{l}_{i}", [128, 512], BF16)) for i in range(2)]
        SQB = [stA.enter_context(nc.sbuf_tensor(f"SQB{l}_{i}", [128, 512], BF16)) for i in range(2)]
        RNB2 = [stA.enter_context(nc.sbuf_tensor(f"RNB2{l}_{i}", [128, 512], F32)) for i in range(2)]
        STATB = self.SROW
        WOI = [stA.enter_context(nc.sbuf_tensor(f"WOI{l}_{i}", [128, 1024], BF16, side="right")) for i in range(4)]
        stW = ExitStack()
        WKV = [stW.enter_context(nc.sbuf_tensor(f"WKV{l}_{i}", [128, 1024], BF16)) for i in range(NFC)]

        pst = ExitStack()
        PQ = [pst.enter_context(nc.psum_tensor(f"iPQ{l}_{i}", [128, 512], F32)) for i in range(2)]
        psA = [pst.enter_context(nc.psum_tensor(f"ipsA{l}_{i}", [1, 512], F32)) for i in range(2)]
        psB = [pst.enter_context(nc.psum_tensor(f"ipsB{l}_{i}", [1, 512], F32)) for i in range(2)]
        psRN = [pst.enter_context(nc.psum_tensor(f"ipsRN{l}_{i}", [128, 512], F32)) for i in range(2)]
        p = Phase(self)

        bw = self.sw_batch()

        def gw(gp):
            w = d[f"wqkv{l}"]
            for i in range(NFC):
                gp.dma_start(out=WKV[i][:], in_=w[128 * i:128 * (i + 1), 512:1536]).then_inc(bw.h, 16)
            wo = d[f"wo{l}"]
            for i in range(4):
                gp.dma_start(out=WOI[i][:], in_=wo[128 * i:128 * (i + 1), :]).then_inc(bw.h, 16)
            bw.n += 16 * (NFC + 4)
        p.g(gw)
        n_w = bw.n

        colname = "x0col" if l == 0 else "x1col"
        n_xcol = self.recv_halo_cols(p, colname, self._ecc[colname], HCOL)

        def tww(t):
            t.wait_ge(bw.h, n_w)
        p.t(tww)

        pss = [{"PQ": PQ[i], "A": psA[i], "B": psB[i], "RN": psRN[i], "KNS": KNS[i],
                "TMPA": TMPA[i], "TMPB": TMPB[i], "SQ2": SQ2[i], "RNB": RNB2[i],
                "SQB": SQB[i][:], "QRW": TMPA[i][:], "free": [0],
                "STAT": (self.STAT if i == 0 else STATB),
                "MC": (0 if i == 0 else 2)} for i in range(2)]
        OT = [KNS[1], TMPA[1], TMPB[1], SQB[1]]
        pqi = [0]
        for tb in [3, 0, 1, 2]:
            if tb == 0:
                def twx(t):
                    t.wait_ge(S.dve.h, n_xcol)
                p.t(twx)

                def vwx(v):
                    v.wait_ge(S.dve.h, n_xcol)
                p.v(vwx)
            for j in range(4):
                self.proj_norm_rope(p, tb, WKV, 128 * j, 2 * l + 1, self.TCK, self.TSK,
                                    pss[pqi[0] % 2], KH[j], 512 + 512 * tb, 16 + 5 * j + 1 + tb)
                pqi[0] += 1
            for n4 in range(4):
                tt = 4 * tb + n4
                slot = pqi[0] % 2
                pqi[0] += 1
                ps_v = pss[slot]

                def tvm(t, tb=tb, n4=n4, slot=slot, ps_v=ps_v):
                    t.wait_ge(S.dve.h, ps_v["free"][0])
                    dst = PQ[slot]
                    for i in range(NFC):
                        c0 = (1 if i < 4 else 0) + 512 * tb + 128 * n4
                        mm = t.matmul(dst[:], self.XS[i][:, c0:c0 + 128], WKV[i][:, 512:1024], start=(i == 0), stop=(i == NFC - 1))
                    mm.then_inc(S.pe.h, 1)
                    S.pe.n += 1
                p.t(tvm)
                npe = S.pe.n

                def vvev(v, tt=tt, npe=npe, slot=slot, ps_v=ps_v):
                    v.wait_ge(S.pe.h, npe)
                    src = PQ[slot]
                    for g in range(8):
                        v.tensor_copy(out=VI[4 + tt][:, 65 * g:65 * g + 64], in_=src[:, 64 * g:64 * (g + 1)])
                    v.drain()
                    v.sem_inc(S.dve.h, 1)
                    S.dve.n += 1
                    ps_v["free"][0] = S.dve.n
                p.v(vvev)
            if tb == 3:
                n_tb3 = S.dve.n

                def ysw(sync):
                    sync.wait_ge(S.dve.h, n_tb3)
                p.y(ysw)
                kvsrc = self.ex[kv_name][0]
                sends = []
                for j in range(4):
                    sends.append((kvsrc.ap()[128 * j:128 * (j + 1), 0:512], KH[j][:, 2048:2560]))
                for n in range(4):
                    sends.append((kvsrc.ap()[128 * n:128 * (n + 1), 512:1032], VI[16 + n][:]))
                nsend = []
                for j in range(4):
                    nsend.append((self.ex[nrm_name][0].ap()[0:1, j:j + 1], self.MSC2[0:1, 16 + 5 * j + 4:16 + 5 * j + 5]))
                cc_kv = self.ex_send(p, kv_name, sends, extra=(nrm_name, nsend))
                recvs = []
                for j in range(4):
                    recvs.append((KH[j][:, 0:512], self.ex[kv_name][1].ap()[0][128 * j:128 * (j + 1), 0:512]))
                for n in range(4):
                    recvs.append((VI[n][:], self.ex[kv_name][1].ap()[0][128 * n:128 * (n + 1), 512:1032]))
                recvs.append((self.MSC[0:1, 4:8], self.ex[nrm_name][1].ap()[0][0:1, 0:4]))
                rkb, ndx_kv = self.ex_recv(p, kv_name, cc_kv, recvs)

                def vnrm(v):
                    v.wait_ge(rkb.h, ndx_kv)
                    for j in range(4):
                        v.tensor_copy(out=self.MSC2[0:1, 16 + 5 * j:16 + 5 * j + 1], in_=self.MSC[0:1, 4 + j:5 + j])
                    v.drain()
                    v.sem_inc(S.dve.h, 1)
                    S.dve.n += 1
                p.v(vnrm)
                n_kvhalo = S.dve.n
        p.emit()
        pst.close()
        stW.close()
        self._n_kvhalo = n_kvhalo

        stB = ExitStack()
        PT = [stB.enter_context(nc.sbuf_tensor(f"PT{l}_{i}", [128, 1024], BF16)) for i in range(5)]
        QW = [stB.enter_context(nc.sbuf_tensor(f"iQW{l}_{j}", [128, 512], BF16)) for j in range(4)]
        WQ1 = [stB.enter_context(nc.sbuf_tensor(f"WQ1{l}_{i}", [128, 512], BF16)) for i in range(NFC)]
        pq1 = Phase(self)
        bq = self.sw_batch()

        def gq(gp):
            wq = self.d[f"wqkv{l}"]
            for i in range(NFC):
                gp.dma_start(out=WQ1[i][:], in_=wq[128 * i:128 * (i + 1), 0:512]).then_inc(bq.h, 16)
            bq.n += 16 * NFC
        pq1.g(gq)
        nbq = bq.n

        def tq1(t):
            t.wait_ge(bq.h, nbq)
        pq1.t(tq1)
        pq1.emit()
        for w in [1, 2, 3, 0]:
            self.inner_window(l, w, KH, VI, QW, WQ1, OT, WOI, PT, RNB2[0],
                              KNS, TMPA, TMPB, SQ2, SQB, RNB2, STATB, HCOL)
        stB.close()
        p = Phase(self)

        def vsh(v):
            for i in range(NFC):
                v.tensor_copy(out=self.XS[i][:, 1:TW], in_=self.XA[i][:, 1:TW])
            v.drain()
            v.sem_inc(S.dve.h, 1)
            S.dve.n += 1
        p.v(vsh)
        p.emit()
        stA.close()
        stKV.close()
        if self.stage == f"L{l}attn":
            self.ph_emit_stream(self.XA)
            self._stop = True
            return

        self.inner_ffn(l)
        if self.stage == f"L{l}":
            self.ph_emit_stream(self.XA)
            self._stop = True

    def inner_window(self, l, w, KH, VI, QW, WQ1, OT, WOI, PT, RNB,
                     KNS, TMPA, TMPB, SQ2, SQB, RNB2, STATB, HCOL):
        nc, S = self.nc, self
        st = ExitStack()
        p = Phase(self)
        pq_st = ExitStack()
        PQ = [pq_st.enter_context(nc.psum_tensor(f"qPQ{l}_{w}_{i}", [128, 512], F32)) for i in range(2)]
        psA = [pq_st.enter_context(nc.psum_tensor(f"qpsA{l}_{w}_{i}", [1, 512], F32)) for i in range(2)]
        psB = [pq_st.enter_context(nc.psum_tensor(f"qpsB{l}_{w}_{i}", [1, 512], F32)) for i in range(2)]
        psRN = [pq_st.enter_context(nc.psum_tensor(f"qpsRN{l}_{w}_{i}", [128, 512], F32)) for i in range(2)]
        pssq = [{"PQ": PQ[i], "A": psA[i], "B": psB[i], "RN": psRN[i], "KNS": KNS[i],
                 "TMPA": TMPA[i], "TMPB": TMPB[i], "SQ2": SQ2[i], "RNB": RNB2[i],
                 "SQB": SQB[i][:], "QRW": TMPA[i][:], "free": [0],
                 "STAT": (self.STAT if i == 0 else STATB),
                 "MC": (0 if i == 0 else 2)} for i in range(2)]
        for j in range(4):
            self.proj_norm_rope(p, w, WQ1, 128 * j, 2 * l, self.TCQ, self.TSQ,
                                pssq[j % 2], QW[j], 0, 4 * j + w)
        n_qdone = S.dve.n
        pq_st.close()

        at_st = ExitStack()
        psS = [at_st.enter_context(nc.psum_tensor(f"apsS{l}_{w}_{i}", [128, 1024], F32)) for i in range(2)]
        psAV = [at_st.enter_context(nc.psum_tensor(f"apsAV{l}_{w}_{i}", [65, 512], F32)) for i in range(2)]
        psBC = at_st.enter_context(nc.psum_tensor(f"apsBC{l}_{w}", [64, 512], F32))

        def vb1(v):
            for hp in range(4):
                v.tensor_tensor(out=self.MSC[0:1, hp:hp + 1],
                                in0=self.MSC2[0:1, 16 + 5 * hp + w:17 + 5 * hp + w],
                                in1=self.MSC2[0:1, 17 + 5 * hp + w:18 + 5 * hp + w], op=ALU.max)
            v.drain()
            for hp in range(4):
                v.tensor_tensor(out=self.MSC[0:1, hp:hp + 1], in0=self.MSC[0:1, hp:hp + 1],
                                in1=self.MSC2[0:1, 4 * hp + w:4 * hp + w + 1], op=ALU.mult)
            v.drain()
            v.sem_inc(S.dve.h, 1)
            S.dve.n += 1
        p.v(vb1)
        nb1 = S.dve.n

        def sb1(s):
            s.wait_ge(S.dve.h, nb1)
            s.activation(out=self.MSC[0:1, 0:4], in_=self.MSC[0:1, 0:4], func=AF.Ln)
            s.drain()
            s.activation(out=self.MSC[0:1, 0:4], in_=self.MSC[0:1, 0:4], func=AF.Exp, scale=0.5)
            s.drain()
            s.sem_inc(S.act.h, 1)
            S.act.n += 1
        p.s(sb1)
        nb2 = S.act.n

        def vb2(v):
            v.wait_ge(S.act.h, nb2)
            v.tensor_scalar(out=self.MSC[0:1, 0:4], in0=self.MSC[0:1, 0:4], scalar1=-8.0, scalar2=None, op0=ALU.mult).then_inc(S.dve.h, 1)
            S.dve.n += 1
        p.v(vb2)
        nb3 = S.dve.n

        def tb1(t):
            t.wait_ge(S.dve.h, nb3)
            t.matmul(psS[0][:, 0:4], self.ONESF[0:1, :], self.MSC[0:1, 0:4], start=True, stop=True).then_inc(S.pe.h, 1)
            S.pe.n += 1
        p.t(tb1)
        nb4 = S.pe.n

        def vb3(v):
            v.wait_ge(S.pe.h, nb4)
            v.tensor_copy(out=self.BCOL[:, 0:4], in_=psS[0][:, 0:4]).then_inc(S.dve.h, 1)
            S.dve.n += 1
        p.v(vb3)
        n_bias = S.dve.n

        def mask_sites():
            sites = []
            for kc in range(8):
                if w == 0 and kc < 4:
                    for qb in range(0, min(kc, 3) + 1):
                        sites.append((kc, qb, self.MW0D0 if qb == kc else self.MW0F))
                elif kc <= 3:
                    sites.append((kc, kc, self.MD0))
                if kc >= 4:
                    sites.append((kc, kc - 4, self.MD4))
            return sites
        SITES = mask_sites()

        qk_pe = {}
        exp_at = {}
        av_pe = {}
        ev_dve = {}
        for hp in range(4):
            for kc in range(9):
                kci = hp * 8 + kc
                if kc < 8:
                    kcc = 512 * w + 128 * kc
                    c0 = max(0, 128 * (kc - 4))
                    c1 = min(512, 128 * (kc + 1))

                    def tqk(t, hp=hp, kc=kc, kci=kci, kcc=kcc, c0=c0, c1=c1):
                        if kci == 0:
                            t.wait_ge(S.dve.h, max(n_qdone, n_bias, self._n_kvhalo if w == 0 else 0))
                        if kci >= 2:
                            t.wait_ge(S.act.h, exp_at[kci - 2])
                        sl = psS[kci % 2]
                        t.matmul(sl[:, c0:c1], KH[hp][0:64, kcc:kcc + 128], QW[hp][0:64, c0:c1], start=True, stop=True)
                        t.matmul(sl[:, 512 + c0:512 + c1], KH[hp][64:128, kcc:kcc + 128], QW[hp][64:128, c0:c1], start=True, stop=True).then_inc(S.pe.h, 1)
                        S.pe.n += 1
                    p.t(tqk)
                    qk_pe[kci] = S.pe.n

                    msites = [(qb, m) for (kc2, qb, m) in SITES if kc2 == kc]
                    if msites:
                        def vmask(v, hp=hp, kc=kc, kci=kci, msites=msites):
                            v.wait_ge(S.pe.h, qk_pe[kci])
                            for qb, m in msites:
                                v.tensor_tensor(out=psS[kci % 2][:, 128 * qb:128 * qb + 128], in0=psS[kci % 2][:, 128 * qb:128 * qb + 128], in1=m[:], op=ALU.add)
                                v.tensor_tensor(out=psS[kci % 2][:, 512 + 128 * qb:512 + 128 * qb + 128], in0=psS[kci % 2][:, 512 + 128 * qb:512 + 128 * qb + 128], in1=m[:], op=ALU.add)
                            v.drain()
                            v.sem_inc(S.dve.h, 1)
                            S.dve.n += 1
                        p.v(vmask)
                        n_mask = S.dve.n
                    else:
                        n_mask = None

                    def sexp(s, hp=hp, kc=kc, kci=kci, n_mask=n_mask, c0=c0, c1=c1):
                        if kci == 0:
                            s.wait_ge(S.dve.h, n_bias)
                        s.wait_ge(S.pe.h, qk_pe[kci])
                        if n_mask is not None:
                            s.wait_ge(S.dve.h, n_mask)
                        if kci >= 5:
                            kprev = kci - 5
                            s.wait_ge(S.pe.h, av_pe.get((kprev // 8, min(3, kprev % 8)), 0))
                        s.activation(out=PT[kci % 5][:, c0:c1], in_=psS[kci % 2][:, c0:c1], func=AF.Exp,
                                     bias=self.BCOL[:, hp:hp + 1], scale=8.0)
                        s.activation(out=PT[kci % 5][:, 512 + c0:512 + c1], in_=psS[kci % 2][:, 512 + c0:512 + c1], func=AF.Exp,
                                     bias=self.BCOL[:, hp:hp + 1], scale=8.0).then_inc(S.act.h, 1)
                        S.act.n += 1
                    p.s(sexp)
                    exp_at[kci] = S.act.n

                if kc >= 5:
                    qb = kc - 5

                    def tav(t, hp=hp, kc=kc, qb=qb, kci=kci):
                        t.wait_ge(S.act.h, exp_at[hp * 8 + qb + 4])
                        if qb == 0 and hp >= 1:
                            t.wait_ge(S.dve.h, ev_dve.get(hp - 1, 0))
                        for head in range(2):
                            for k5 in range(qb, qb + 5):
                                mm = t.matmul(psAV[head][:, 128 * qb:128 * qb + 128],
                                              VI[4 * w + k5][:, 65 * (2 * hp + head):65 * (2 * hp + head) + 65],
                                              PT[(hp * 8 + k5) % 5][:, 512 * head + 128 * qb:512 * head + 128 * qb + 128],
                                              start=(k5 == qb), stop=(k5 == qb + 4))
                        mm.then_inc(S.pe.h, 1)
                        S.pe.n += 1
                    p.t(tav)
                    av_pe[(hp, qb)] = S.pe.n
            for qb in [3]:
                def tav2(t, hp=hp, qb=qb):
                    t.wait_ge(S.act.h, exp_at[hp * 8 + 7])
                    for head in range(2):
                        for k5 in range(qb, qb + 5):
                            mm = t.matmul(psAV[head][:, 128 * qb:128 * qb + 128],
                                          VI[4 * w + k5][:, 65 * (2 * hp + head):65 * (2 * hp + head) + 65],
                                          PT[(hp * 8 + k5) % 5][:, 512 * head + 128 * qb:512 * head + 128 * qb + 128],
                                          start=(k5 == qb), stop=(k5 == qb + 4))
                    mm.then_inc(S.pe.h, 1)
                    S.pe.n += 1
                p.t(tav2)
                av_pe[(hp, 3)] = S.pe.n
            n_av = S.pe.n

            def srec2(s, hp=hp, n_av=n_av):
                s.wait_ge(S.pe.h, n_av)
                s.activation(out=self.STAT[0:1, 0:512], in_=psAV[0][64:65, :], func=AF.Ln)
                s.activation(out=self.STAT[0:1, 512:1024], in_=psAV[1][64:65, :], func=AF.Ln)
                s.drain()
                s.activation(out=self.STAT[0:1, :], in_=self.STAT[0:1, :], func=AF.Exp, scale=-1.0)
                s.drain()
                s.sem_inc(S.act.h, 1)
                S.act.n += 1
            p.s(srec2)
            n_rc = S.act.n

            def trb(t, hp=hp):
                t.wait_ge(S.act.h, n_rc)
                t.matmul(psBC[:, :], self.ONESF[0:1, 0:64], self.STAT[0:1, 0:512], start=True, stop=True).then_inc(S.pe.h, 1)
                S.pe.n += 1
            p.t(trb)
            n_rb = S.pe.n

            def vev(v, hp=hp, n_rb=n_rb):
                v.wait_ge(S.pe.h, n_rb)
                v.tensor_copy(out=RNB[0:64, :], in_=psBC[:, :])
                v.drain()
                v.tensor_tensor(out=OT[hp][0:64, :], in0=psAV[0][0:64, :], in1=RNB[0:64, :], op=ALU.mult).then_inc(S.dve.h, 1)
                S.dve.n += 1
            p.v(vev)
            n_ev1 = S.dve.n

            def trb2(t, hp=hp):
                t.wait_ge(S.dve.h, n_ev1)
                t.matmul(psBC[:, :], self.ONESF[0:1, 0:64], self.STAT[0:1, 512:1024], start=True, stop=True).then_inc(S.pe.h, 1)
                S.pe.n += 1
            p.t(trb2)
            n_rb2 = S.pe.n

            def vev2(v, hp=hp, n_rb2=n_rb2):
                v.wait_ge(S.pe.h, n_rb2)
                v.tensor_copy(out=RNB[0:64, :], in_=psBC[:, :])
                v.drain()
                v.tensor_tensor(out=OT[hp][64:128, :], in0=psAV[1][0:64, :], in1=RNB[0:64, :], op=ALU.mult)
                v.drain()
                v.sem_inc(S.dve.h, 1)
                S.dve.n += 1
            p.v(vev2)
            ev_dve[hp] = S.dve.n
        n_ot = S.dve.n

        at_st.close()
        psW = [st.enter_context(nc.psum_tensor(f"wpsW{l}_{w}_{i}", [128, 512], F32)) for i in range(2)]
        cols = slice(1 + 512 * w, 1 + 512 * w + 512)
        ev_w = []
        for hc in range(NFC):
            def two(t, hc=hc):
                if hc == 0:
                    t.wait_ge(S.dve.h, n_ot)
                if hc >= 2:
                    t.wait_ge(S.dve.h, ev_w[hc - 2])
                dst = psW[hc % 2]
                for kt in range(4):
                    mm = t.matmul(dst[:], WOI[kt][:, 128 * hc:128 * (hc + 1)], OT[kt][:], start=(kt == 0), stop=(kt == 3))
                mm.then_inc(S.pe.h, 1)
                S.pe.n += 1
            p.t(two)
            npe = S.pe.n

            def vwo(v, hc=hc, npe=npe):
                v.wait_ge(S.pe.h, npe)
                v.tensor_tensor(out=self.XA[hc][:, cols], in0=psW[hc % 2][:], in1=self.XA[hc][:, cols], op=ALU.add)
                v.drain()
                v.sem_inc(S.dve.h, 1)
                S.dve.n += 1
            p.v(vwo)
            ev_w.append(S.dve.n)
        if w == 3:
            xm_name = f"xm{l}"
            self._ecc[xm_name] = self.ex_send_cols(p, xm_name, [self.XA[i][:, TW - 1:TW] for i in range(NFC)])
        p.emit()
        st.close()

    def inner_ffn(self, l):
        nc, S, d = self.nc, self, self.d
        st = ExitStack()
        T2 = [st.enter_context(nc.sbuf_tensor(f"T2_{l}_{i}", [128, 1024], BF16)) for i in range(NFC)]
        ACTB = [st.enter_context(nc.sbuf_tensor(f"ACTB{l}_{i}", [128, 1024], BF16)) for i in range(22)]
        GLB = [st.enter_context(nc.sbuf_tensor(f"GLB{l}_0", [128, 1024], BF16))] * 2
        W1S = [st.enter_context(nc.sbuf_tensor(f"W1S{l}_{i}", [128, 2048], BF16)) for i in range(2)]
        W2S = [st.enter_context(nc.sbuf_tensor(f"W2S{l}_{i}", [128, 2816], BF16)) for i in range(2)]
        HC2 = st.enter_context(nc.sbuf_tensor(f"HC2{l}", [128, 8], F32))
        jw = [128] * 21 + [42]

        for tt in [1, 0]:
            pst = ExitStack()
            psA = pst.enter_context(nc.psum_tensor(f"fpsA{l}_{tt}", [1, 512], F32))
            psB = pst.enter_context(nc.psum_tensor(f"fpsB{l}_{tt}", [1, 512], F32))
            psR = pst.enter_context(nc.psum_tensor(f"fpsR{l}_{tt}", [128, 512], F32))
            psM = pst.enter_context(nc.psum_tensor(f"fpsM{l}_{tt}", [128, 512], F32))
            scr = pst.enter_context(nc.sbuf_tensor(f"fscr{l}_{tt}", [128, 512], F32))
            sq = GLB[0]
            psums = {"A": psA, "B": psB, "R": psR, "M": psM, "scr": scr}
            p = Phase(self)
            if tt == 0:
                xm_name = f"xm{l}"
                self.recv_halo_cols(p, xm_name, self._ecc[xm_name], HC2)
            for half in range(2):
                cols = slice(1 + 1024 * tt + 512 * half, 1 + 1024 * tt + 512 * half + 512)
                self.ln_block(p, psums, self.XA, self.XS, cols, T2,
                              out_cols=slice(512 * half, 512 * half + 512),
                              sq_scr=sq, lg_off=l * NFC, lb_off=l * NFC, shift=True)
            p.emit()
            pst.close()

            pst = ExitStack()
            psa2 = [pst.enter_context(nc.psum_tensor(f"gpsa{l}_{tt}_{i}", [128, 1024], F32)) for i in range(2)]
            psg2 = [pst.enter_context(nc.psum_tensor(f"gpsg{l}_{tt}_{i}", [128, 1024], F32)) for i in range(2)]
            p = Phase(self)
            w1 = d[f"w1{l}"]
            w1s_done = [0, 0]
            act_done = []
            for jc in range(22):
                wj = jw[jc]
                slot = jc % 2

                bj = self.sw_batch()

                def gld(gp, jc=jc, wj=wj, slot=slot):
                    if jc >= 2:
                        gp.wait_ge(S.pe.h, w1s_done[slot])
                    a0 = 128 * jc
                    g0 = 2730 + 128 * jc
                    ina = AP(tensor=w1.tensor, offset=a0, ap=[[5460, 128], [128 * 5460, 8], [1, wj]])
                    ing = AP(tensor=w1.tensor, offset=g0, ap=[[5460, 128], [128 * 5460, 8], [1, wj]])
                    gp.dma_start(out=W1S[slot][:, 0:8 * wj].rearrange("p (k w) -> p k w", k=8), in_=ina).then_inc(bj.h, 16)
                    gp.dma_start(out=W1S[slot][:, 1024:1024 + 8 * wj].rearrange("p (k w) -> p k w", k=8), in_=ing).then_inc(bj.h, 16)
                    bj.n += 32
                p.g(gld)
                n_ld = bj.n

                def tg(t, jc=jc, wj=wj, slot=slot):
                    t.wait_ge(bj.h, n_ld)
                    if jc >= 2:
                        t.wait_ge(S.dve.h, act_done[jc - 2])
                    psa, psg = psa2[slot], psg2[slot]
                    for h2 in range(2):
                        for i in range(NFC):
                            t.matmul(psa[0:wj, 512 * h2:512 * h2 + 512], W1S[slot][:, wj * i:wj * (i + 1)],
                                     T2[i][:, 512 * h2:512 * h2 + 512], start=(i == 0), stop=(i == NFC - 1))
                        for i in range(NFC):
                            mm = t.matmul(psg[0:wj, 512 * h2:512 * h2 + 512], W1S[slot][:, 1024 + wj * i:1024 + wj * (i + 1)],
                                          T2[i][:, 512 * h2:512 * h2 + 512], start=(i == 0), stop=(i == NFC - 1))
                    mm.then_inc(S.pe.h, 1)
                    S.pe.n += 1
                p.t(tg)
                w1s_done[slot] = S.pe.n
                n_mm = S.pe.n

                def sgl(s, jc=jc, wj=wj, slot=slot):
                    s.wait_ge(S.pe.h, n_mm)
                    if jc >= 1:
                        s.wait_ge(S.dve.h, act_done[jc - 1])
                    s.activation(out=GLB[slot][0:wj, :], in_=psg2[slot][0:wj, :], func=AF.Gelu,
                                 bias=self.B1C[0:wj, l * 44 + 22 + jc:l * 44 + 23 + jc], scale=1.0).then_inc(S.act.h, 1)
                    S.act.n += 1
                p.s(sgl)
                n_gl = S.act.n

                def vac(v, jc=jc, wj=wj, slot=slot):
                    v.wait_ge(S.act.h, n_gl)
                    v.scalar_tensor_tensor(out=ACTB[jc][0:wj, :], in0=psa2[slot][0:wj, :],
                                           scalar=self.B1C[0:wj, l * 44 + jc:l * 44 + jc + 1],
                                           in1=GLB[slot][0:wj, :], op0=ALU.add, op1=ALU.mult)
                    v.drain()
                    v.sem_inc(S.dve.h, 1)
                    S.dve.n += 1
                p.v(vac)
                act_done.append(S.dve.n)
            n_act_all = S.dve.n
            p.emit()
            pst.close()

            pst = ExitStack()
            psw = [pst.enter_context(nc.psum_tensor(f"hpsw{l}_{tt}_{i}", [128, 1024], F32)) for i in range(2)]
            p = Phase(self)
            w2 = d[f"w2{l}"]
            w2_done = [0, 0]
            ev_done = []
            for hc in range(NFC):
                slot = hc % 2

                bh = self.sw_batch()

                def gld2(gp, hc=hc, slot=slot):
                    if hc >= 2:
                        gp.wait_ge(S.pe.h, w2_done[slot])
                    c0 = 128 * hc
                    in1 = AP(tensor=w2.tensor, offset=c0, ap=[[1024, 128], [128 * 1024, 21], [1, 128]])
                    in2 = AP(tensor=w2.tensor, offset=21 * 128 * 1024 + c0, ap=[[1024, 42], [1, 128]])
                    gp.dma_start(out=W2S[slot][:, 0:21 * 128].rearrange("p (k w) -> p k w", k=21), in_=in1).then_inc(bh.h, 16)
                    gp.dma_start(out=W2S[slot][0:42, 21 * 128:22 * 128], in_=in2).then_inc(bh.h, 16)
                    bh.n += 32
                p.g(gld2)
                n_ld2 = bh.n

                def tw2(t, hc=hc, slot=slot):
                    t.wait_ge(bh.h, n_ld2)
                    if hc == 0:
                        t.wait_ge(S.dve.h, n_act_all)
                    if hc >= 2:
                        t.wait_ge(S.dve.h, ev_done[hc - 2])
                    for h2 in range(2):
                        for jc in range(22):
                            mm = t.matmul(psw[slot][:, 512 * h2:512 * h2 + 512],
                                          W2S[slot][0:jw[jc], 128 * jc:128 * jc + 128],
                                          ACTB[jc][0:jw[jc], 512 * h2:512 * h2 + 512],
                                          start=(jc == 0), stop=(jc == 21))
                    mm.then_inc(S.pe.h, 1)
                    S.pe.n += 1
                p.t(tw2)
                w2_done[slot] = S.pe.n
                n_mm2 = S.pe.n

                def vev3(v, hc=hc, slot=slot, tt=tt):
                    v.wait_ge(S.pe.h, n_mm2)
                    cols = slice(1 + 1024 * tt, 1 + 1024 * tt + 1024)
                    v.scalar_tensor_tensor(out=self.XA[hc][:, cols], in0=psw[slot][:],
                                           scalar=self.B2C[:, l * NFC + hc:l * NFC + hc + 1],
                                           in1=self.XA[hc][:, cols], op0=ALU.add, op1=ALU.add)
                    v.drain()
                    v.tensor_copy(out=self.XS[hc][:, cols], in_=self.XA[hc][:, cols])
                    v.drain()
                    v.sem_inc(S.dve.h, 1)
                    S.dve.n += 1
                p.v(vev3)
                ev_done.append(S.dve.n)
            if tt == 1 and l == 0:
                self._ecc["x1col"] = self.ex_send_cols(p, "x1col", [self.XA[i][:, TW - 1:TW] for i in range(NFC)])
            p.emit()
            pst.close()
        st.close()

    def ph_final(self):
        nc, S = self.nc, self
        st = ExitStack()
        XR = [[st.enter_context(nc.sbuf_tensor(f"fXR{b}_{i}", [128, 512], F32)) for i in range(NFC)] for b in range(2)]
        OUTT = [st.enter_context(nc.sbuf_tensor(f"fOT{i}", [128, 512], F32)) for i in range(NFC)]
        psA = st.enter_context(nc.psum_tensor("fpsA", [1, 512], F32))
        psB = st.enter_context(nc.psum_tensor("fpsB", [1, 512], F32))
        psR = st.enter_context(nc.psum_tensor("fpsR", [128, 512], F32))
        psM = st.enter_context(nc.psum_tensor("fpsM", [128, 512], F32))
        scr = st.enter_context(nc.sbuf_tensor("fscr", [128, 512], F32))
        sq = st.enter_context(nc.sbuf_tensor("fsq", [128, 1024], BF16))
        psums = {"A": psA, "B": psB, "R": psR, "M": psM, "scr": scr}
        p = Phase(self)
        bl = self.hw_batch()
        vno_at = {}

        def yld(tb):
            gate = vno_at.get(tb - 2)

            def y(sync, tb=tb, gate=gate):
                if gate:
                    sync.wait_ge(S.dve.h, gate)
                for i in range(NFC):
                    sync.dma_start(out=XR[tb % 2][i][:], in_=self.xo_spill.ap()[128 * i:128 * (i + 1), 512 * tb:512 * tb + 512]).then_inc(bl.h, 16)
                bl.n += 16 * NFC
            p.y(y)
            return bl.n

        ld_at = [yld(0), yld(1)]
        for tb in range(4):
            cols = slice(1 + 512 * tb, 1 + 512 * tb + 512)
            nld = ld_at[tb]

            def vw(v, nld=nld):
                v.wait_ge(bl.h, nld)
            p.v(vw)
            og = (S.sout.h, S.sout.n) if tb >= 1 else None
            self.ln_block(p, psums, self.XA, self.XS, cols, OUTT,
                          out_cols=slice(0, 512), sq_scr=sq,
                          res_tiles=XR[tb % 2], res_cols=slice(0, 512),
                          act_gate=S.act.n, out_gate=og, pe_gate=S.pe.n)
            vno_at[tb] = S.dve.n

            def yout(sync, tb=tb, nd=S.dve.n):
                sync.wait_ge(S.dve.h, nd)
                for i in range(NFC):
                    sync.dma_start(out=self.d["out"][128 * i:128 * (i + 1), 512 * tb:512 * tb + 512], in_=OUTT[i][:]).then_inc(S.sout.h, 16)
                S.sout.n += 16 * NFC
                if tb == 3:
                    sync.wait_ge(S.sout.h, S.sout.n)
            p.y(yout)
            if tb + 2 < 4:
                ld_at.append(yld(tb + 2))
        p.emit()
        st.close()

    def ph_emit_stream(self, tiles):
        S = self
        p = Phase(self)

        def y(sync):
            for i in range(NFC):
                sync.dma_start(out=self.d["out"][128 * i:128 * (i + 1), :], in_=tiles[i][:, 1:TW]).then_inc(S.sout.h, 16)
            S.sout.n += 16 * NFC
            sync.wait_ge(S.sout.h, S.sout.n)
        p.y(y)
        p.emit()


def _host_inputs(inputs, core):
    b, h = core // 2, core % 2
    sl = slice(2048 * h, 2048 * h + 2048)
    m = {}
    x = np.asarray(inputs["L0_attn_x"][b, sl], dtype=np.float32)
    m["x"] = np.ascontiguousarray(x.T)
    m["wqT"] = np.ascontiguousarray(np.asarray(inputs["L0_attn_wq"], np.float32).T)
    m["wkT"] = np.ascontiguousarray(np.asarray(inputs["L0_attn_wk"], np.float32).T)
    m["wvT"] = np.ascontiguousarray(np.asarray(inputs["L0_attn_wv"], np.float32).T)
    m["woT"] = np.ascontiguousarray(np.asarray(inputs["L0_attn_wo"], np.float32).T)
    for l in range(2):
        m[f"wqkv{l}"] = np.ascontiguousarray(np.asarray(inputs["L0_ffn_wqkv"][l], np.float32))
        m[f"wo{l}"] = np.ascontiguousarray(np.asarray(inputs["L0_ffn_wo"][l], np.float32))
        m[f"w1{l}"] = np.ascontiguousarray(np.asarray(inputs["L0_ffn_w1"][l], np.float32))
        m[f"w2{l}"] = np.ascontiguousarray(np.asarray(inputs["L0_ffn_w2"][l], np.float32))
    lg = np.zeros((128, 2 * NFC), np.float32)
    lb = np.zeros((128, 2 * NFC), np.float32)
    for l in range(2):
        lg[:, l * NFC:(l + 1) * NFC] = np.asarray(inputs["L0_ffn_ln_g"][l], np.float32).reshape(NFC, 128).T
        lb[:, l * NFC:(l + 1) * NFC] = np.asarray(inputs["L0_ffn_ln_b"][l], np.float32).reshape(NFC, 128).T
    m["lgcols"] = lg
    m["lbcols"] = lb
    b1 = np.zeros((128, 2 * 44), np.float32)
    for l in range(2):
        v = np.asarray(inputs["L0_ffn_b1"][l], np.float32)
        a = np.zeros(22 * 128, np.float32); a[:2730] = v[:2730]
        g = np.zeros(22 * 128, np.float32); g[:2730] = v[2730:]
        b1[:, l * 44:l * 44 + 22] = a.reshape(22, 128).T
        b1[:, l * 44 + 22:l * 44 + 44] = g.reshape(22, 128).T
    m["b1cols"] = b1
    b2 = np.zeros((128, 2 * NFC), np.float32)
    for l in range(2):
        b2[:, l * NFC:(l + 1) * NFC] = np.asarray(inputs["L0_ffn_b2"][l], np.float32).reshape(NFC, 128).T
    m["b2cols"] = b2
    qk = np.zeros((128, 4), np.float32)
    qk[:, 0] = np.tile(np.asarray(inputs["L0_ffn_q_scale"][0], np.float32), 2)
    qk[:, 1] = np.tile(np.asarray(inputs["L0_ffn_k_scale"][0], np.float32), 2)
    qk[:, 2] = np.tile(np.asarray(inputs["L0_ffn_q_scale"][1], np.float32), 2)
    qk[:, 3] = np.tile(np.asarray(inputs["L0_ffn_k_scale"][1], np.float32), 2)
    m["qkscale"] = qk
    tg = np.broadcast_to(np.arange(2048 * h, 2048 * h + T, dtype=np.float32), (128, T))
    m["tgrid"] = np.ascontiguousarray(tg)
    ic = ((np.arange(128) % 64) % 32).astype(np.float32)[:, None]
    m["icol"] = np.ascontiguousarray(ic)
    sg = np.where((np.arange(128) % 64) < 32, -1.0, 1.0).astype(np.float32)[:, None]
    m["sgncol"] = np.ascontiguousarray(sg)
    m["gatecol"] = np.full((128, 1), float(h), np.float32)
    jj = np.arange(128)[:, None]
    tt = np.arange(128)[None, :]
    import ml_dtypes
    bfd = ml_dtypes.bfloat16
    d0 = np.where(jj - tt >= 0, 0.0, NEG).astype(bfd)
    d4 = np.where(jj - tt <= 0, 0.0, NEG).astype(bfd)
    m["mask_d0"] = d0
    m["mask_d4"] = d4
    m["mask_w0d0"] = d0 if h == 1 else np.full((128, 128), NEG, bfd)
    m["mask_w0fill"] = (np.zeros((128, 128)) if h == 1 else np.full((128, 128), NEG)).astype(bfd)
    return m


def kernel(**inputs):
    prog = Prog(stage=_STAGE)
    nc = prog.build()
    in_maps = [_host_inputs(inputs, c) for c in range(8)]
    res = run_bass_kernel_spmd(nc, in_maps, core_ids=list(range(8)))
    out = np.zeros((4, 4096, 1024), dtype=np.float32)
    for c in range(8):
        b, h = c // 2, c % 2
        out[b, 2048 * h:2048 * h + 2048] = np.asarray(res.results[c]["out"]).T
    return out
